# version 02
# baseline (speedup 1.0000x reference)
import os
import contextlib
import numpy as np
import ml_dtypes
import concourse.bass as bass
import concourse.mybir as mybir
from concourse.bass_utils import run_bass_kernel_spmd





F32 = mybir.dt.float32
BF16 = mybir.dt.bfloat16
U32 = mybir.dt.uint32
I32 = mybir.dt.int32
U8 = mybir.dt.uint8
AF = mybir.ActivationFunctionType
ALU = mybir.AluOpType
AX = mybir.AxisListType

STREAMS = ("sync", "tensor", "vector", "scalar", "gpsimd")


class Em:
    def __init__(self, nc, ctx):
        self.nc = nc
        self.ctx = ctx
        self.ops = {s: [] for s in STREAMS}
        self.csem = {s: ctx.enter_context(nc.semaphore("c_" + s)) for s in STREAMS}
        self.dsem = {s: ctx.enter_context(nc.semaphore("d_" + s))
                     for s in ("sync", "scalar", "gpsimd")}
        self.ccount = {s: 0 for s in STREAMS}
        self.dcount = {s: 0 for s in ("sync", "scalar", "gpsimd")}
        self.kn = {s: {} for s in STREAMS}
        self.snaps = {("c", s): [] for s in STREAMS}
        for s in ("sync", "scalar", "gpsimd"):
            self.snaps[("d", s)] = []
        self.last_writer = {}
        self.readers = {}
        self.nwaits = 0

    def _tok_needed(self, s, toks):
        need = {}
        for tok in toks:
            if tok is None:
                continue
            kind, s2, n = tok
            if kind == "c" and s2 == "tensor" and s == "tensor":
                continue
            key = (kind, s2)
            if need.get(key, 0) < n:
                need[key] = n
        return need

    def op(self, s, fn, reads=(), writes=(), psreads=(), dma=False, tag=""):
        if psreads:
            reads = tuple(reads) + tuple(psreads)
            writes = tuple(writes) + tuple(psreads)
        toks = []
        for k in reads:
            toks.append(self.last_writer.get(k))
        for k in writes:
            toks.append(self.last_writer.get(k))
            toks.extend(self.readers.get(k, ()))
        need = self._tok_needed(s, toks)
        waits = []
        kn = self.kn[s]
        for key, n in sorted(need.items()):
            if kn.get(key, 0) >= n:
                continue
            kind, s2 = key
            if kind == "c":
                waits.append((self.csem[s2], n))
            else:
                waits.append((self.dsem[s2], 16 * n))
            snap = self.snaps[key][n - 1]
            for kk, vv in snap.items():
                if kn.get(kk, 0) < vv:
                    kn[kk] = vv
            if kn.get(key, 0) < n:
                kn[key] = n
        self.nwaits += len(waits)
        if dma:
            assert s in self.dsem, f"stream {s} cannot issue DMAs"
            prev = self.dcount[s]
            if prev and kn.get(("d", s), 0) < prev:
                waits.append((self.dsem[s], 16 * prev))
                snap = self.snaps[("d", s)][prev - 1]
                for kk, vv in snap.items():
                    if kn.get(kk, 0) < vv:
                        kn[kk] = vv
                kn[("d", s)] = prev
                self.nwaits += 1
            self.dcount[s] += 1
            tok = ("d", s, self.dcount[s])
            inc = (self.dsem[s], 16)
        else:
            self.ccount[s] += 1
            tok = ("c", s, self.ccount[s])
            inc = (self.csem[s], 1)
        snap = dict(kn)
        snap[(tok[0], s)] = tok[2]
        self.snaps[(tok[0], s)].append(snap)
        self.ops[s].append((waits, fn, inc, tag))
        for k in reads:
            self.readers.setdefault(k, []).append(tok)
        for k in writes:
            self.last_writer[k] = tok
            self.readers[k] = []
        return tok

    def raw(self, s, fn, tag="raw"):
        self.ops[s].append(([], fn, None, tag))

    def v(self, fn, reads=(), writes=(), psreads=(), tag=""):
        return self.op("vector", fn, reads, writes, psreads, tag=tag)

    def sc(self, fn, reads=(), writes=(), psreads=(), tag=""):
        return self.op("scalar", fn, reads, writes, psreads, tag=tag)

    def t(self, fn, reads=(), writes=(), psreads=(), tag=""):
        return self.op("tensor", fn, reads, writes, psreads, tag=tag)

    def g(self, fn, reads=(), writes=(), psreads=(), tag=""):
        return self.op("gpsimd", fn, reads, writes, psreads, tag=tag)

    def dma(self, q, out, in_, reads=(), writes=(), tag="", **kw):
        return self.op(q, lambda e: e.dma_start(out=out, in_=in_, **kw),
                       reads, writes, dma=True, tag=tag)

    def barrier(self):
        counts_c = dict(self.ccount)
        counts_d = dict(self.dcount)
        for s in STREAMS:
            waits = []
            kn = self.kn[s]
            for s2 in STREAMS:
                if s2 == s:
                    continue
                n = counts_c[s2]
                if n and kn.get(("c", s2), 0) < n:
                    waits.append((self.csem[s2], n))
                    snap = self.snaps[("c", s2)][n - 1]
                    for kk, vv in snap.items():
                        if kn.get(kk, 0) < vv:
                            kn[kk] = vv
                    kn[("c", s2)] = max(kn.get(("c", s2), 0), n)
            for s2 in ("sync", "scalar", "gpsimd"):
                n = counts_d[s2]
                if n and kn.get(("d", s2), 0) < n:
                    waits.append((self.dsem[s2], 16 * n))
                    snap = self.snaps[("d", s2)][n - 1]
                    for kk, vv in snap.items():
                        if kn.get(kk, 0) < vv:
                            kn[kk] = vv
                    kn[("d", s2)] = max(kn.get(("d", s2), 0), n)
            n_own = counts_c[s]
            if n_own and kn.get(("c", s), 0) < n_own:
                waits.append((self.csem[s], n_own))
                kn[("c", s)] = n_own
            if s in self.dcount:
                n = counts_d[s]
                if n and kn.get(("d", s), 0) < n:
                    waits.append((self.dsem[s], 16 * n))
                    kn[("d", s)] = n
            self.ccount[s] += 1
            tok = ("c", s, self.ccount[s])
            snap = dict(kn)
            snap[("c", s)] = tok[2]
            self.snaps[("c", s)].append(snap)
            self.ops[s].append((waits, (lambda e: e.nop()), (self.csem[s], 1), "barrier"))
            self.nwaits += len(waits)

    def barrier_reads_of_all(self):
        def fin(eng):
            return eng.nop()
        waits = []
        kn = self.kn["sync"]
        for s in STREAMS:
            n = self.ccount[s]
            if n and kn.get(("c", s), 0) < n:
                waits.append((self.csem[s], n))
        for s in ("sync", "scalar", "gpsimd"):
            n = self.dcount[s]
            if n and kn.get(("d", s), 0) < n:
                waits.append((self.dsem[s], 16 * n))
        n_own = self.ccount["sync"]
        if n_own and kn.get(("c", "sync"), 0) < n_own:
            waits.append((self.csem["sync"], n_own))
            kn[("c", "sync")] = n_own
        self.ccount["sync"] += 1
        self.ops["sync"].append((waits, fin, (self.csem["sync"], 1), "final"))

    def replay(self, block):
        em = self

        def make(stream):
            oplist = em.ops[stream]

            def runner(eng):
                for idx, (waits, fn, inc, tag) in enumerate(oplist):
                    for sem, val in waits:
                        eng.wait_ge(sem, val)
                    try:
                        inst = fn(eng)
                    except Exception:
                        print(f"EMIT FAIL stream={stream} idx={idx} tag={tag} "
                              f"waits={[(s.name, v) for s, v in waits]}")
                        raise
                    if inc is not None:
                        inst.then_inc(inc[0], inc[1])

            return runner

        block.sync(make("sync"))
        block.tensor(make("tensor"))
        block.vector(make("vector"))
        block.scalar(make("scalar"))
        block.gpsimd(make("gpsimd"))


def ceil_div(a, b):
    return (a + b - 1) // b


T = 4096
D = 1024
DI = 2048
NS = 16
RK = 64
TC = 512
NCH = T // TC
NDT = DI // 128
NFT = 2 * DI // 128
NTT = T // 128
EPS = 1e-5


def l0a_host_prep(w, pref=""):
    cw = w["conv_w"].reshape(NDT, 128, 4).transpose(1, 0, 2).copy()
    cb = w["conv_b"].reshape(NDT, 128).T.copy()
    dtb = w["dt_proj_b"].reshape(NDT, 128).T.copy()
    alog = w["A_log"].reshape(NDT, 128, NS).transpose(1, 0, 2).copy()
    dsk = w["D_skip"].reshape(NDT, 128).T.copy()
    d = {
        "ipw": np.ascontiguousarray(w["in_proj_w"]),
        "convw": cw, "convb": cb,
        "xpw": np.ascontiguousarray(w["x_proj_w"]),
        "dtw": np.ascontiguousarray(w["dt_proj_w"]),
        "dtb": dtb, "alog": alog, "dskip": dsk,
        "opw": np.ascontiguousarray(w["out_proj_w"]),
    }
    return {pref + k: v for k, v in d.items()}


def host_consts():
    ident = np.eye(128, dtype=np.float32)
    sel = np.zeros((32, 32, 128), np.float32)
    for n in range(32):
        sel[n, n, :] = 1.0
    return {
        "identf": ident,
        "identb": ident.astype(ml_dtypes.bfloat16),
        "sel32": sel.reshape(32, 32 * 128).astype(ml_dtypes.bfloat16),
    }


def declare_consts(nc):
    d = {}
    d["identf"] = nc.dram_tensor("identf", [128, 128], F32, kind="ExternalInput").ap()
    d["identb"] = nc.dram_tensor("identb", [128, 128], BF16, kind="ExternalInput").ap()
    d["sel32"] = nc.dram_tensor("sel32", [32, 32 * 128], BF16, kind="ExternalInput").ap()
    return d


def l0a_declare_dram(nc, pref=""):
    d = {}
    def inp(name, shape, dt=F32):
        d[name] = nc.dram_tensor(pref + name, shape, dt, kind="ExternalInput").ap()
    inp("ipw", [D, 2 * DI])
    inp("convw", [128, NDT, 4])
    inp("convb", [128, NDT])
    inp("xpw", [DI, 96])
    inp("dtw", [RK, DI])
    inp("dtb", [128, NDT])
    inp("alog", [128, NDT, NS])
    inp("dskip", [128, NDT])
    inp("opw", [DI, D])
    return d


class Consts:
    def __init__(self, em, nc, ctx, dc):
        def sb(name, shape, dtype=F32):
            return ctx.enter_context(nc.sbuf_tensor("sb_" + name, shape, dtype))
        self.ident_b = sb("identb", [128, 128], BF16)
        self.ident_f = sb("identf", [128, 128], F32)
        self.sel32 = sb("sel32", [32, 32 * 128], BF16)
        em.dma("sync", self.ident_b[:], dc["identb"][:, :], writes=["identb"])
        em.dma("sync", self.ident_f[:], dc["identf"][:, :], writes=["identf"])
        em.dma("sync", self.sel32[:], dc["sel32"][:, :], writes=["sel32"])


def emit_ln(em, P, src, ksrc, sq_t, red, dst, kdst):
    em.v(lambda e: e.reduce_sum(red[:, 0:1], src[:], axis=AX.X),
         reads=[ksrc], writes=[P + "red"])
    em.v(lambda e: e.tensor_scalar_mul(red[:, 1:2], red[:, 0:1], -1.0 / D),
         reads=[P + "red"], writes=[P + "red"])
    em.v(lambda e: e.tensor_scalar_add(src[:], src[:], red[:, 1:2]),
         reads=[ksrc, P + "red"], writes=[ksrc])
    em.sc(lambda e: e.activation(out=sq_t[:], in_=src[:], func=AF.Square,
                                 accum_out=red[:, 2:3]),
          reads=[ksrc], writes=[P + "sq", P + "red"])
    em.v(lambda e: e.tensor_scalar(red[:, 3:4], red[:, 2:3], 1.0 / D, EPS, ALU.mult, ALU.add),
         reads=[P + "red"], writes=[P + "red"])
    em.sc(lambda e: e.activation(out=red[:, 3:4], in_=red[:, 3:4], func=AF.Ln),
          reads=[P + "red"], writes=[P + "red"])
    em.sc(lambda e: e.activation(out=red[:, 3:4], in_=red[:, 3:4], func=AF.Exp, scale=-0.5),
          reads=[P + "red"], writes=[P + "red"])
    em.v(lambda e: e.tensor_scalar_mul(dst[:], src[:], red[:, 3:4]),
         reads=[ksrc, P + "red"], writes=[kdst])


def l0a_emit(em, nc, ctx, cst, dw, x_in, x_out, pref="l0a", stage=99, dbg=False):
    P = pref
    def mksb(c):
        def sb(name, shape, dtype=F32):
            return c.enter_context(nc.sbuf_tensor(P + "sb_" + name, shape, dtype))
        return sb
    sbp = mksb(ctx)
    def psum(name, shape, dtype=F32):
        return ctx.enter_context(nc.psum_tensor(P + "pp_" + name, shape, dtype))

    kind = "ExternalOutput" if dbg else "Internal"
    u_dram = nc.dram_tensor(P + "u_scr", [NDT, 128, T], F32, kind=kind).ap()
    sres_dram = nc.dram_tensor(P + "sres_scr", [NDT, 128, T], BF16, kind=kind).ap()

    xdbl = sbp("xdbl", [128, NTT, 96], F32)
    dtfm = sbp("dtfm", [RK, T], BF16)
    bmcm = sbp("bmcm", [32, T], BF16)
    xpw_b = sbp("xpw", [128, NDT, 96], BF16)
    dtw_b = sbp("dtw", [RK, DI], BF16)
    convw_s = sbp("convw", [128, NDT, 4], F32)
    convb_s = sbp("convb", [128, NDT], F32)
    dtb_s = sbp("dtb", [128, NDT], F32)
    acol_s = sbp("acol", [128, NDT * NS], F32)
    dskip_s = sbp("dskip", [128, NDT], F32)
    state = sbp("state", [128, NDT * NS], F32)
    xres = [sbp(f"xres{i}", [128, D], F32) for i in range(2)]
    s_t = [sbp(f"st{i}", [128, D], F32) for i in range(2)]
    sq_t = sbp("sq", [128, D], F32)
    red = sbp("red", [128, 4], F32)
    outt = [sbp(f"outt{i}", [128, D], F32) for i in range(2)]


    em.dma("sync", convw_s[:], dw["convw"][:, :, :], writes=[P+"convw"])
    em.dma("sync", convb_s[:], dw["convb"][:, :], writes=[P+"convb"])
    em.dma("sync", dtb_s[:], dw["dtb"][:, :], writes=[P+"dtb"])
    em.dma("sync", acol_s[:], dw["alog"].rearrange("p a b -> p (a b)"), writes=[P+"acol"])
    em.dma("sync", dskip_s[:], dw["dskip"][:, :], writes=[P+"dskip"])
    em.dma("gpsimd", xpw_b[:], dw["xpw"].rearrange("(c p) m -> p c m", p=128), writes=[P+"xpw"])
    em.dma("gpsimd", dtw_b[:], dw["dtw"][:, :], writes=[P+"dtw"])
    em.sc(lambda e: e.activation(out=acol_s[:], in_=acol_s[:], func=AF.Exp),
          reads=[P+"acol"], writes=[P+"acol"])
    em.v(lambda e: e.tensor_scalar_mul(acol_s[:], acol_s[:], -1.0),
         reads=[P+"acol"], writes=[P+"acol"])
    em.v(lambda e: e.memset(state[:], 0.0), writes=[P+"state"])
    em.v(lambda e: e.memset(xdbl[:], 0.0), writes=[P+"xdbl"])

    ctxA = contextlib.ExitStack()
    sba = mksb(ctxA)
    def mkps(c):
        def ps(name, shape, dtype=F32):
            return c.enter_context(nc.psum_tensor(P + "pp_" + name, shape, dtype))
        return ps
    with ctxA:
        psa_ = mkps(ctxA)
        ps_a = psa_("psa", [128, 512])
        ps_tr = [psa_(f"pstr{i}", [128, 128], BF16) for i in range(2)]
        ps_96 = [psa_(f"ps96{i}", [128, 96]) for i in range(2)]
        ps_tf = [psa_(f"pstf{i}", [128, 128]) for i in range(2)]
        xfm = sba("xfm", [128, 8, T], BF16)
        xld = [sba(f"xld{i}", [128, D], F32) for i in range(2)]
        xbf = [sba(f"xbf{i}", [128, D], BF16) for i in range(2)]
        ipw_t = [sba(f"ipwt{i}", [128, 8, 128], BF16) for i in range(2)]
        xz_t = sba("xz", [128, T], F32)
        cacc = sba("cacc", [128, T], F32)
        ubf_t = sba("ubf", [128, T], BF16)
        sres_t = sba("srest", [128, T], BF16)

        import os as _os
        _cap = int(_os.environ.get("NTTCAP", str(NTT)))
        for tt in range(_cap):
            i = tt % 2
            kxl, kxb = f"{P}xld{i}", f"{P}xbf{i}"
            em.dma("sync", xld[i][:], x_in[tt * 128:(tt + 1) * 128, :], writes=[kxl], tag="p0ld")
            em.v(lambda e, i=i: e.tensor_copy(xbf[i][:], xld[i][:]), reads=[kxl], writes=[kxb])
            for dc in range(8):
                slot = dc % 2
                kps = f"{P}pstr{slot}"
                em.t(lambda e, i=i, dc=dc, slot=slot: e.transpose(
                    ps_tr[slot][:, :], xbf[i][:, dc * 128:(dc + 1) * 128], cst.ident_b[:]),
                    reads=[kxb, "identb"], writes=[kps])
                if dc % 2:
                    em.sc(lambda e, tt=tt, dc=dc, slot=slot: e.copy(
                        xfm[:, dc, tt * 128:(tt + 1) * 128], ps_tr[slot][:, :]),
                        reads=[], psreads=[kps], writes=[P+"xfm"])
                else:
                    em.v(lambda e, tt=tt, dc=dc, slot=slot: e.tensor_copy(
                        xfm[:, dc, tt * 128:(tt + 1) * 128], ps_tr[slot][:, :]),
                        reads=[], psreads=[kps], writes=[P+"xfm"])

        for ft in (range(NFT) if stage >= 2 else []):
            i = ft % 2
            kw = f"{P}ipwt{i}"
            em.dma("gpsimd", ipw_t[i][:], dw["ipw"][:, ft * 128:(ft + 1) * 128].rearrange("(c p) m -> p c m", p=128),
                   writes=[kw], tag="p1w")
            for w in range(8):
                for dc in range(8):
                    em.t(lambda e, i=i, w=w, dc=dc: e.matmul(
                        ps_a[:, :], ipw_t[i][:, dc, :], xfm[:, dc, w * 512:(w + 1) * 512],
                        start=(dc == 0), stop=(dc == 7)),
                        reads=[kw, P+"xfm"], writes=[P+"psa"], tag="p1mm")
                em.sc(lambda e, w=w: e.copy(xz_t[:, w * 512:(w + 1) * 512], ps_a[:, :]),
                      psreads=[P+"psa"], writes=[P+"xz"])
            if ft < NDT:
                dt_i = ft
                em.v(lambda e, dt_i=dt_i: e.tensor_scalar_mul(cacc[:], xz_t[:], convw_s[:, dt_i, 3:4]),
                     reads=[P+"xz", P+"convw"], writes=[P+"cacc"])
                for s in range(1, 4):
                    em.v(lambda e, s=s, dt_i=dt_i: e.scalar_tensor_tensor(
                        cacc[:, s:T], xz_t[:, 0:T - s], convw_s[:, dt_i, 3 - s:4 - s], cacc[:, s:T],
                        ALU.mult, ALU.add),
                        reads=[P+"xz", P+"convw", P+"cacc"], writes=[P+"cacc"])
                em.sc(lambda e, dt_i=dt_i: e.activation(out=cacc[:], in_=cacc[:], func=AF.Silu,
                                                        bias=convb_s[:, dt_i:dt_i + 1]),
                      reads=[P+"cacc", P+"convb"], writes=[P+"cacc"])
                em.dma("scalar", u_dram[dt_i, :, :], cacc[:], reads=[P+"cacc"], tag="p1us")
                em.v(lambda e: e.tensor_copy(ubf_t[:], cacc[:]), reads=[P+"cacc"], writes=[P+"ubf"])
                for tt in range(NTT):
                    j = tt % 2
                    kp = f"{P}ps96{j}"
                    em.t(lambda e, tt=tt, dt_i=dt_i, j=j: e.matmul(
                        ps_96[j][:, :], ubf_t[:, tt * 128:(tt + 1) * 128], xpw_b[:, dt_i, :],
                        start=True, stop=True),
                        reads=[P+"ubf", P+"xpw"], writes=[kp])
                    em.v(lambda e, tt=tt, j=j: e.tensor_add(xdbl[:, tt, :], xdbl[:, tt, :], ps_96[j][:, :]),
                         reads=[P+"xdbl"], psreads=[kp], writes=[P+"xdbl"])
            else:
                dt_i = ft - NDT
                em.sc(lambda e: e.activation(out=sres_t[:], in_=xz_t[:], func=AF.Silu),
                      reads=[P+"xz"], writes=[P+"srest"])
                em.dma("scalar", sres_dram[dt_i, :, :], sres_t[:], reads=[P+"srest"], tag="p1ss")

        for tt in (range(NTT) if stage >= 3 else []):
            em.t(lambda e, tt=tt: e.transpose(ps_tf[0][0:64, :], xdbl[:, tt, 0:64], cst.ident_f[:]),
                 reads=[P+"xdbl", "identf"], writes=[P+"pstf0"])
            em.v(lambda e, tt=tt: e.tensor_copy(dtfm[:, tt * 128:(tt + 1) * 128], ps_tf[0][0:64, :]),
                 psreads=[P+"pstf0"], writes=[P+"dtfm"])
            em.t(lambda e, tt=tt: e.transpose(ps_tf[1][0:32, :], xdbl[:, tt, 64:96], cst.ident_f[:]),
                 reads=[P+"xdbl", "identf"], writes=[P+"pstf1"])
            em.v(lambda e, tt=tt: e.tensor_copy(bmcm[:, tt * 128:(tt + 1) * 128], ps_tf[1][0:32, :]),
                 psreads=[P+"pstf1"], writes=[P+"bmcm"])

        em.barrier()

    if stage < 4:
        em.v(lambda e: e.memset(outt[0][:], 0.0), writes=[P+"outt0"])
        for tt in range(NTT):
            em.dma("scalar", x_out[tt * 128:(tt + 1) * 128, :], outt[0][:],
                   reads=[P+"outt0"], tag="dummy")
        return
    ctxB = contextlib.ExitStack()
    sbb = mksb(ctxB)
    with ctxB:
        psb_ = mkps(ctxB)
        ps_a = psb_("psdt", [128, 512])
        ps_b = psb_("psbc", [128, 512])
        ps_y = psb_("psy", [128, 512])
        ps_o = [psb_(f"pso{i}", [128, 512]) for i in range(2)]
        opw_b = sbb("opw", [128, NDT, D], BF16)
        bcB = sbb("bcB", [128, NS, TC], BF16)
        bcC = sbb("bcC", [128, NS, TC], BF16)
        sp1 = sbb("sp1", [128, TC], F32)
        spt = sbb("spt", [128, TC], F32)
        delta_t = sbb("delta", [128, TC], F32)
        uch = [sbb(f"uch{i}", [128, TC], F32) for i in range(2)]
        du_t = sbb("du", [128, TC], F32)
        uDb_t = sbb("uDb", [128, TC], BF16)
        dA = [sbb(f"dA{i}", [128, TC], F32) for i in range(2)]
        dBu = [sbb(f"dBu{i}", [128, TC], F32) for i in range(2)]
        h_t = [sbb(f"h{i}", [128, TC], F32) for i in range(2)]
        tmpb = [sbb(f"tmpb{i}", [128, TC], BF16) for i in range(2)]
        sresc = [sbb(f"sresc{i}", [128, TC], BF16) for i in range(2)]
        yfin = sbb("yfin", [128, NDT, TC], BF16)

        em.dma("gpsimd", opw_b[:], dw["opw"].rearrange("(c p) m -> p c m", p=128), writes=[P+"opw"])

        SS = os.environ.get("SCANSUB", "d")
        _nchcap = int(os.environ.get("NCHCAP", str(NCH)))
        _ndtcap = int(os.environ.get("NDTCAP", str(NDT)))
        for ch in range(_nchcap):
            t0 = ch * TC
            if SS == "a":
                em.v(lambda e: e.memset(yfin[:], 0.0), writes=[P+"yfin"])
            for n in (range(NS) if SS != "z" else []):
                for half in range(2):
                    row = n if half == 0 else NS + n
                    dst = bcB if half == 0 else bcC
                    kd = (P + "bcB") if half == 0 else (P + "bcC")
                    em.t(lambda e, row=row, t0=t0: e.matmul(
                        ps_b[:, :], cst.sel32[:, row * 128:(row + 1) * 128],
                        bmcm[:, t0:t0 + TC], start=True, stop=True),
                        reads=["sel32", P+"bmcm"], writes=[P+"psb"], tag="bc")
                    em.v(lambda e, dst=dst, n=n: e.tensor_copy(dst[:, n, :], ps_b[:, :]),
                         psreads=[P+"psb"], writes=[kd])
            DOPS = os.environ.get("DTOPS", "m12345uyrf")
            for dt_i in (range(_ndtcap) if SS not in ("a", "z") else []):
              if "m" in DOPS:
                em.t(lambda e, dt_i=dt_i, t0=t0: e.matmul(
                    ps_a[:, :], dtw_b[:, dt_i * 128:(dt_i + 1) * 128], dtfm[:, t0:t0 + TC],
                    start=True, stop=True),
                    reads=[P+"dtw", P+"dtfm"], writes=[P+"psa"], tag="dt")
              if "1" in DOPS:
                em.v(lambda e, dt_i=dt_i: e.tensor_scalar(
                    sp1[:], ps_a[:, :], dtb_s[:, dt_i:dt_i + 1], 0.0, ALU.add, ALU.max),
                    reads=[P+"dtb"], psreads=[P+"psa"], writes=[P+"sp1"])
              if "2" in DOPS:
                em.sc(lambda e, dt_i=dt_i: e.activation(out=spt[:], in_=ps_a[:, :], func=AF.Abs,
                                                        bias=dtb_s[:, dt_i:dt_i + 1]),
                      reads=[P+"dtb"], psreads=[P+"psa"], writes=[P+"spt"])
              if "A" in DOPS:
                em.sc(lambda e: e.activation(out=spt[:], in_=ps_a[:, :], func=AF.Abs),
                      psreads=[P+"psa"], writes=[P+"spt"])
              if "B" in DOPS:
                em.sc(lambda e, dt_i=dt_i: e.activation(out=spt[:], in_=ps_a[:, :], func=AF.Identity,
                                                        bias=dtb_s[:, dt_i:dt_i + 1]),
                      reads=[P+"dtb"], psreads=[P+"psa"], writes=[P+"spt"])
              if "C" in DOPS:
                em.sc(lambda e, dt_i=dt_i: e.activation(out=spt[:], in_=sp1[:], func=AF.Abs,
                                                        bias=dtb_s[:, dt_i:dt_i + 1]),
                      reads=[P+"sp1", P+"dtb"], writes=[P+"spt"])
              if "3" in DOPS:
                em.sc(lambda e: e.activation(out=spt[:], in_=spt[:], func=AF.Exp, scale=-1.0),
                      reads=[P+"spt"], writes=[P+"spt"])
              if "4" in DOPS:
                em.sc(lambda e: e.activation(out=spt[:], in_=spt[:], func=AF.Ln, bias=1.0),
                      reads=[P+"spt"], writes=[P+"spt"])
              if "5" in DOPS:
                em.v(lambda e: e.tensor_add(delta_t[:], sp1[:], spt[:]),
                     reads=[P+"sp1", P+"spt"], writes=[P+"delta"])
              iu = dt_i % 2
              if "u" in DOPS:
                ku = f"{P}uch{iu}"
                em.dma("sync", uch[iu][:], u_dram[dt_i, :, t0:t0 + TC], writes=[ku], tag="p3u")
                em.v(lambda e, iu=iu: e.tensor_mul(du_t[:], delta_t[:], uch[iu][:]),
                     reads=[P+"delta", ku], writes=[P+"du"])
                em.v(lambda e, iu=iu, dt_i=dt_i: e.tensor_scalar_mul(uDb_t[:], uch[iu][:], dskip_s[:, dt_i:dt_i + 1]),
                     reads=[ku, P+"dskip"], writes=[P+"uDb"])
              if "y" in DOPS:
                em.t(lambda e, _st=(SS == "b"): e.matmul(ps_y[:, :], cst.ident_b[:], uDb_t[:], start=True, stop=_st),
                     reads=["identb", P+"uDb"], writes=[P+"psy"], tag="yacc")
              if True:
                for n in (range(NS) if SS != "b" else []):
                    j = n % 2
                    col = dt_i * NS + n
                    em.sc(lambda e, j=j, col=col: e.activation(
                        out=dA[j][:], in_=delta_t[:], func=AF.Exp, scale=acol_s[:, col:col + 1]),
                        reads=[P+"delta", P+"acol"], writes=[f"{P}dA{j}"])
                    em.v(lambda e, j=j, n=n: e.tensor_mul(dBu[j][:], du_t[:], bcB[:, n, :]),
                         reads=[P+"du", P+"bcB"], writes=[f"{P}dBu{j}"])
                    if SS == "c":
                        em.v(lambda e, j=j: e.tensor_copy(h_t[j][:], dBu[j][:]),
                             reads=[f"{P}dBu{j}"], writes=[f"{P}h{j}"])
                    else:
                        em.v(lambda e, j=j, col=col: e.tensor_tensor_scan(
                            h_t[j][:], dA[j][:], dBu[j][:], state[:, col:col + 1], ALU.mult, ALU.add),
                            reads=[f"{P}dA{j}", f"{P}dBu{j}", P+"state"], writes=[f"{P}h{j}"])
                    em.v(lambda e, j=j, col=col: e.tensor_copy(state[:, col:col + 1], h_t[j][:, TC - 1:TC]),
                         reads=[f"{P}h{j}"], writes=[P+"state"])
                    em.v(lambda e, j=j, n=n: e.tensor_mul(tmpb[j][:], h_t[j][:], bcC[:, n, :]),
                         reads=[f"{P}h{j}", P+"bcC"], writes=[f"{P}tmpb{j}"])
                    em.t(lambda e, j=j, n=n: e.matmul(
                        ps_y[:, :], cst.ident_b[:], tmpb[j][:], start=False, stop=(n == NS - 1)),
                        reads=["identb", f"{P}tmpb{j}"], writes=[P+"psy"], tag="yacc")
              isr = dt_i % 2
              ksr = f"{P}sresc{isr}"
              if "r" in DOPS:
                em.dma("sync", sresc[isr][:], sres_dram[dt_i, :, t0:t0 + TC], writes=[ksr], tag="p3s")
              if "f" in DOPS:
                em.v(lambda e, isr=isr, dt_i=dt_i: e.tensor_mul(yfin[:, dt_i, :], ps_y[:, :], sresc[isr][:]),
                     reads=[ksr], psreads=[P+"psy"], writes=[P+"yfin"])
            for q in range(TC // 128):
                iq = q % 2
                rows = slice(t0 + q * 128, t0 + (q + 1) * 128)
                em.dma("sync", xres[iq][:], x_in[rows, :], writes=[f"{P}xres{iq}"], tag="p3x")
                for half in range(2):
                    for dt_i in range(NDT):
                        em.t(lambda e, q=q, dt_i=dt_i, half=half: e.matmul(
                            ps_o[half][:, :],
                            yfin[:, dt_i, q * 128:(q + 1) * 128],
                            opw_b[:, dt_i, half * 512:(half + 1) * 512],
                            start=(dt_i == 0), stop=(dt_i == NDT - 1)),
                            reads=[P+"yfin", P+"opw"], writes=[f"{P}pso{half}"], tag="op")
                for half in range(2):
                    em.v(lambda e, iq=iq, half=half: e.tensor_add(
                        s_t[iq][:, half * 512:(half + 1) * 512], ps_o[half][:, :],
                        xres[iq][:, half * 512:(half + 1) * 512]),
                        reads=[f"{P}xres{iq}"], psreads=[f"{P}pso{half}"], writes=[f"{P}st{iq}"])
                emit_ln(em, P, s_t[iq], f"{P}st{iq}", sq_t, red, outt[iq], f"{P}outt{iq}")
                em.dma("scalar", x_out[rows, :], outt[iq][:], reads=[f"{P}outt{iq}"], tag="p3o")
        em.barrier()




T = 4096
D = 1024
E = 8
K = 2
H = 4096
CAP = 640
NTT = T // 128
NST = E * CAP // 128
BIGP = 1.0e6
EPS = 1e-5


def l0f_host_prep(w, pref=""):
    b1 = w["b1"].reshape(E, 32, 128).transpose(2, 0, 1).copy()
    d = {
        "wr": np.ascontiguousarray(w["Wr"]),
        "w1": np.ascontiguousarray(w["W1"]),
        "b1": b1,
        "w2": np.ascontiguousarray(w["W2"]),
        "b2": np.ascontiguousarray(w["b2"]),
    }
    return {pref + k: v for k, v in d.items()}


def host_consts2():
    lstrict = np.tril(np.ones((128, 128), np.float32), -1).T
    return {"lstrict": lstrict.astype(ml_dtypes.bfloat16)}


def declare_consts2(nc):
    return {"lstrict": nc.dram_tensor("lstrict", [128, 128], BF16, kind="ExternalInput").ap()}


def l0f_declare_dram(nc, pref=""):
    d = {}
    def inp(name, shape, dt=F32):
        d[name] = nc.dram_tensor(pref + name, shape, dt, kind="ExternalInput").ap()
    inp("wr", [D, E])
    inp("w1", [E, D, H])
    inp("b1", [128, E, 32])
    inp("w2", [E, H, D])
    inp("b2", [E, D])
    return d


def l0f_emit(em, nc, ctx, cst, cst2, dw, x_in, x_out, pref="l0f"):
    P = pref
    def mksb(c):
        def sb(name, shape, dtype=F32):
            return c.enter_context(nc.sbuf_tensor(P + "sb_" + name, shape, dtype))
        return sb
    def mkps(c):
        def ps(name, shape, dtype=F32):
            return c.enter_context(nc.psum_tensor(P + "pp_" + name, shape, dtype))
        return ps
    sbp = mksb(ctx)

    slot_dram = nc.dram_tensor(P + "slot_scr", [E * CAP, 1], U32, kind="Internal").ap()
    y_dram = nc.dram_tensor(P + "y_scr", [E * CAP, D], BF16, kind="Internal").ap()

    lstr = sbp("lstr", [128, 128], BF16)
    wr_s = sbp("wr", [128, 8, E], F32)
    b1_s = sbp("b1", [128, E, 32], F32)
    addr_sb = sbp("addr", [128, NTT, K], U32)
    pbuf = sbp("pbuf", [128, NTT, K], F32)
    io8 = sbp("io8", [128, E], F32)
    io8B = sbp("io8B", [128, E], F32)
    tok_sb = sbp("tok", [128, NTT], U32)
    onesf1 = sbp("onesf1", [1, 128], F32)
    onesb1 = sbp("onesb1", [128, 1], BF16)
    onesbr = sbp("onesbr", [1, 128], BF16)
    red = sbp("red", [128, 4], F32)
    sq_t = sbp("sq", [128, D], F32)

    em.dma("sync", lstr[:], cst2["lstrict"][:, :], writes=[P+"lstr"])
    em.dma("sync", wr_s[:], dw["wr"].rearrange("(c p) m -> p c m", p=128), writes=[P+"wr"])
    em.dma("sync", b1_s[:], dw["b1"][:, :, :], writes=[P+"b1"])
    em.g(lambda e: e.iota(io8[:], pattern=[[1, E]], base=0, channel_multiplier=0,
                          allow_small_or_imprecise_dtypes=True), writes=[P+"io8"])
    em.g(lambda e: e.iota(io8B[:], pattern=[[1, E]], base=int(BIGP), channel_multiplier=0,
                          allow_small_or_imprecise_dtypes=True), writes=[P+"io8B"])
    em.g(lambda e: e.iota(tok_sb[:], pattern=[[128, NTT]], base=0, channel_multiplier=1),
         writes=[P+"tok"])
    bc_slot = nc.gpsimd.alloc_register(P + "bcslot")
    bc_tok = nc.gpsimd.alloc_register(P + "bctok")
    em.raw("gpsimd", lambda e: e.reg_mov(bc_slot, E * CAP - 1))
    em.raw("gpsimd", lambda e: e.reg_mov(bc_tok, T - 1))
    em.v(lambda e: e.memset(onesf1[:], 1.0), writes=[P+"onesf1"])
    em.v(lambda e: e.memset(onesb1[:], 1.0), writes=[P+"onesb1"])
    em.v(lambda e: e.memset(onesbr[:], 1.0), writes=[P+"onesbr"])

    ctxR = contextlib.ExitStack()
    sbr = mksb(ctxR)
    with ctxR:
        psr = mkps(ctxR)
        ps_log = psr("pslog", [128, E])
        ps_cum = psr("pscum", [128, E])
        ps_row = psr("psrow", [1, E])
        ps_car = psr("pscar", [128, E])
        xld = [sbr(f"xld{i}", [128, D], F32) for i in range(2)]
        xfm = sbr("xfm", [128, 8, 128], F32)
        pstrf = [psr(f"pstrf{i}", [128, 128]) for i in range(2)]
        probs = sbr("probs", [128, E])
        prow = sbr("prow", [128, E])
        scr = sbr("scr", [128, E])
        eqf = [sbr(f"eqf{i}", [128, E]) for i in range(2)]
        rsum2 = sbr("rsum2", [128, E], BF16)
        carry = sbr("carry", [128, E])
        posx = sbr("posx", [128, E])
        row_sb = sbr("rowsb", [1, E])
        cols = sbr("cols", [128, 12])
        addrf = sbr("addrf", [128, 2])
        em.v(lambda e: e.memset(carry[:], 0.0), writes=[P+"carry"])

        for tt in range(NTT):
            i = tt % 2
            kxl = f"{P}xld{i}"
            em.dma("sync", xld[i][:], x_in[tt * 128:(tt + 1) * 128, :], writes=[kxl])
            for dc in range(8):
                sl = dc % 2
                em.t(lambda e, i=i, dc=dc, sl=sl: e.transpose(
                    pstrf[sl][:, :], xld[i][:, dc * 128:(dc + 1) * 128], cst.ident_f[:]),
                    reads=[kxl, "identf"], writes=[f"{P}pstrf{sl}"])
                em.v(lambda e, dc=dc, sl=sl: e.tensor_copy(xfm[:, dc, :], pstrf[sl][:, :]),
                     psreads=[f"{P}pstrf{sl}"], writes=[P+"xfm"])
            for dc in range(8):
                em.t(lambda e, dc=dc: e.matmul(
                    ps_log[:, :], xfm[:, dc, :], wr_s[:, dc, :],
                    start=(dc == 0), stop=(dc == 7)),
                    reads=[P+"xfm", P+"wr"], writes=[P+"pslog"], tag="router")
            em.v(lambda e: e.reduce_max(cols[:, 0:1], ps_log[:, :], axis=AX.X),
                 psreads=[P+"pslog"], writes=[P+"cols"])
            em.v(lambda e: e.tensor_scalar_mul(cols[:, 0:1], cols[:, 0:1], -1.0),
                 reads=[P+"cols"], writes=[P+"cols"])
            em.sc(lambda e: e.activation(out=prow[:], in_=ps_log[:, :], func=AF.Exp,
                                         bias=cols[:, 0:1], accum_out=cols[:, 9:10]),
                  reads=[P+"cols"], psreads=[P+"pslog"], writes=[P+"prow", P+"cols"])
            em.v(lambda e: e.reciprocal(cols[:, 10:11], cols[:, 9:10]),
                 reads=[P+"cols"], writes=[P+"cols"])
            em.v(lambda e: e.tensor_scalar_mul(probs[:], prow[:], cols[:, 10:11]),
                 reads=[P+"prow", P+"cols"], writes=[P+"probs"])
            em.v(lambda e: e.reduce_max(cols[:, 3:4], probs[:], axis=AX.X),
                 reads=[P+"probs"], writes=[P+"cols"])
            em.v(lambda e: e.tensor_single_scalar(scr[:], probs[:], cols[:, 3:4], ALU.is_equal),
                 reads=[P+"probs", P+"cols"], writes=[P+"scr"])
            em.v(lambda e: e.scalar_tensor_tensor(scr[:], scr[:], -BIGP, io8B[:], ALU.mult, ALU.add),
                 reads=[P+"scr", P+"io8B"], writes=[P+"scr"])
            em.v(lambda e: e.tensor_reduce(cols[:, 1:2], scr[:], axis=AX.X, op=ALU.min),
                 reads=[P+"scr"], writes=[P+"cols"])
            em.v(lambda e: e.tensor_single_scalar(eqf[0][:], io8[:], cols[:, 1:2], ALU.is_equal),
                 reads=[P+"io8", P+"cols"], writes=[P+"eqf0"])
            em.v(lambda e: e.scalar_tensor_tensor(prow[:], eqf[0][:], -2.0, probs[:], ALU.mult, ALU.add),
                 reads=[P+"eqf0", P+"probs"], writes=[P+"prow"])
            em.v(lambda e: e.reduce_max(cols[:, 4:5], prow[:], axis=AX.X),
                 reads=[P+"prow"], writes=[P+"cols"])
            em.v(lambda e: e.tensor_single_scalar(scr[:], prow[:], cols[:, 4:5], ALU.is_equal),
                 reads=[P+"prow", P+"cols"], writes=[P+"scr"])
            em.v(lambda e: e.scalar_tensor_tensor(scr[:], scr[:], -BIGP, io8B[:], ALU.mult, ALU.add),
                 reads=[P+"scr", P+"io8B"], writes=[P+"scr"])
            em.v(lambda e: e.tensor_reduce(cols[:, 2:3], scr[:], axis=AX.X, op=ALU.min),
                 reads=[P+"scr"], writes=[P+"cols"])
            em.v(lambda e: e.tensor_single_scalar(eqf[1][:], io8[:], cols[:, 2:3], ALU.is_equal),
                 reads=[P+"io8", P+"cols"], writes=[P+"eqf1"])
            em.v(lambda e, tt=tt: e.tensor_copy(pbuf[:, tt, :], cols[:, 3:5]),
                 reads=[P+"cols"], writes=[P+"pbuf"])
            em.v(lambda e: e.tensor_add(rsum2[:], eqf[0][:], eqf[1][:]),
                 reads=[P+"eqf0", P+"eqf1"], writes=[P+"rsum2"])
            em.t(lambda e: e.matmul(ps_cum[:, :], lstr[:], rsum2[:], start=True, stop=True),
                 reads=[P+"lstr", P+"rsum2"], writes=[P+"pscum"], tag="cum")
            em.v(lambda e: e.tensor_add(posx[:], ps_cum[:, :], carry[:]),
                 reads=[P+"carry"], psreads=[P+"pscum"], writes=[P+"posx"])
            em.t(lambda e: e.matmul(ps_row[:, :], onesb1[:], rsum2[:], start=True, stop=True),
                 reads=[P+"onesb1", P+"rsum2"], writes=[P+"psrow"], tag="rowsum")
            em.v(lambda e: e.tensor_copy(row_sb[:], ps_row[:, :]),
                 psreads=[P+"psrow"], writes=[P+"rowsb"])
            em.t(lambda e: e.matmul(ps_car[:, :], onesf1[:], row_sb[:], start=True, stop=True),
                 reads=[P+"onesf1", P+"rowsb"], writes=[P+"pscar"], tag="carbc")
            em.v(lambda e: e.tensor_add(carry[:], carry[:], ps_car[:, :]),
                 reads=[P+"carry"], psreads=[P+"pscar"], writes=[P+"carry"])
            for k in range(K):
                em.v(lambda e, k=k: e.tensor_mul(scr[:], posx[:], eqf[k][:]),
                     reads=[P+"posx", f"{P}eqf{k}"], writes=[P+"scr"])
                em.v(lambda e, k=k: e.reduce_sum(cols[:, 5 + k:6 + k], scr[:], axis=AX.X),
                     reads=[P+"scr"], writes=[P+"cols"])
                em.v(lambda e, k=k: e.scalar_tensor_tensor(
                    addrf[:, k:k + 1], cols[:, 1 + k:2 + k], float(CAP), cols[:, 5 + k:6 + k],
                    ALU.mult, ALU.add),
                    reads=[P+"cols"], writes=[P+"addrf"])
                em.v(lambda e, k=k: e.tensor_scalar(
                    cols[:, 9:10], cols[:, 5 + k:6 + k], float(CAP), 8192.0, ALU.is_ge, ALU.mult),
                    reads=[P+"cols"], writes=[P+"cols"])
                em.v(lambda e, k=k: e.tensor_add(addrf[:, k:k + 1], addrf[:, k:k + 1], cols[:, 9:10]),
                     reads=[P+"cols", P+"addrf"], writes=[P+"addrf"])
            em.v(lambda e, tt=tt: e.tensor_copy(addr_sb[:, tt, :], addrf[:]),
                 reads=[P+"addrf"], writes=[P+"addr"])
        bigt = sbr("bigt", [128, 40], U32)
        em.v(lambda e: e.memset(bigt[:], 1 << 20), writes=[P+"bigt"])
        em.dma("gpsimd", slot_dram.rearrange("(a b) c -> a (b c)", a=128), bigt[:],
               reads=[P+"bigt"], writes=[P+"slotd"], tag="slotinit")
        for tt in range(NTT):
            for k in range(K):
                em.op("gpsimd", lambda e, tt=tt, k=k: e.indirect_dma_start(
                    out=slot_dram[:, :],
                    out_offset=bass.IndirectOffsetOnAxis(addr_sb[:, tt, k:k + 1], 0),
                    in_=tok_sb[:, tt:tt + 1],
                    in_offset=None,
                    bounds_check=bc_slot,
                    oob_is_err=False),
                    reads=[P+"addr", P+"tok", P+"slotd"], writes=[P+"slotd"],
                    dma=True, tag="scat")
        em.barrier()

    ctxF = contextlib.ExitStack()
    sbf = mksb(ctxF)
    with ctxF:
        psf = mkps(ctxF)
        ps_h = [psf(f"psh{i}", [128, 320]) for i in range(2)]
        ps_y = [psf(f"psy{i}", [128, 512]) for i in range(2)]
        ps_tr = [psf(f"pstr{i}", [128, 128], BF16) for i in range(2)]
        bufE = sbf("bufE", [128, 8, CAP], BF16)
        hfm = sbf("hfm", [128, 32, CAP], BF16)
        w1t = [sbf(f"w1t{i}", [128, 8, 1024], BF16) for i in range(2)]
        w2t = sbf("w2t", [128, 32, 1024], BF16)
        slot_ids = sbf("slotids", [128, 5], U32)
        gstag = [sbf(f"gst{i}", [128, D], BF16) for i in range(2)]
        b2row = sbf("b2row", [1, D], F32)
        b2hi = sbf("b2hi", [1, D], BF16)
        b2lo = sbf("b2lo", [1, D], BF16)
        b2lof = sbf("b2lof", [1, D], F32)
        ytile = sbf("ytile", [128, D], BF16)

        for e_i in range(E):
            for st in range(5):
                em.dma("sync", slot_ids[:, st:st + 1],
                       slot_dram[e_i * CAP + st * 128:e_i * CAP + (st + 1) * 128, :],
                       reads=[P+"slotd"], writes=[P+"slotids"], tag="sids")
            em.dma("sync", b2row[:], dw["b2"][e_i:e_i + 1, :], writes=[P+"b2row"])
            em.v(lambda e: e.tensor_copy(b2hi[:], b2row[:]), reads=[P+"b2row"], writes=[P+"b2hi"])
            em.v(lambda e: e.tensor_copy(b2lof[:], b2hi[:]), reads=[P+"b2hi"], writes=[P+"b2lof"])
            em.v(lambda e: e.tensor_sub(b2lof[:], b2row[:], b2lof[:]),
                 reads=[P+"b2row", P+"b2lof"], writes=[P+"b2lof"])
            em.v(lambda e: e.tensor_copy(b2lo[:], b2lof[:]), reads=[P+"b2lof"], writes=[P+"b2lo"])
            for st in range(5):
                i = st % 2
                kg = f"{P}gst{i}"
                em.v(lambda e, i=i: e.memset(gstag[i][:], 0.0), writes=[kg])
                em.op("gpsimd", lambda e, st=st, i=i: e.indirect_dma_start(
                    out=gstag[i][:, :],
                    out_offset=None,
                    in_=x_in[:, :],
                    in_offset=bass.IndirectOffsetOnAxis(slot_ids[:, st:st + 1], 0),
                    bounds_check=bc_tok,
                    oob_is_err=False),
                    reads=[P+"slotids", kg], writes=[kg], dma=True, tag="gath")
                for dc in range(8):
                    sl = dc % 2
                    em.t(lambda e, i=i, dc=dc, sl=sl: e.transpose(
                        ps_tr[sl][:, :], gstag[i][:, dc * 128:(dc + 1) * 128], cst.ident_b[:]),
                        reads=[kg, "identb"], writes=[f"{P}pstr{sl}"])
                    em.v(lambda e, st=st, dc=dc, sl=sl: e.tensor_copy(
                        bufE[:, dc, st * 128:(st + 1) * 128], ps_tr[sl][:, :]),
                        psreads=[f"{P}pstr{sl}"], writes=[P+"bufE"])
            for q in range(4):
                iw = q % 2
                kw = f"{P}w1t{iw}"
                em.dma("gpsimd", w1t[iw][:],
                       dw["w1"][e_i, :, q * 1024:(q + 1) * 1024].rearrange("(c p) m -> p c m", p=128),
                       writes=[kw], tag="w1")
                for hh in range(8):
                    ht = q * 8 + hh
                    for w in range(2):
                        jp = (ht * 2 + w) % 2
                        kp = f"{P}psh{jp}"
                        for dc in range(8):
                            em.t(lambda e, iw=iw, hh=hh, w=w, dc=dc, jp=jp: e.matmul(
                                ps_h[jp][:, :], w1t[iw][:, dc, hh * 128:(hh + 1) * 128],
                                bufE[:, dc, w * 320:(w + 1) * 320],
                                start=(dc == 0), stop=(dc == 7)),
                                reads=[kw, P+"bufE"], writes=[kp], tag="w1mm")
                        em.sc(lambda e, ht=ht, w=w, e_i=e_i, jp=jp: e.activation(
                            out=hfm[:, ht, w * 320:(w + 1) * 320], in_=ps_h[jp][:, :], func=AF.Relu,
                            bias=b1_s[:, e_i, ht:ht + 1]),
                            reads=[P+"b1"], psreads=[kp], writes=[P+"hfm"])
            for q in range(4):
                em.dma("gpsimd", w2t[:, q * 8:(q + 1) * 8, :],
                       dw["w2"][e_i, q * 1024:(q + 1) * 1024, :].rearrange("(c p) m -> p c m", p=128),
                       writes=[P+"w2t"], tag="w2")
            for st in range(5):
                for cw in range(2):
                    jp = (st * 2 + cw) % 2
                    kp = f"{P}psy{jp}"
                    em.t(lambda e, cw=cw, jp=jp: e.matmul(
                        ps_y[jp][:, :], onesbr[:], b2hi[:, cw * 512:(cw + 1) * 512],
                        start=True, stop=False),
                        reads=[P+"onesbr", P+"b2hi"], writes=[kp], tag="b2a")
                    em.t(lambda e, cw=cw, jp=jp: e.matmul(
                        ps_y[jp][:, :], onesbr[:], b2lo[:, cw * 512:(cw + 1) * 512],
                        start=False, stop=False),
                        reads=[P+"onesbr", P+"b2lo"], writes=[kp], tag="b2b")
                    for hc in range(32):
                        em.t(lambda e, st=st, cw=cw, hc=hc, jp=jp: e.matmul(
                            ps_y[jp][:, :], hfm[:, hc, st * 128:(st + 1) * 128],
                            w2t[:, hc, cw * 512:(cw + 1) * 512],
                            start=False, stop=(hc == 31)),
                            reads=[P+"hfm", P+"w2t"], writes=[kp], tag="w2mm")
                    em.v(lambda e, cw=cw, jp=jp: e.tensor_copy(
                        ytile[:, cw * 512:(cw + 1) * 512], ps_y[jp][:, :]),
                        psreads=[kp], writes=[P+"ytile"])
                em.dma("scalar", y_dram[e_i * CAP + st * 128:e_i * CAP + (st + 1) * 128, :],
                       ytile[:], reads=[P+"ytile"], writes=[P+"ydram"], tag="yst")
        em.barrier()

    ctxC = contextlib.ExitStack()
    sbc = mksb(ctxC)
    with ctxC:
        ybt = [sbc(f"ybt{i}", [128, D], BF16) for i in range(2)]
        xres = [sbc(f"xres{i}", [128, D], F32) for i in range(2)]
        s_t = [sbc(f"sc{i}", [128, D], F32) for i in range(2)]
        outt = [sbc(f"outc{i}", [128, D], F32) for i in range(2)]
        for tt in range(NTT):
            iq = tt % 2
            rows = slice(tt * 128, (tt + 1) * 128)
            em.dma("sync", xres[iq][:], x_in[rows, :], writes=[f"{P}xres{iq}"])
            em.v(lambda e, iq=iq: e.tensor_copy(s_t[iq][:], xres[iq][:]),
                 reads=[f"{P}xres{iq}"], writes=[f"{P}sc{iq}"])
            for k in range(K):
                em.v(lambda e, iq=iq: e.memset(ybt[iq][:], 0.0), writes=[f"{P}ybt{iq}"])
                em.op("gpsimd", lambda e, tt=tt, k=k, iq=iq: e.indirect_dma_start(
                    out=ybt[iq][:, :],
                    out_offset=None,
                    in_=y_dram[:, :],
                    in_offset=bass.IndirectOffsetOnAxis(addr_sb[:, tt, k:k + 1], 0),
                    bounds_check=bc_slot,
                    oob_is_err=False),
                    reads=[P+"addr", P+"ydram", f"{P}ybt{iq}"], writes=[f"{P}ybt{iq}"],
                    dma=True, tag="gathy")
                em.v(lambda e, tt=tt, k=k, iq=iq: e.scalar_tensor_tensor(
                    s_t[iq][:], ybt[iq][:], pbuf[:, tt, k:k + 1], s_t[iq][:], ALU.mult, ALU.add),
                    reads=[f"{P}ybt{iq}", P+"pbuf", f"{P}sc{iq}"], writes=[f"{P}sc{iq}"])
            emit_ln(em, P, s_t[iq], f"{P}sc{iq}", sq_t, red, outt[iq], f"{P}outc{iq}")
            em.dma("scalar", x_out[rows, :], outt[iq][:], reads=[f"{P}outc{iq}"], tag="xout")
        em.barrier()




T = 4096
D = 1024
NH = 16
HD = 64
W2 = 128
NTT = T // 128
EPS = 1e-5
NEG = -30000.0


def l1a_host_prep(w, pref=""):
    d = {
        "wqkT": np.ascontiguousarray(w["w_qkv"][0:2048, :].T),
        "wvT": np.ascontiguousarray(w["w_qkv"][2048:3072, :].T),
        "woT": np.ascontiguousarray(w["w_out"].T),
        "gamma": w["gamma"].reshape(1, D),
        "beta": w["beta"].reshape(1, D),
    }
    return {pref + k: v for k, v in d.items()}


def host_consts3():
    p = np.arange(128)[:, None]
    c = np.arange(384)[None, :]
    mi = np.where(np.abs(c - 128 - p) <= W2, 0.0, NEG).astype(np.float32)
    c2 = np.arange(256)[None, :]
    mf = np.where(c2 - p <= W2, 0.0, NEG).astype(np.float32)
    ml = np.where(c2 >= p, 0.0, NEG).astype(np.float32)
    return {"maski": mi, "maskf": mf, "maskl": ml}


def declare_consts3(nc):
    return {
        "maski": nc.dram_tensor("maski", [128, 384], F32, kind="ExternalInput").ap(),
        "maskf": nc.dram_tensor("maskf", [128, 256], F32, kind="ExternalInput").ap(),
        "maskl": nc.dram_tensor("maskl", [128, 256], F32, kind="ExternalInput").ap(),
    }


def l1a_declare_dram(nc, pref=""):
    d = {}
    def inp(name, shape, dt=F32):
        d[name] = nc.dram_tensor(pref + name, shape, dt, kind="ExternalInput").ap()
    inp("wqkT", [D, 2048])
    inp("wvT", [D, D])
    inp("woT", [D, D])
    inp("gamma", [1, D])
    inp("beta", [1, D])
    return d


def l1a_emit(em, nc, ctx, cst, cst3, dw, x_in, x_out, pref="l1a"):
    P = pref
    def mksb(c):
        def sb(name, shape, dtype=F32):
            return c.enter_context(nc.sbuf_tensor(P + "sb_" + name, shape, dtype))
        return sb
    def mkps(c):
        def ps(name, shape, dtype=F32):
            return c.enter_context(nc.psum_tensor(P + "pp_" + name, shape, dtype))
        return ps
    sbp = mksb(ctx)

    xn_dram = nc.dram_tensor(P + "xn_scr", [T, D], BF16, kind="Internal").ap()

    wqk_b = sbp("wqk", [128, 8, 2048], BF16)
    wv_b = sbp("wv", [128, 8, D], BF16)
    wo_b = sbp("wo", [128, 8, D], BF16)
    gbc = sbp("gbc", [128, D])
    bbc = sbp("bbc", [128, D])
    grow = sbp("grow", [1, D])
    brow = sbp("brow", [1, D])
    onesf1 = sbp("onesf1", [1, 128], F32)
    maski = sbp("maski", [128, 384])
    maskf = sbp("maskf", [128, 256])
    maskl = sbp("maskl", [128, 256])
    red = sbp("red", [128, 4], F32)
    sq_t = sbp("sq", [128, D], F32)

    em.dma("gpsimd", wqk_b[:], dw["wqkT"].rearrange("(c p) m -> p c m", p=128), writes=[P+"wqk"])
    em.dma("gpsimd", wv_b[:], dw["wvT"].rearrange("(c p) m -> p c m", p=128), writes=[P+"wv"])
    em.dma("gpsimd", wo_b[:], dw["woT"].rearrange("(c p) m -> p c m", p=128), writes=[P+"wo"])
    em.dma("sync", grow[:], dw["gamma"][:, :], writes=[P+"grow"])
    em.dma("sync", brow[:], dw["beta"][:, :], writes=[P+"brow"])
    em.dma("sync", maski[:], cst3["maski"][:, :], writes=[P+"maski"])
    em.dma("sync", maskf[:], cst3["maskf"][:, :], writes=[P+"maskf"])
    em.dma("sync", maskl[:], cst3["maskl"][:, :], writes=[P+"maskl"])
    em.v(lambda e: e.memset(onesf1[:], 1.0), writes=[P+"onesf1"])

    ctxN = contextlib.ExitStack()
    sbn = mksb(ctxN)
    with ctxN:
        psn = mkps(ctxN)
        ps_g = psn("psg", [128, 512])
        ps_g2 = psn("psg2", [128, 512])
        xld = [sbn(f"xld{i}", [128, D], F32) for i in range(2)]
        xnb = [sbn(f"xnb{i}", [128, D], BF16) for i in range(2)]
        tmp = sbn("tmp", [128, D], F32)
        for half in range(2):
            row = grow if half == 0 else brow
            krow = P + ("grow" if half == 0 else "brow")
            dst = gbc if half == 0 else bbc
            kdst = P + ("gbc" if half == 0 else "bbc")
            for hw in range(2):
                pg = ps_g if hw == 0 else ps_g2
                kpg = P + ("psg0" if hw == 0 else "psg1")
                em.t(lambda e, pg=pg, row=row, hw=hw: e.matmul(
                    pg[:, :], onesf1[:], row[:, hw * 512:(hw + 1) * 512],
                    start=True, stop=True),
                    reads=[P+"onesf1", krow], writes=[kpg], tag="gb")
                em.v(lambda e, dst=dst, hw=hw, pg=pg: e.tensor_copy(
                    dst[:, hw * 512:(hw + 1) * 512], pg[:, :]),
                    psreads=[kpg], writes=[kdst])
        for tt in range(NTT):
            i = tt % 2
            kxl = f"{P}xld{i}"
            rows = slice(tt * 128, (tt + 1) * 128)
            em.dma("sync", xld[i][:], x_in[rows, :], writes=[kxl])
            em.v(lambda e, i=i: e.reduce_sum(red[:, 0:1], xld[i][:], axis=AX.X),
                 reads=[kxl], writes=[P+"red"])
            em.v(lambda e: e.tensor_scalar_mul(red[:, 1:2], red[:, 0:1], -1.0 / D),
                 reads=[P+"red"], writes=[P+"red"])
            em.v(lambda e, i=i: e.tensor_scalar_add(tmp[:], xld[i][:], red[:, 1:2]),
                 reads=[kxl, P+"red"], writes=[P+"tmp"])
            em.sc(lambda e: e.activation(out=sq_t[:], in_=tmp[:], func=AF.Square,
                                         accum_out=red[:, 2:3]),
                  reads=[P+"tmp"], writes=[P+"sq", P+"red"])
            em.v(lambda e: e.tensor_scalar(red[:, 3:4], red[:, 2:3], 1.0 / D, EPS, ALU.mult, ALU.add),
                 reads=[P+"red"], writes=[P+"red"])
            em.sc(lambda e: e.activation(out=red[:, 3:4], in_=red[:, 3:4], func=AF.Ln),
                  reads=[P+"red"], writes=[P+"red"])
            em.sc(lambda e: e.activation(out=red[:, 3:4], in_=red[:, 3:4], func=AF.Exp, scale=-0.5),
                  reads=[P+"red"], writes=[P+"red"])
            em.v(lambda e: e.tensor_scalar_mul(tmp[:], tmp[:], red[:, 3:4]),
                 reads=[P+"tmp", P+"red"], writes=[P+"tmp"])
            em.v(lambda e: e.tensor_mul(tmp[:], tmp[:], gbc[:]),
                 reads=[P+"tmp", P+"gbc"], writes=[P+"tmp"])
            em.v(lambda e, i=i: e.tensor_add(xnb[i][:], tmp[:], bbc[:]),
                 reads=[P+"tmp", P+"bbc"], writes=[f"{P}xnb{i}"])
            em.dma("scalar", xn_dram[rows, :], xnb[i][:], reads=[f"{P}xnb{i}"],
                   writes=[P+"xnd"], tag="xnst")
        em.barrier()

    ctxK = contextlib.ExitStack()
    sbk = mksb(ctxK)
    kfm = ctx.enter_context(nc.sbuf_tensor(P + "sb_kfm", [128, 8, T], BF16))
    with ctxK:
        psk = mkps(ctxK)
        ps_trK = [psk(f"pstr{i}", [128, 128], BF16) for i in range(2)]
        ps_kK = [psk(f"psk{i}", [128, 128]) for i in range(2)]
        xldbK = [sbk(f"xldbK{i}", [128, D], BF16) for i in range(2)]
        xfmtK = sbk("xfmtK", [128, 8, 128], BF16)
        for tt in range(NTT):
            i = tt % 2
            kxl = f"{P}xldbK{i}"
            em.dma("sync", xldbK[i][:], xn_dram[tt * 128:(tt + 1) * 128, :],
                   reads=[P+"xnd"], writes=[kxl])
            for dc in range(8):
                sl = dc % 2
                em.t(lambda e, i=i, dc=dc, sl=sl: e.transpose(
                    ps_trK[sl][:, :], xldbK[i][:, dc * 128:(dc + 1) * 128], cst.ident_b[:]),
                    reads=[kxl, "identb"], writes=[f"{P}pstr{sl}"])
                em.v(lambda e, dc=dc, sl=sl: e.tensor_copy(xfmtK[:, dc, :], ps_trK[sl][:, :]),
                     psreads=[f"{P}pstr{sl}"], writes=[P+"xfmtK"])
            for kf in range(8):
                sl = kf % 2
                kp = f"{P}psk{sl}"
                for dc in range(8):
                    em.t(lambda e, kf=kf, dc=dc, sl=sl: e.matmul(
                        ps_kK[sl][:, :], wqk_b[:, dc, 1024 + kf * 128:1024 + (kf + 1) * 128],
                        xfmtK[:, dc, :], start=(dc == 0), stop=(dc == 7)),
                        reads=[P+"wqk", P+"xfmtK"], writes=[kp], tag="kmm")
                em.sc(lambda e, kf=kf, tt=tt, sl=sl: e.copy(
                    kfm[:, kf, tt * 128:(tt + 1) * 128], ps_kK[sl][:, :]),
                    psreads=[kp], writes=[P+"kfm"])
        em.barrier()

    ctxA = contextlib.ExitStack()
    sba = mksb(ctxA)
    with ctxA:
        psa = mkps(ctxA)
        ps_q = [psa(f"psq{i}", [128, 128]) for i in range(2)]
        ps_v = [psa(f"psv{i}", [128, 512]) for i in range(2)]
        ps_s = psa("pss", [128, 384])
        ps_trA = psa("pstrA", [128, 128], BF16)
        ps_av = psa("psav", [128, 128])
        ps_o = psa("pso", [128, 512])
        xnr = sba("xnr", [128, 2, 8, 128], BF16)
        vring = sba("vring", [128, 3, D], BF16)
        q_sb = sba("qsb", [128, 8, 128], BF16)
        att = sba("att", [128, 384])
        attb = sba("attb", [128, 384], BF16)
        attT = sba("attT", [128, 3, 128], BF16)
        aout = sba("aout", [128, 8, 128], BF16)
        xres = [sba(f"xres{i}", [128, D], F32) for i in range(2)]
        s_t = [sba(f"sa{i}", [128, D], F32) for i in range(2)]
        outt = [sba(f"outa{i}", [128, D], F32) for i in range(2)]
        xldbA = [sba(f"xldc{i}", [128, D], BF16) for i in range(2)]

        def load_xn_tile(tile, slot):
            i = tile % 2
            kxl = f"{P}xldc{i}"
            em.dma("sync", xldbA[i][:], xn_dram[tile * 128:(tile + 1) * 128, :],
                   reads=[P+"xnd"], writes=[kxl])
            for dc in range(8):
                sl = dc % 2
                em.t(lambda e, i=i, dc=dc: e.transpose(
                    ps_trA[:, :], xldbA[i][:, dc * 128:(dc + 1) * 128], cst.ident_b[:]),
                    reads=[kxl, "identb"], writes=[P+"pstrA"])
                em.v(lambda e, dc=dc, slot=slot: e.tensor_copy(xnr[:, slot, dc, :], ps_trA[:, :]),
                     psreads=[P+"pstrA"], writes=[P+"xnr"])

        def compute_v_tile(tile, vslot, xslot):
            for hw in range(2):
                kp = f"{P}psv{hw}"
                for dc in range(8):
                    em.t(lambda e, hw=hw, dc=dc, xslot=xslot: e.matmul(
                        ps_v[hw][:, :], xnr[:, xslot, dc, :], wv_b[:, dc, hw * 512:(hw + 1) * 512],
                        start=(dc == 0), stop=(dc == 7)),
                        reads=[P+"xnr", P+"wv"], writes=[kp], tag="vmm")
                em.sc(lambda e, hw=hw, vslot=vslot: e.copy(
                    vring[:, vslot, hw * 512:(hw + 1) * 512], ps_v[hw][:, :]),
                    psreads=[kp], writes=[P+"vring"])

        load_xn_tile(0, 0)
        compute_v_tile(0, 0, 0)
        load_xn_tile(1, 1)
        compute_v_tile(1, 1, 1)

        for qt in range(NTT):
            if qt + 1 < NTT and qt > 0:
                xslot = (qt + 1) % 2
                load_xn_tile(qt + 1, xslot)
                compute_v_tile(qt + 1, (qt + 1) % 3, xslot)
            for qf in range(8):
                sl = qf % 2
                kp = f"{P}psq{sl}"
                for dc in range(8):
                    em.t(lambda e, qf=qf, dc=dc, sl=sl, qt=qt: e.matmul(
                        ps_q[sl][:, :], wqk_b[:, dc, qf * 128:(qf + 1) * 128],
                        xnr[:, qt % 2, dc, :], start=(dc == 0), stop=(dc == 7)),
                        reads=[P+"wqk", P+"xnr"], writes=[kp], tag="qmm")
                em.sc(lambda e, qf=qf, sl=sl: e.activation(
                    out=q_sb[:, qf, :], in_=ps_q[sl][:, :], func=AF.Copy, scale=0.125),
                    psreads=[kp], writes=[P+"qsb"])
            ks = [t for t in (qt - 1, qt, qt + 1) if 0 <= t < NTT]
            kw = len(ks) * 128
            k0 = ks[0] * 128
            mask = maski if len(ks) == 3 else (maskf if qt == 0 else maskl)
            kmask = P + ("maski" if len(ks) == 3 else ("maskf" if qt == 0 else "maskl"))
            for h in range(NH):
                hp = (h % 2) * 64
                hf = h // 2
                em.t(lambda e, hp=hp, hf=hf, k0=k0, kw=kw: e.matmul(
                    ps_s[:, 0:kw], q_sb[hp:hp + 64, hf, :], kfm[hp:hp + 64, hf, k0:k0 + kw],
                    start=True, stop=True),
                    reads=[P+"qsb", P+"kfm"], writes=[P+"pss"], tag="smm")
                em.v(lambda e, kw=kw, mask=mask: e.tensor_add(att[:, 0:kw], ps_s[:, 0:kw], mask[:, 0:kw]),
                     reads=[kmask], psreads=[P+"pss"], writes=[P+"att"])
                em.v(lambda e, kw=kw: e.reduce_max(red[:, 0:1], att[:, 0:kw], axis=AX.X),
                     reads=[P+"att"], writes=[P+"red"])
                em.v(lambda e: e.tensor_scalar_mul(red[:, 1:2], red[:, 0:1], -1.0),
                     reads=[P+"red"], writes=[P+"red"])
                em.sc(lambda e, kw=kw: e.activation(out=att[:, 0:kw], in_=att[:, 0:kw], func=AF.Exp,
                                                    bias=red[:, 1:2], accum_out=red[:, 2:3]),
                      reads=[P+"att", P+"red"], writes=[P+"att", P+"red"])
                em.v(lambda e: e.reciprocal(red[:, 3:4], red[:, 2:3]),
                     reads=[P+"red"], writes=[P+"red"])
                em.sc(lambda e, kw=kw: e.mul(attb[:, 0:kw], att[:, 0:kw], red[:, 3:4]),
                      reads=[P+"att", P+"red"], writes=[P+"attb"])
                for j in range(len(ks)):
                    em.t(lambda e, j=j: e.transpose(ps_trA[:, :], attb[:, j * 128:(j + 1) * 128],
                                                    cst.ident_b[:]),
                         reads=[P+"attb", "identb"], writes=[P+"pstrA"])
                    em.v(lambda e, j=j: e.tensor_copy(attT[:, j, :], ps_trA[:, :]),
                         psreads=[P+"pstrA"], writes=[P+"attT"])
                for j, kt in enumerate(ks):
                    em.t(lambda e, j=j, kt=kt, hp=hp, hf=hf, nk=len(ks): e.matmul(
                        ps_av[0:64, :], vring[:, kt % 3, (hf * 2 * 64 + hp):(hf * 2 * 64 + hp) + 64],
                        attT[:, j, :], start=(j == 0), stop=(j == nk - 1)),
                        reads=[P+"vring", P+"attT"], writes=[P+"psav"], tag="avmm")
                em.v(lambda e, hp=hp, hf=hf: e.tensor_copy(aout[hp:hp + 64, hf, :], ps_av[0:64, :]),
                     psreads=[P+"psav"], writes=[P+"aout"])
            iq = qt % 2
            rows = slice(qt * 128, (qt + 1) * 128)
            em.dma("sync", xres[iq][:], x_in[rows, :], writes=[f"{P}xres{iq}"])
            for hw in range(2):
                for dc in range(8):
                    em.t(lambda e, hw=hw, dc=dc: e.matmul(
                        ps_o[:, :], aout[:, dc, :], wo_b[:, dc, hw * 512:(hw + 1) * 512],
                        start=(dc == 0), stop=(dc == 7)),
                        reads=[P+"aout", P+"wo"], writes=[P+"pso"], tag="omm")
                em.v(lambda e, iq=iq, hw=hw: e.scalar_tensor_tensor(
                    s_t[iq][:, hw * 512:(hw + 1) * 512], xres[iq][:, hw * 512:(hw + 1) * 512],
                    2.0, ps_o[:, :], ALU.mult, ALU.add),
                    reads=[f"{P}xres{iq}"], psreads=[P+"pso"], writes=[f"{P}sa{iq}"])
            emit_ln(em, P, s_t[iq], f"{P}sa{iq}", sq_t, red, outt[iq], f"{P}outa{iq}")
            em.dma("scalar", x_out[rows, :], outt[iq][:], reads=[f"{P}outa{iq}"], tag="xout")
        em.barrier()




T = 4096
D = 1024
E = 8
HH = 1024
NTT = T // 128
BIGP = 1.0e6
EPS = 1e-5


def l1f_host_prep(w, pref=""):
    d = {
        "wr": np.ascontiguousarray(w["Wr"]),
        "wi": np.ascontiguousarray(w["Wi"]),
        "wo": np.ascontiguousarray(w["Wo"]),
    }
    return {pref + k: v for k, v in d.items()}


def host_consts4():
    sel = np.zeros((8, 8, 128), np.float32)
    for e in range(8):
        sel[e, e, :] = 1.0
    return {"sel8f": sel.reshape(8, 8 * 128)}


def declare_consts4(nc):
    return {"sel8f": nc.dram_tensor("sel8f", [8, 8 * 128], F32, kind="ExternalInput").ap()}


def l1f_declare_dram(nc, pref=""):
    d = {}
    def inp(name, shape, dt=F32):
        d[name] = nc.dram_tensor(pref + name, shape, dt, kind="ExternalInput").ap()
    inp("wr", [D, E])
    inp("wi", [E, D, HH])
    inp("wo", [E, HH, D])
    return d


def l1f_emit(em, nc, ctx, cst, cst4, dw, x_in, x_out, pref="l1f"):
    P = pref
    def mksb(c):
        def sb(name, shape, dtype=F32):
            return c.enter_context(nc.sbuf_tensor(P + "sb_" + name, shape, dtype))
        return sb
    def mkps(c):
        def ps(name, shape, dtype=F32):
            return c.enter_context(nc.psum_tensor(P + "pp_" + name, shape, dtype))
        return ps
    sbp = mksb(ctx)

    xbf_dram = nc.dram_tensor(P + "xbf_scr", [T, D], BF16, kind="Internal").ap()

    wr_s = sbp("wr", [128, 8, E], F32)
    cfm = sbp("cfm", [8, T], F32)
    io8 = sbp("io8", [128, E], F32)
    io8B = sbp("io8B", [128, E], F32)
    sel8f = sbp("sel8f", [8, 8 * 128], F32)
    red = sbp("red", [128, 4], F32)
    sq_t = sbp("sq", [128, D], F32)

    em.dma("sync", wr_s[:], dw["wr"].rearrange("(c p) m -> p c m", p=128), writes=[P+"wr"])
    em.dma("sync", sel8f[:], cst4["sel8f"][:, :], writes=[P+"sel8f"])
    em.g(lambda e: e.iota(io8[:], pattern=[[1, E]], base=0, channel_multiplier=0,
                          allow_small_or_imprecise_dtypes=True), writes=[P+"io8"])
    em.g(lambda e: e.iota(io8B[:], pattern=[[1, E]], base=int(BIGP), channel_multiplier=0,
                          allow_small_or_imprecise_dtypes=True), writes=[P+"io8B"])

    ctxR = contextlib.ExitStack()
    sbr = mksb(ctxR)
    with ctxR:
        psr = mkps(ctxR)
        ps_log = psr("pslog", [128, E])
        ps_ct = psr("psct", [128, 128])
        pstrf = [psr(f"pstrf{i}", [128, 128]) for i in range(2)]
        xld = [sbr(f"xld{i}", [128, D], F32) for i in range(2)]
        xbfs = [sbr(f"xbfs{i}", [128, D], BF16) for i in range(2)]
        xfmR = sbr("xfmR", [128, 8, 128], F32)
        probs = sbr("probs", [128, E])
        prow = sbr("prow", [128, E])
        scr = sbr("scr", [128, E])
        eqf = [sbr(f"eqf{i}", [128, E]) for i in range(2)]
        csb = sbr("csb", [128, E])
        cols = sbr("cols", [128, 12])

        for tt in range(NTT):
            i = tt % 2
            kxl = f"{P}xld{i}"
            rows = slice(tt * 128, (tt + 1) * 128)
            em.dma("sync", xld[i][:], x_in[rows, :], writes=[kxl])
            em.v(lambda e, i=i: e.tensor_copy(xbfs[i][:], xld[i][:]),
                 reads=[kxl], writes=[f"{P}xbfs{i}"])
            em.dma("scalar", xbf_dram[rows, :], xbfs[i][:], reads=[f"{P}xbfs{i}"],
                   writes=[P+"xbfd"], tag="xbst")
            for dc in range(8):
                sl = dc % 2
                em.t(lambda e, i=i, dc=dc, sl=sl: e.transpose(
                    pstrf[sl][:, :], xld[i][:, dc * 128:(dc + 1) * 128], cst.ident_f[:]),
                    reads=[kxl, "identf"], writes=[f"{P}pstrf{sl}"])
                em.v(lambda e, dc=dc, sl=sl: e.tensor_copy(xfmR[:, dc, :], pstrf[sl][:, :]),
                     psreads=[f"{P}pstrf{sl}"], writes=[P+"xfmR"])
            for dc in range(8):
                em.t(lambda e, dc=dc: e.matmul(
                    ps_log[:, :], xfmR[:, dc, :], wr_s[:, dc, :],
                    start=(dc == 0), stop=(dc == 7)),
                    reads=[P+"xfmR", P+"wr"], writes=[P+"pslog"], tag="router")
            em.v(lambda e: e.reduce_max(cols[:, 0:1], ps_log[:, :], axis=AX.X),
                 psreads=[P+"pslog"], writes=[P+"cols"])
            em.v(lambda e: e.tensor_scalar_mul(cols[:, 0:1], cols[:, 0:1], -1.0),
                 reads=[P+"cols"], writes=[P+"cols"])
            em.sc(lambda e: e.activation(out=prow[:], in_=ps_log[:, :], func=AF.Exp,
                                         bias=cols[:, 0:1], accum_out=cols[:, 9:10]),
                  reads=[P+"cols"], psreads=[P+"pslog"], writes=[P+"prow", P+"cols"])
            em.v(lambda e: e.reciprocal(cols[:, 10:11], cols[:, 9:10]),
                 reads=[P+"cols"], writes=[P+"cols"])
            em.v(lambda e: e.tensor_scalar_mul(probs[:], prow[:], cols[:, 10:11]),
                 reads=[P+"prow", P+"cols"], writes=[P+"probs"])
            em.v(lambda e: e.reduce_max(cols[:, 3:4], probs[:], axis=AX.X),
                 reads=[P+"probs"], writes=[P+"cols"])
            em.v(lambda e: e.tensor_single_scalar(scr[:], probs[:], cols[:, 3:4], ALU.is_equal),
                 reads=[P+"probs", P+"cols"], writes=[P+"scr"])
            em.v(lambda e: e.scalar_tensor_tensor(scr[:], scr[:], -BIGP, io8B[:], ALU.mult, ALU.add),
                 reads=[P+"scr", P+"io8B"], writes=[P+"scr"])
            em.v(lambda e: e.tensor_reduce(cols[:, 1:2], scr[:], axis=AX.X, op=ALU.min),
                 reads=[P+"scr"], writes=[P+"cols"])
            em.v(lambda e: e.tensor_single_scalar(eqf[0][:], io8[:], cols[:, 1:2], ALU.is_equal),
                 reads=[P+"io8", P+"cols"], writes=[P+"eqf0"])
            em.v(lambda e: e.scalar_tensor_tensor(prow[:], eqf[0][:], -2.0, probs[:], ALU.mult, ALU.add),
                 reads=[P+"eqf0", P+"probs"], writes=[P+"prow"])
            em.v(lambda e: e.reduce_max(cols[:, 4:5], prow[:], axis=AX.X),
                 reads=[P+"prow"], writes=[P+"cols"])
            em.v(lambda e: e.tensor_single_scalar(scr[:], prow[:], cols[:, 4:5], ALU.is_equal),
                 reads=[P+"prow", P+"cols"], writes=[P+"scr"])
            em.v(lambda e: e.scalar_tensor_tensor(scr[:], scr[:], -BIGP, io8B[:], ALU.mult, ALU.add),
                 reads=[P+"scr", P+"io8B"], writes=[P+"scr"])
            em.v(lambda e: e.tensor_reduce(cols[:, 2:3], scr[:], axis=AX.X, op=ALU.min),
                 reads=[P+"scr"], writes=[P+"cols"])
            em.v(lambda e: e.tensor_single_scalar(eqf[1][:], io8[:], cols[:, 2:3], ALU.is_equal),
                 reads=[P+"io8", P+"cols"], writes=[P+"eqf1"])
            em.v(lambda e: e.tensor_add(scr[:], eqf[0][:], eqf[1][:]),
                 reads=[P+"eqf0", P+"eqf1"], writes=[P+"scr"])
            em.v(lambda e: e.tensor_mul(csb[:], probs[:], scr[:]),
                 reads=[P+"probs", P+"scr"], writes=[P+"csb"])
            em.t(lambda e: e.transpose(ps_ct[0:8, :], csb[:], cst.ident_f[:]),
                 reads=[P+"csb", "identf"], writes=[P+"psct"])
            em.v(lambda e, tt=tt: e.tensor_copy(cfm[:, tt * 128:(tt + 1) * 128], ps_ct[0:8, :]),
                 psreads=[P+"psct"], writes=[P+"cfm"])
        em.barrier()

    ctxF = contextlib.ExitStack()
    sbf = mksb(ctxF)
    with ctxF:
        psf = mkps(ctxF)
        ps_cb = psf("pscb", [128, 512])
        ps_he = [psf(f"pshe{i}", [128, 512]) for i in range(2)]
        ps_o = [psf(f"pso{i}", [128, 512]) for i in range(2)]
        xfmH = sbf("xfmH", [128, 8, 2048], BF16)
        out_acc = sbf("oacc", [128, 16, D])
        wi_b = [sbf("wib0", [128, 8, HH], BF16)]
        wo_b = [sbf("wob0", [128, 8, D], BF16)]
        cb_sb = sbf("cbsb", [128, 512])
        g_fm = sbf("gfm", [128, 8, 512], BF16)
        xres = [sbf(f"xres{i}", [128, D], F32) for i in range(2)]
        s_t = [sbf(f"sf{i}", [128, D], F32) for i in range(2)]
        outt = [sbf(f"outf{i}", [128, D], F32) for i in range(2)]

        for half in range(2):
            th0 = half * 2048
            for ch in range(4):
                for dc in range(8):
                    em.dma("sync", xfmH[:, dc, ch * 512:(ch + 1) * 512],
                           xbf_dram[th0 + ch * 512:th0 + (ch + 1) * 512, dc * 128:(dc + 1) * 128],
                           reads=[P+"xbfd"], writes=[P+"xfmH"], transpose=True, tag="xtr")
            em.v(lambda e: e.memset(out_acc[:], 0.0), writes=[P+"oacc"])
            for e_i in range(E):
                ib = 0
                kwi, kwo = f"{P}wib0", f"{P}wob0"
                em.dma("gpsimd", wi_b[ib][:], dw["wi"][e_i].rearrange("(c p) m -> p c m", p=128),
                       writes=[kwi], tag="wi")
                em.dma("gpsimd", wo_b[ib][:], dw["wo"][e_i].rearrange("(c p) m -> p c m", p=128),
                       writes=[kwo], tag="wo")
                for tch in range(4):
                    tw = slice(tch * 512, (tch + 1) * 512)
                    em.t(lambda e, e_i=e_i, tch=tch, th0=th0: e.matmul(
                        ps_cb[:, :], sel8f[:, e_i * 128:(e_i + 1) * 128],
                        cfm[:, th0 + tch * 512:th0 + (tch + 1) * 512],
                        start=True, stop=True),
                        reads=[P+"sel8f", P+"cfm"], writes=[P+"pscb"], tag="cb")
                    em.v(lambda e: e.tensor_copy(cb_sb[:], ps_cb[:, :]),
                         psreads=[P+"pscb"], writes=[P+"cbsb"])
                    for ht in range(8):
                        sl = ht % 2
                        kp = f"{P}pshe{sl}"
                        for dc in range(8):
                            em.t(lambda e, ht=ht, dc=dc, sl=sl, ib=ib, tch=tch: e.matmul(
                                ps_he[sl][:, :], wi_b[ib][:, dc, ht * 128:(ht + 1) * 128],
                                xfmH[:, dc, tch * 512:(tch + 1) * 512],
                                start=(dc == 0), stop=(dc == 7)),
                                reads=[kwi, P+"xfmH"], writes=[kp], tag="hemm")
                        em.v(lambda e, ht=ht, sl=sl: e.tensor_mul(g_fm[:, ht, :], ps_he[sl][:, :], cb_sb[:]),
                             reads=[P+"cbsb"], psreads=[kp], writes=[P+"gfm"])
                    for tt2 in range(4):
                        for cw in range(2):
                            sl = (tt2 * 2 + cw) % 2
                            kp = f"{P}pso{sl}"
                            for hc in range(8):
                                em.t(lambda e, tt2=tt2, cw=cw, hc=hc, sl=sl, ib=ib: e.matmul(
                                    ps_o[sl][:, :], g_fm[:, hc, tt2 * 128:(tt2 + 1) * 128],
                                    wo_b[ib][:, hc, cw * 512:(cw + 1) * 512],
                                    start=(hc == 0), stop=(hc == 7)),
                                    reads=[P+"gfm", kwo], writes=[kp], tag="womm")
                            em.v(lambda e, tch=tch, tt2=tt2, cw=cw, sl=sl: e.tensor_add(
                                out_acc[:, tch * 4 + tt2, cw * 512:(cw + 1) * 512],
                                out_acc[:, tch * 4 + tt2, cw * 512:(cw + 1) * 512],
                                ps_o[sl][:, :]),
                                reads=[P+"oacc"], psreads=[kp], writes=[P+"oacc"])
            for tl in range(16):
                tt = half * 16 + tl
                iq = tl % 2
                rows = slice(tt * 128, (tt + 1) * 128)
                em.dma("sync", xres[iq][:], x_in[rows, :], writes=[f"{P}xres{iq}"])
                em.v(lambda e, iq=iq, tl=tl: e.tensor_add(s_t[iq][:], out_acc[:, tl, :], xres[iq][:]),
                     reads=[P+"oacc", f"{P}xres{iq}"], writes=[f"{P}sf{iq}"])
                emit_ln(em, P, s_t[iq], f"{P}sf{iq}", sq_t, red, outt[iq], f"{P}outf{iq}")
                em.dma("scalar", x_out[rows, :], outt[iq][:], reads=[f"{P}outf{iq}"], tag="xout")
        em.barrier()







def _build_fused():
    nc = bass.Bass("TRN2", target_bir_lowering=False, debug=False)
    ctx = contextlib.ExitStack()
    with ctx:
        x_in = nc.dram_tensor("x", [T, D], F32, kind="ExternalInput").ap()
        x_out = nc.dram_tensor("x4", [T, D], F32, kind="ExternalOutput").ap()
        x1s = nc.dram_tensor("x1s", [T, D], F32, kind="Internal").ap()
        x2s = nc.dram_tensor("x2s", [T, D], F32, kind="Internal").ap()
        x3s = nc.dram_tensor("x3s", [T, D], F32, kind="Internal").ap()
        dcst = declare_consts(nc)
        dcst2 = declare_consts2(nc)
        dcst3 = declare_consts3(nc)
        dcst4 = declare_consts4(nc)
        dw0a = l0a_declare_dram(nc, "l0a")
        dw0f = l0f_declare_dram(nc, "l0f")
        dw1a = l1a_declare_dram(nc, "l1a")
        dw1f = l1f_declare_dram(nc, "l1f")
        em = Em(nc, ctx)
        cst = Consts(em, nc, ctx, dcst)
        with contextlib.ExitStack() as c1:
            l0a_emit(em, nc, c1, cst, dw0a, x_in, x1s, pref="l0a")
            em.barrier()
        with contextlib.ExitStack() as c2:
            l0f_emit(em, nc, c2, cst, dcst2, dw0f, x1s, x2s, pref="l0f")
            em.barrier()
        with contextlib.ExitStack() as c3:
            l1a_emit(em, nc, c3, cst, dcst3, dw1a, x2s, x3s, pref="l1a")
            em.barrier()
        with contextlib.ExitStack() as c4:
            l1f_emit(em, nc, c4, cst, dcst4, dw1f, x3s, x_out, pref="l1f")
            em.barrier()
        em.barrier_reads_of_all()
        block = ctx.enter_context(nc.Block())
        em.replay(block)
    return nc


def kernel(**inputs):
    def grp(pref):
        return {k[len(pref):]: np.ascontiguousarray(np.asarray(v, np.float32))
                for k, v in inputs.items() if k.startswith(pref)}

    cst_all = host_consts()
    hp = {}
    hp.update(l0a_host_prep(grp("L0_attn_"), "l0a"))
    hp.update(l0f_host_prep(grp("L0_ffn_"), "l0f"))
    hp.update(l1a_host_prep(grp("L1_attn_"), "l1a"))
    hp.update(l1f_host_prep(grp("L1_ffn_"), "l1f"))
    hp.update(cst_all)
    hp.update(host_consts2())
    hp.update(host_consts3())
    hp.update(host_consts4())

    x = np.ascontiguousarray(np.asarray(inputs["L0_attn_hidden_states"], np.float32))
    nc = _build_fused()
    maps = [{**hp, "x": x[b]} for b in range(8)]
    res = run_bass_kernel_spmd(nc, maps, core_ids=list(range(8)))
    out = np.stack([np.asarray(res.results[b]["x4"]) for b in range(8)])
    return np.ascontiguousarray(out.astype(np.float32))


# version 03
# speedup vs baseline: 1.0336x; 1.0336x over previous
import os
import contextlib
import numpy as np
import ml_dtypes
import concourse.bass as bass
import concourse.mybir as mybir
from concourse.bass_utils import run_bass_kernel_spmd





F32 = mybir.dt.float32
BF16 = mybir.dt.bfloat16
U32 = mybir.dt.uint32
I32 = mybir.dt.int32
U8 = mybir.dt.uint8
AF = mybir.ActivationFunctionType
ALU = mybir.AluOpType
AX = mybir.AxisListType

STREAMS = ("sync", "tensor", "vector", "scalar", "gpsimd")


class Em:
    def __init__(self, nc, ctx):
        self.nc = nc
        self.ctx = ctx
        self.ops = {s: [] for s in STREAMS}
        self.csem = {s: ctx.enter_context(nc.semaphore("c_" + s)) for s in STREAMS}
        self.dsem = {s: ctx.enter_context(nc.semaphore("d_" + s))
                     for s in ("sync", "scalar", "gpsimd")}
        self.ccount = {s: 0 for s in STREAMS}
        self.dcount = {s: 0 for s in ("sync", "scalar", "gpsimd")}
        self.kn = {s: {} for s in STREAMS}
        self.snaps = {("c", s): [] for s in STREAMS}
        for s in ("sync", "scalar", "gpsimd"):
            self.snaps[("d", s)] = []
        self.last_writer = {}
        self.readers = {}
        self.nwaits = 0

    def _tok_needed(self, s, toks):
        need = {}
        for tok in toks:
            if tok is None:
                continue
            kind, s2, n = tok
            if kind == "c" and s2 == "tensor" and s == "tensor":
                continue
            key = (kind, s2)
            if need.get(key, 0) < n:
                need[key] = n
        return need

    def op(self, s, fn, reads=(), writes=(), psreads=(), dma=False, tag=""):
        if psreads:
            reads = tuple(reads) + tuple(psreads)
            writes = tuple(writes) + tuple(psreads)
        toks = []
        for k in reads:
            toks.append(self.last_writer.get(k))
        for k in writes:
            toks.append(self.last_writer.get(k))
            toks.extend(self.readers.get(k, ()))
        need = self._tok_needed(s, toks)
        waits = []
        kn = self.kn[s]
        for key, n in sorted(need.items()):
            if kn.get(key, 0) >= n:
                continue
            kind, s2 = key
            if kind == "c":
                waits.append((self.csem[s2], n))
            else:
                waits.append((self.dsem[s2], 16 * n))
            snap = self.snaps[key][n - 1]
            for kk, vv in snap.items():
                if kn.get(kk, 0) < vv:
                    kn[kk] = vv
            if kn.get(key, 0) < n:
                kn[key] = n
        self.nwaits += len(waits)
        if dma:
            assert s in self.dsem, f"stream {s} cannot issue DMAs"
            prev = self.dcount[s]
            if prev and kn.get(("d", s), 0) < prev:
                waits.append((self.dsem[s], 16 * prev))
                snap = self.snaps[("d", s)][prev - 1]
                for kk, vv in snap.items():
                    if kn.get(kk, 0) < vv:
                        kn[kk] = vv
                kn[("d", s)] = prev
                self.nwaits += 1
            self.dcount[s] += 1
            tok = ("d", s, self.dcount[s])
            inc = (self.dsem[s], 16)
        else:
            self.ccount[s] += 1
            tok = ("c", s, self.ccount[s])
            inc = (self.csem[s], 1)
        snap = dict(kn)
        snap[(tok[0], s)] = tok[2]
        self.snaps[(tok[0], s)].append(snap)
        self.ops[s].append((waits, fn, inc, tag))
        for k in reads:
            self.readers.setdefault(k, []).append(tok)
        for k in writes:
            self.last_writer[k] = tok
            self.readers[k] = []
        return tok

    def raw(self, s, fn, tag="raw"):
        self.ops[s].append(([], fn, None, tag))

    def v(self, fn, reads=(), writes=(), psreads=(), tag=""):
        return self.op("vector", fn, reads, writes, psreads, tag=tag)

    def sc(self, fn, reads=(), writes=(), psreads=(), tag=""):
        return self.op("scalar", fn, reads, writes, psreads, tag=tag)

    def t(self, fn, reads=(), writes=(), psreads=(), tag=""):
        return self.op("tensor", fn, reads, writes, psreads, tag=tag)

    def g(self, fn, reads=(), writes=(), psreads=(), tag=""):
        return self.op("gpsimd", fn, reads, writes, psreads, tag=tag)

    def dma(self, q, out, in_, reads=(), writes=(), tag="", **kw):
        return self.op(q, lambda e: e.dma_start(out=out, in_=in_, **kw),
                       reads, writes, dma=True, tag=tag)

    def barrier(self):
        counts_c = dict(self.ccount)
        counts_d = dict(self.dcount)
        for s in STREAMS:
            waits = []
            kn = self.kn[s]
            for s2 in STREAMS:
                if s2 == s:
                    continue
                n = counts_c[s2]
                if n and kn.get(("c", s2), 0) < n:
                    waits.append((self.csem[s2], n))
                    snap = self.snaps[("c", s2)][n - 1]
                    for kk, vv in snap.items():
                        if kn.get(kk, 0) < vv:
                            kn[kk] = vv
                    kn[("c", s2)] = max(kn.get(("c", s2), 0), n)
            for s2 in ("sync", "scalar", "gpsimd"):
                n = counts_d[s2]
                if n and kn.get(("d", s2), 0) < n:
                    waits.append((self.dsem[s2], 16 * n))
                    snap = self.snaps[("d", s2)][n - 1]
                    for kk, vv in snap.items():
                        if kn.get(kk, 0) < vv:
                            kn[kk] = vv
                    kn[("d", s2)] = max(kn.get(("d", s2), 0), n)
            n_own = counts_c[s]
            if n_own and kn.get(("c", s), 0) < n_own:
                waits.append((self.csem[s], n_own))
                kn[("c", s)] = n_own
            if s in self.dcount:
                n = counts_d[s]
                if n and kn.get(("d", s), 0) < n:
                    waits.append((self.dsem[s], 16 * n))
                    kn[("d", s)] = n
            self.ccount[s] += 1
            tok = ("c", s, self.ccount[s])
            snap = dict(kn)
            snap[("c", s)] = tok[2]
            self.snaps[("c", s)].append(snap)
            self.ops[s].append((waits, (lambda e: e.nop()), (self.csem[s], 1), "barrier"))
            self.nwaits += len(waits)

    def barrier_reads_of_all(self):
        def fin(eng):
            return eng.nop()
        waits = []
        kn = self.kn["sync"]
        for s in STREAMS:
            n = self.ccount[s]
            if n and kn.get(("c", s), 0) < n:
                waits.append((self.csem[s], n))
        for s in ("sync", "scalar", "gpsimd"):
            n = self.dcount[s]
            if n and kn.get(("d", s), 0) < n:
                waits.append((self.dsem[s], 16 * n))
        n_own = self.ccount["sync"]
        if n_own and kn.get(("c", "sync"), 0) < n_own:
            waits.append((self.csem["sync"], n_own))
            kn[("c", "sync")] = n_own
        self.ccount["sync"] += 1
        self.ops["sync"].append((waits, fin, (self.csem["sync"], 1), "final"))

    def replay(self, block):
        em = self

        def make(stream):
            oplist = em.ops[stream]

            def runner(eng):
                for idx, (waits, fn, inc, tag) in enumerate(oplist):
                    for sem, val in waits:
                        eng.wait_ge(sem, val)
                    try:
                        inst = fn(eng)
                    except Exception:
                        print(f"EMIT FAIL stream={stream} idx={idx} tag={tag} "
                              f"waits={[(s.name, v) for s, v in waits]}")
                        raise
                    if inc is not None:
                        inst.then_inc(inc[0], inc[1])

            return runner

        block.sync(make("sync"))
        block.tensor(make("tensor"))
        block.vector(make("vector"))
        block.scalar(make("scalar"))
        block.gpsimd(make("gpsimd"))


def ceil_div(a, b):
    return (a + b - 1) // b


T = 4096
D = 1024
DI = 2048
NS = 16
RK = 64
TC = 512
NCH = T // TC
NDT = DI // 128
NFT = 2 * DI // 128
NTT = T // 128
EPS = 1e-5


def l0a_host_prep(w, pref=""):
    cw = w["conv_w"].reshape(NDT, 128, 4).transpose(1, 0, 2).copy()
    cb = w["conv_b"].reshape(NDT, 128).T.copy()
    dtb = w["dt_proj_b"].reshape(NDT, 128).T.copy()
    alog = w["A_log"].reshape(NDT, 128, NS).transpose(1, 0, 2).copy()
    dsk = w["D_skip"].reshape(NDT, 128).T.copy()
    d = {
        "ipw": np.ascontiguousarray(w["in_proj_w"]),
        "convw": cw, "convb": cb,
        "xpw": np.ascontiguousarray(w["x_proj_w"]),
        "dtw": np.ascontiguousarray(w["dt_proj_w"]),
        "dtb": dtb, "alog": alog, "dskip": dsk,
        "opw": np.ascontiguousarray(w["out_proj_w"]),
    }
    return {pref + k: v for k, v in d.items()}


def host_consts():
    ident = np.eye(128, dtype=np.float32)
    sel = np.zeros((32, 32, 128), np.float32)
    for n in range(32):
        sel[n, n, :] = 1.0
    return {
        "identf": ident,
        "identb": ident.astype(ml_dtypes.bfloat16),
        "sel32": sel.reshape(32, 32 * 128).astype(ml_dtypes.bfloat16),
    }


def declare_consts(nc):
    d = {}
    d["identf"] = nc.dram_tensor("identf", [128, 128], F32, kind="ExternalInput").ap()
    d["identb"] = nc.dram_tensor("identb", [128, 128], BF16, kind="ExternalInput").ap()
    d["sel32"] = nc.dram_tensor("sel32", [32, 32 * 128], BF16, kind="ExternalInput").ap()
    return d


def l0a_declare_dram(nc, pref=""):
    d = {}
    def inp(name, shape, dt=F32):
        d[name] = nc.dram_tensor(pref + name, shape, dt, kind="ExternalInput").ap()
    inp("ipw", [D, 2 * DI])
    inp("convw", [128, NDT, 4])
    inp("convb", [128, NDT])
    inp("xpw", [DI, 96])
    inp("dtw", [RK, DI])
    inp("dtb", [128, NDT])
    inp("alog", [128, NDT, NS])
    inp("dskip", [128, NDT])
    inp("opw", [DI, D])
    return d


class Consts:
    def __init__(self, em, nc, ctx, dc):
        def sb(name, shape, dtype=F32):
            return ctx.enter_context(nc.sbuf_tensor("sb_" + name, shape, dtype))
        self.ident_b = sb("identb", [128, 128], BF16)
        self.ident_f = sb("identf", [128, 128], F32)
        self.sel32 = sb("sel32", [32, 32 * 128], BF16)
        em.dma("sync", self.ident_b[:], dc["identb"][:, :], writes=["identb"])
        em.dma("sync", self.ident_f[:], dc["identf"][:, :], writes=["identf"])
        em.dma("sync", self.sel32[:], dc["sel32"][:, :], writes=["sel32"])


def emit_ln(em, P, src, ksrc, sq_t, red, dst, kdst):
    em.v(lambda e: e.reduce_sum(red[:, 0:1], src[:], axis=AX.X),
         reads=[ksrc], writes=[P + "red"])
    em.v(lambda e: e.tensor_scalar_mul(red[:, 1:2], red[:, 0:1], -1.0 / D),
         reads=[P + "red"], writes=[P + "red"])
    em.v(lambda e: e.tensor_scalar_add(src[:], src[:], red[:, 1:2]),
         reads=[ksrc, P + "red"], writes=[ksrc])
    em.sc(lambda e: e.activation(out=sq_t[:], in_=src[:], func=AF.Square,
                                 accum_out=red[:, 2:3]),
          reads=[ksrc], writes=[P + "sq", P + "red"])
    em.v(lambda e: e.tensor_scalar(red[:, 3:4], red[:, 2:3], 1.0 / D, EPS, ALU.mult, ALU.add),
         reads=[P + "red"], writes=[P + "red"])
    em.sc(lambda e: e.activation(out=red[:, 3:4], in_=red[:, 3:4], func=AF.Ln),
          reads=[P + "red"], writes=[P + "red"])
    em.sc(lambda e: e.activation(out=red[:, 3:4], in_=red[:, 3:4], func=AF.Exp, scale=-0.5),
          reads=[P + "red"], writes=[P + "red"])
    em.v(lambda e: e.tensor_scalar_mul(dst[:], src[:], red[:, 3:4]),
         reads=[ksrc, P + "red"], writes=[kdst])


def l0a_emit(em, nc, ctx, cst, dw, x_in, x_out, pref="l0a", stage=99, dbg=False):
    P = pref
    def mksb(c):
        def sb(name, shape, dtype=F32):
            return c.enter_context(nc.sbuf_tensor(P + "sb_" + name, shape, dtype))
        return sb
    sbp = mksb(ctx)
    def psum(name, shape, dtype=F32):
        return ctx.enter_context(nc.psum_tensor(P + "pp_" + name, shape, dtype))

    kind = "ExternalOutput" if dbg else "Internal"
    u_dram = nc.dram_tensor(P + "u_scr", [NDT, 128, T], F32, kind=kind).ap()
    sres_dram = nc.dram_tensor(P + "sres_scr", [NDT, 128, T], BF16, kind=kind).ap()

    xdbl = sbp("xdbl", [128, NTT, 96], F32)
    dtfm = sbp("dtfm", [RK, T], BF16)
    bmcm = sbp("bmcm", [32, T], BF16)
    xpw_b = sbp("xpw", [128, NDT, 96], BF16)
    dtw_b = sbp("dtw", [RK, DI], BF16)
    convw_s = sbp("convw", [128, NDT, 4], F32)
    convb_s = sbp("convb", [128, NDT], F32)
    dtb_s = sbp("dtb", [128, NDT], F32)
    acol_s = sbp("acol", [128, NDT * NS], F32)
    dskip_s = sbp("dskip", [128, NDT], F32)
    state = sbp("state", [128, NDT * NS], F32)
    xres = [sbp(f"xres{i}", [128, D], F32) for i in range(2)]
    s_t = [sbp(f"st{i}", [128, D], F32) for i in range(2)]
    sq_t = sbp("sq", [128, D], F32)
    red = sbp("red", [128, 4], F32)
    outt = [sbp(f"outt{i}", [128, D], F32) for i in range(2)]


    em.dma("sync", convw_s[:], dw["convw"][:, :, :], writes=[P+"convw"])
    em.dma("sync", convb_s[:], dw["convb"][:, :], writes=[P+"convb"])
    em.dma("sync", dtb_s[:], dw["dtb"][:, :], writes=[P+"dtb"])
    em.dma("sync", acol_s[:], dw["alog"].rearrange("p a b -> p (a b)"), writes=[P+"acol"])
    em.dma("sync", dskip_s[:], dw["dskip"][:, :], writes=[P+"dskip"])
    em.dma("gpsimd", xpw_b[:], dw["xpw"].rearrange("(c p) m -> p c m", p=128), writes=[P+"xpw"])
    em.dma("gpsimd", dtw_b[:], dw["dtw"][:, :], writes=[P+"dtw"])
    em.sc(lambda e: e.activation(out=acol_s[:], in_=acol_s[:], func=AF.Exp),
          reads=[P+"acol"], writes=[P+"acol"])
    em.v(lambda e: e.tensor_scalar_mul(acol_s[:], acol_s[:], -1.0),
         reads=[P+"acol"], writes=[P+"acol"])
    em.v(lambda e: e.memset(state[:], 0.0), writes=[P+"state"])
    em.v(lambda e: e.memset(xdbl[:], 0.0), writes=[P+"xdbl"])

    ctxA = contextlib.ExitStack()
    sba = mksb(ctxA)
    def mkps(c):
        def ps(name, shape, dtype=F32):
            return c.enter_context(nc.psum_tensor(P + "pp_" + name, shape, dtype))
        return ps
    with ctxA:
        psa_ = mkps(ctxA)
        ps_a = psa_("psa", [128, 512])
        ps_tr = [psa_(f"pstr{i}", [128, 128], BF16) for i in range(2)]
        ps_96 = [psa_(f"ps96{i}", [128, 96]) for i in range(2)]
        ps_tf = [psa_(f"pstf{i}", [128, 128]) for i in range(2)]
        xfm = sba("xfm", [128, 8, T], BF16)
        xld = [sba(f"xld{i}", [128, D], F32) for i in range(2)]
        xbf = [sba(f"xbf{i}", [128, D], BF16) for i in range(2)]
        ipw_t = [sba(f"ipwt{i}", [128, 8, 128], BF16) for i in range(2)]
        xz_t = sba("xz", [128, T], F32)
        cacc = sba("cacc", [128, T], F32)
        ubf_t = sba("ubf", [128, T], BF16)
        sres_t = sba("srest", [128, T], BF16)

        import os as _os
        _cap = int(_os.environ.get("NTTCAP", str(NTT)))
        for tt in range(_cap):
            i = tt % 2
            kxl, kxb = f"{P}xld{i}", f"{P}xbf{i}"
            em.dma("sync", xld[i][:], x_in[tt * 128:(tt + 1) * 128, :], writes=[kxl], tag="p0ld")
            em.v(lambda e, i=i: e.tensor_copy(xbf[i][:], xld[i][:]), reads=[kxl], writes=[kxb])
            for dc in range(8):
                slot = dc % 2
                kps = f"{P}pstr{slot}"
                em.t(lambda e, i=i, dc=dc, slot=slot: e.transpose(
                    ps_tr[slot][:, :], xbf[i][:, dc * 128:(dc + 1) * 128], cst.ident_b[:]),
                    reads=[kxb, "identb"], writes=[kps])
                if dc % 2:
                    em.sc(lambda e, tt=tt, dc=dc, slot=slot: e.copy(
                        xfm[:, dc, tt * 128:(tt + 1) * 128], ps_tr[slot][:, :]),
                        reads=[], psreads=[kps], writes=[P+"xfm"])
                else:
                    em.v(lambda e, tt=tt, dc=dc, slot=slot: e.tensor_copy(
                        xfm[:, dc, tt * 128:(tt + 1) * 128], ps_tr[slot][:, :]),
                        reads=[], psreads=[kps], writes=[P+"xfm"])

        for ft in (range(NFT) if stage >= 2 else []):
            i = ft % 2
            kw = f"{P}ipwt{i}"
            em.dma("gpsimd", ipw_t[i][:], dw["ipw"][:, ft * 128:(ft + 1) * 128].rearrange("(c p) m -> p c m", p=128),
                   writes=[kw], tag="p1w")
            for w in range(8):
                for dc in range(8):
                    em.t(lambda e, i=i, w=w, dc=dc: e.matmul(
                        ps_a[:, :], ipw_t[i][:, dc, :], xfm[:, dc, w * 512:(w + 1) * 512],
                        start=(dc == 0), stop=(dc == 7)),
                        reads=[kw, P+"xfm"], writes=[P+"psa"], tag="p1mm")
                em.sc(lambda e, w=w: e.copy(xz_t[:, w * 512:(w + 1) * 512], ps_a[:, :]),
                      psreads=[P+"psa"], writes=[P+"xz"])
            if ft < NDT:
                dt_i = ft
                em.v(lambda e, dt_i=dt_i: e.tensor_scalar_mul(cacc[:], xz_t[:], convw_s[:, dt_i, 3:4]),
                     reads=[P+"xz", P+"convw"], writes=[P+"cacc"])
                for s in range(1, 4):
                    em.v(lambda e, s=s, dt_i=dt_i: e.scalar_tensor_tensor(
                        cacc[:, s:T], xz_t[:, 0:T - s], convw_s[:, dt_i, 3 - s:4 - s], cacc[:, s:T],
                        ALU.mult, ALU.add),
                        reads=[P+"xz", P+"convw", P+"cacc"], writes=[P+"cacc"])
                em.sc(lambda e, dt_i=dt_i: e.activation(out=cacc[:], in_=cacc[:], func=AF.Silu,
                                                        bias=convb_s[:, dt_i:dt_i + 1]),
                      reads=[P+"cacc", P+"convb"], writes=[P+"cacc"])
                em.dma("scalar", u_dram[dt_i, :, :], cacc[:], reads=[P+"cacc"], tag="p1us")
                em.v(lambda e: e.tensor_copy(ubf_t[:], cacc[:]), reads=[P+"cacc"], writes=[P+"ubf"])
                for tt in range(NTT):
                    j = tt % 2
                    kp = f"{P}ps96{j}"
                    em.t(lambda e, tt=tt, dt_i=dt_i, j=j: e.matmul(
                        ps_96[j][:, :], ubf_t[:, tt * 128:(tt + 1) * 128], xpw_b[:, dt_i, :],
                        start=True, stop=True),
                        reads=[P+"ubf", P+"xpw"], writes=[kp])
                    em.v(lambda e, tt=tt, j=j: e.tensor_add(xdbl[:, tt, :], xdbl[:, tt, :], ps_96[j][:, :]),
                         reads=[P+"xdbl"], psreads=[kp], writes=[P+"xdbl"])
            else:
                dt_i = ft - NDT
                em.sc(lambda e: e.activation(out=sres_t[:], in_=xz_t[:], func=AF.Silu),
                      reads=[P+"xz"], writes=[P+"srest"])
                em.dma("scalar", sres_dram[dt_i, :, :], sres_t[:], reads=[P+"srest"], tag="p1ss")

        for tt in (range(NTT) if stage >= 3 else []):
            em.t(lambda e, tt=tt: e.transpose(ps_tf[0][0:64, :], xdbl[:, tt, 0:64], cst.ident_f[:]),
                 reads=[P+"xdbl", "identf"], writes=[P+"pstf0"])
            em.v(lambda e, tt=tt: e.tensor_copy(dtfm[:, tt * 128:(tt + 1) * 128], ps_tf[0][0:64, :]),
                 psreads=[P+"pstf0"], writes=[P+"dtfm"])
            em.t(lambda e, tt=tt: e.transpose(ps_tf[1][0:32, :], xdbl[:, tt, 64:96], cst.ident_f[:]),
                 reads=[P+"xdbl", "identf"], writes=[P+"pstf1"])
            em.v(lambda e, tt=tt: e.tensor_copy(bmcm[:, tt * 128:(tt + 1) * 128], ps_tf[1][0:32, :]),
                 psreads=[P+"pstf1"], writes=[P+"bmcm"])

        em.barrier()

    if stage < 4:
        em.v(lambda e: e.memset(outt[0][:], 0.0), writes=[P+"outt0"])
        for tt in range(NTT):
            em.dma("scalar", x_out[tt * 128:(tt + 1) * 128, :], outt[0][:],
                   reads=[P+"outt0"], tag="dummy")
        return
    ctxB = contextlib.ExitStack()
    sbb = mksb(ctxB)
    with ctxB:
        psb_ = mkps(ctxB)
        ps_a = psb_("psdt", [128, 512])
        ps_b = psb_("psbc", [128, 512])
        ps_y = psb_("psy", [128, 512])
        ps_o = [psb_(f"pso{i}", [128, 512]) for i in range(2)]
        opw_b = sbb("opw", [128, NDT, D], BF16)
        bcB = sbb("bcB", [128, NS, TC], BF16)
        bcC = sbb("bcC", [128, NS, TC], BF16)
        sp1 = sbb("sp1", [128, TC], F32)
        spt = sbb("spt", [128, TC], F32)
        delta_t = sbb("delta", [128, TC], F32)
        uch = [sbb(f"uch{i}", [128, TC], F32) for i in range(2)]
        du_t = sbb("du", [128, TC], F32)
        uDb_t = sbb("uDb", [128, TC], BF16)
        dA = [sbb(f"dA{i}", [128, TC], F32) for i in range(2)]
        dBu = [sbb(f"dBu{i}", [128, TC], F32) for i in range(2)]
        h_t = [sbb(f"h{i}", [128, TC], F32) for i in range(2)]
        tmpb = [sbb(f"tmpb{i}", [128, TC], BF16) for i in range(2)]
        sresc = [sbb(f"sresc{i}", [128, TC], BF16) for i in range(2)]
        yfin = sbb("yfin", [128, NDT, TC], BF16)

        em.dma("gpsimd", opw_b[:], dw["opw"].rearrange("(c p) m -> p c m", p=128), writes=[P+"opw"])

        SS = os.environ.get("SCANSUB", "d")
        _nchcap = int(os.environ.get("NCHCAP", str(NCH)))
        _ndtcap = int(os.environ.get("NDTCAP", str(NDT)))
        for ch in range(_nchcap):
            t0 = ch * TC
            if SS == "a":
                em.v(lambda e: e.memset(yfin[:], 0.0), writes=[P+"yfin"])
            for n in (range(NS) if SS != "z" else []):
                for half in range(2):
                    row = n if half == 0 else NS + n
                    dst = bcB if half == 0 else bcC
                    kd = (P + "bcB") if half == 0 else (P + "bcC")
                    em.t(lambda e, row=row, t0=t0: e.matmul(
                        ps_b[:, :], cst.sel32[:, row * 128:(row + 1) * 128],
                        bmcm[:, t0:t0 + TC], start=True, stop=True),
                        reads=["sel32", P+"bmcm"], writes=[P+"psb"], tag="bc")
                    em.v(lambda e, dst=dst, n=n: e.tensor_copy(dst[:, n, :], ps_b[:, :]),
                         psreads=[P+"psb"], writes=[kd])
            DOPS = os.environ.get("DTOPS", "m12345uyrf")
            for dt_i in (range(_ndtcap) if SS not in ("a", "z") else []):
              if "m" in DOPS:
                em.t(lambda e, dt_i=dt_i, t0=t0: e.matmul(
                    ps_a[:, :], dtw_b[:, dt_i * 128:(dt_i + 1) * 128], dtfm[:, t0:t0 + TC],
                    start=True, stop=True),
                    reads=[P+"dtw", P+"dtfm"], writes=[P+"psa"], tag="dt")
              if "1" in DOPS:
                em.v(lambda e, dt_i=dt_i: e.tensor_scalar(
                    sp1[:], ps_a[:, :], dtb_s[:, dt_i:dt_i + 1], 0.0, ALU.add, ALU.max),
                    reads=[P+"dtb"], psreads=[P+"psa"], writes=[P+"sp1"])
              if "2" in DOPS:
                em.sc(lambda e, dt_i=dt_i: e.activation(out=spt[:], in_=ps_a[:, :], func=AF.Abs,
                                                        bias=dtb_s[:, dt_i:dt_i + 1]),
                      reads=[P+"dtb"], psreads=[P+"psa"], writes=[P+"spt"])
              if "A" in DOPS:
                em.sc(lambda e: e.activation(out=spt[:], in_=ps_a[:, :], func=AF.Abs),
                      psreads=[P+"psa"], writes=[P+"spt"])
              if "B" in DOPS:
                em.sc(lambda e, dt_i=dt_i: e.activation(out=spt[:], in_=ps_a[:, :], func=AF.Identity,
                                                        bias=dtb_s[:, dt_i:dt_i + 1]),
                      reads=[P+"dtb"], psreads=[P+"psa"], writes=[P+"spt"])
              if "C" in DOPS:
                em.sc(lambda e, dt_i=dt_i: e.activation(out=spt[:], in_=sp1[:], func=AF.Abs,
                                                        bias=dtb_s[:, dt_i:dt_i + 1]),
                      reads=[P+"sp1", P+"dtb"], writes=[P+"spt"])
              if "3" in DOPS:
                em.sc(lambda e: e.activation(out=spt[:], in_=spt[:], func=AF.Exp, scale=-1.0),
                      reads=[P+"spt"], writes=[P+"spt"])
              if "4" in DOPS:
                em.sc(lambda e: e.activation(out=spt[:], in_=spt[:], func=AF.Ln, bias=1.0),
                      reads=[P+"spt"], writes=[P+"spt"])
              if "5" in DOPS:
                em.v(lambda e: e.tensor_add(delta_t[:], sp1[:], spt[:]),
                     reads=[P+"sp1", P+"spt"], writes=[P+"delta"])
              iu = dt_i % 2
              if "u" in DOPS:
                ku = f"{P}uch{iu}"
                em.dma("sync", uch[iu][:], u_dram[dt_i, :, t0:t0 + TC], writes=[ku], tag="p3u")
                em.v(lambda e, iu=iu: e.tensor_mul(du_t[:], delta_t[:], uch[iu][:]),
                     reads=[P+"delta", ku], writes=[P+"du"])
                em.v(lambda e, iu=iu, dt_i=dt_i: e.tensor_scalar_mul(uDb_t[:], uch[iu][:], dskip_s[:, dt_i:dt_i + 1]),
                     reads=[ku, P+"dskip"], writes=[P+"uDb"])
              if "y" in DOPS:
                em.t(lambda e, _st=(SS == "b"): e.matmul(ps_y[:, :], cst.ident_b[:], uDb_t[:], start=True, stop=_st),
                     reads=["identb", P+"uDb"], writes=[P+"psy"], tag="yacc")
              if True:
                for n in (range(NS) if SS != "b" else []):
                    j = n % 2
                    col = dt_i * NS + n
                    em.sc(lambda e, j=j, col=col: e.activation(
                        out=dA[j][:], in_=delta_t[:], func=AF.Exp, scale=acol_s[:, col:col + 1]),
                        reads=[P+"delta", P+"acol"], writes=[f"{P}dA{j}"])
                    em.v(lambda e, j=j, n=n: e.tensor_mul(dBu[j][:], du_t[:], bcB[:, n, :]),
                         reads=[P+"du", P+"bcB"], writes=[f"{P}dBu{j}"])
                    if SS == "c":
                        em.v(lambda e, j=j: e.tensor_copy(h_t[j][:], dBu[j][:]),
                             reads=[f"{P}dBu{j}"], writes=[f"{P}h{j}"])
                    else:
                        em.v(lambda e, j=j, col=col: e.tensor_tensor_scan(
                            h_t[j][:], dA[j][:], dBu[j][:], state[:, col:col + 1], ALU.mult, ALU.add),
                            reads=[f"{P}dA{j}", f"{P}dBu{j}", P+"state"], writes=[f"{P}h{j}"])
                    em.sc(lambda e, j=j, col=col: e.copy(state[:, col:col + 1], h_t[j][:, TC - 1:TC]),
                          reads=[f"{P}h{j}"], writes=[P+"state"])
                    em.g(lambda e, j=j, n=n: e.tensor_mul(tmpb[j][:], h_t[j][:], bcC[:, n, :]),
                         reads=[f"{P}h{j}", P+"bcC"], writes=[f"{P}tmpb{j}"])
                    em.t(lambda e, j=j, n=n: e.matmul(
                        ps_y[:, :], cst.ident_b[:], tmpb[j][:], start=False, stop=(n == NS - 1)),
                        reads=["identb", f"{P}tmpb{j}"], writes=[P+"psy"], tag="yacc")
              isr = dt_i % 2
              ksr = f"{P}sresc{isr}"
              if "r" in DOPS:
                em.dma("sync", sresc[isr][:], sres_dram[dt_i, :, t0:t0 + TC], writes=[ksr], tag="p3s")
              if "f" in DOPS:
                em.v(lambda e, isr=isr, dt_i=dt_i: e.tensor_mul(yfin[:, dt_i, :], ps_y[:, :], sresc[isr][:]),
                     reads=[ksr], psreads=[P+"psy"], writes=[P+"yfin"])
            for q in range(TC // 128):
                iq = q % 2
                rows = slice(t0 + q * 128, t0 + (q + 1) * 128)
                em.dma("sync", xres[iq][:], x_in[rows, :], writes=[f"{P}xres{iq}"], tag="p3x")
                for half in range(2):
                    for dt_i in range(NDT):
                        em.t(lambda e, q=q, dt_i=dt_i, half=half: e.matmul(
                            ps_o[half][:, :],
                            yfin[:, dt_i, q * 128:(q + 1) * 128],
                            opw_b[:, dt_i, half * 512:(half + 1) * 512],
                            start=(dt_i == 0), stop=(dt_i == NDT - 1)),
                            reads=[P+"yfin", P+"opw"], writes=[f"{P}pso{half}"], tag="op")
                for half in range(2):
                    em.v(lambda e, iq=iq, half=half: e.tensor_add(
                        s_t[iq][:, half * 512:(half + 1) * 512], ps_o[half][:, :],
                        xres[iq][:, half * 512:(half + 1) * 512]),
                        reads=[f"{P}xres{iq}"], psreads=[f"{P}pso{half}"], writes=[f"{P}st{iq}"])
                emit_ln(em, P, s_t[iq], f"{P}st{iq}", sq_t, red, outt[iq], f"{P}outt{iq}")
                em.dma("scalar", x_out[rows, :], outt[iq][:], reads=[f"{P}outt{iq}"], tag="p3o")
        em.barrier()




T = 4096
D = 1024
E = 8
K = 2
H = 4096
CAP = 640
NTT = T // 128
NST = E * CAP // 128
BIGP = 1.0e6
EPS = 1e-5


def l0f_host_prep(w, pref=""):
    b1 = w["b1"].reshape(E, 32, 128).transpose(2, 0, 1).copy()
    d = {
        "wr": np.ascontiguousarray(w["Wr"]),
        "w1": np.ascontiguousarray(w["W1"]),
        "b1": b1,
        "w2": np.ascontiguousarray(w["W2"]),
        "b2": np.ascontiguousarray(w["b2"]),
    }
    return {pref + k: v for k, v in d.items()}


def host_consts2():
    lstrict = np.tril(np.ones((128, 128), np.float32), -1).T
    return {"lstrict": lstrict.astype(ml_dtypes.bfloat16)}


def declare_consts2(nc):
    return {"lstrict": nc.dram_tensor("lstrict", [128, 128], BF16, kind="ExternalInput").ap()}


def l0f_declare_dram(nc, pref=""):
    d = {}
    def inp(name, shape, dt=F32):
        d[name] = nc.dram_tensor(pref + name, shape, dt, kind="ExternalInput").ap()
    inp("wr", [D, E])
    inp("w1", [E, D, H])
    inp("b1", [128, E, 32])
    inp("w2", [E, H, D])
    inp("b2", [E, D])
    return d


def l0f_emit(em, nc, ctx, cst, cst2, dw, x_in, x_out, pref="l0f"):
    P = pref
    def mksb(c):
        def sb(name, shape, dtype=F32):
            return c.enter_context(nc.sbuf_tensor(P + "sb_" + name, shape, dtype))
        return sb
    def mkps(c):
        def ps(name, shape, dtype=F32):
            return c.enter_context(nc.psum_tensor(P + "pp_" + name, shape, dtype))
        return ps
    sbp = mksb(ctx)

    slot_dram = nc.dram_tensor(P + "slot_scr", [E * CAP, 1], U32, kind="Internal").ap()
    y_dram = nc.dram_tensor(P + "y_scr", [E * CAP, D], BF16, kind="Internal").ap()

    lstr = sbp("lstr", [128, 128], BF16)
    wr_s = sbp("wr", [128, 8, E], F32)
    b1_s = sbp("b1", [128, E, 32], F32)
    addr_sb = sbp("addr", [128, NTT, K], U32)
    pbuf = sbp("pbuf", [128, NTT, K], F32)
    io8 = sbp("io8", [128, E], F32)
    io8B = sbp("io8B", [128, E], F32)
    tok_sb = sbp("tok", [128, NTT], U32)
    onesf1 = sbp("onesf1", [1, 128], F32)
    onesb1 = sbp("onesb1", [128, 1], BF16)
    onesbr = sbp("onesbr", [1, 128], BF16)
    red = sbp("red", [128, 4], F32)
    sq_t = sbp("sq", [128, D], F32)

    em.dma("sync", lstr[:], cst2["lstrict"][:, :], writes=[P+"lstr"])
    em.dma("sync", wr_s[:], dw["wr"].rearrange("(c p) m -> p c m", p=128), writes=[P+"wr"])
    em.dma("sync", b1_s[:], dw["b1"][:, :, :], writes=[P+"b1"])
    em.g(lambda e: e.iota(io8[:], pattern=[[1, E]], base=0, channel_multiplier=0,
                          allow_small_or_imprecise_dtypes=True), writes=[P+"io8"])
    em.g(lambda e: e.iota(io8B[:], pattern=[[1, E]], base=int(BIGP), channel_multiplier=0,
                          allow_small_or_imprecise_dtypes=True), writes=[P+"io8B"])
    em.g(lambda e: e.iota(tok_sb[:], pattern=[[128, NTT]], base=0, channel_multiplier=1),
         writes=[P+"tok"])
    bc_slot = nc.gpsimd.alloc_register(P + "bcslot")
    bc_tok = nc.gpsimd.alloc_register(P + "bctok")
    em.raw("gpsimd", lambda e: e.reg_mov(bc_slot, E * CAP - 1))
    em.raw("gpsimd", lambda e: e.reg_mov(bc_tok, T - 1))
    em.v(lambda e: e.memset(onesf1[:], 1.0), writes=[P+"onesf1"])
    em.v(lambda e: e.memset(onesb1[:], 1.0), writes=[P+"onesb1"])
    em.v(lambda e: e.memset(onesbr[:], 1.0), writes=[P+"onesbr"])

    ctxR = contextlib.ExitStack()
    sbr = mksb(ctxR)
    with ctxR:
        psr = mkps(ctxR)
        ps_log = psr("pslog", [128, E])
        ps_cum = psr("pscum", [128, E])
        ps_row = psr("psrow", [1, E])
        ps_car = psr("pscar", [128, E])
        xld = [sbr(f"xld{i}", [128, D], F32) for i in range(2)]
        xfm = sbr("xfm", [128, 8, 128], F32)
        pstrf = [psr(f"pstrf{i}", [128, 128]) for i in range(2)]
        probs = sbr("probs", [128, E])
        prow = sbr("prow", [128, E])
        scr = sbr("scr", [128, E])
        eqf = [sbr(f"eqf{i}", [128, E]) for i in range(2)]
        rsum2 = sbr("rsum2", [128, E], BF16)
        carry = sbr("carry", [128, E])
        posx = sbr("posx", [128, E])
        row_sb = sbr("rowsb", [1, E])
        cols = sbr("cols", [128, 12])
        addrf = sbr("addrf", [128, 2])
        em.v(lambda e: e.memset(carry[:], 0.0), writes=[P+"carry"])

        for tt in range(NTT):
            i = tt % 2
            kxl = f"{P}xld{i}"
            em.dma("sync", xld[i][:], x_in[tt * 128:(tt + 1) * 128, :], writes=[kxl])
            for dc in range(8):
                sl = dc % 2
                em.t(lambda e, i=i, dc=dc, sl=sl: e.transpose(
                    pstrf[sl][:, :], xld[i][:, dc * 128:(dc + 1) * 128], cst.ident_f[:]),
                    reads=[kxl, "identf"], writes=[f"{P}pstrf{sl}"])
                em.v(lambda e, dc=dc, sl=sl: e.tensor_copy(xfm[:, dc, :], pstrf[sl][:, :]),
                     psreads=[f"{P}pstrf{sl}"], writes=[P+"xfm"])
            for dc in range(8):
                em.t(lambda e, dc=dc: e.matmul(
                    ps_log[:, :], xfm[:, dc, :], wr_s[:, dc, :],
                    start=(dc == 0), stop=(dc == 7)),
                    reads=[P+"xfm", P+"wr"], writes=[P+"pslog"], tag="router")
            em.v(lambda e: e.reduce_max(cols[:, 0:1], ps_log[:, :], axis=AX.X),
                 psreads=[P+"pslog"], writes=[P+"cols"])
            em.v(lambda e: e.tensor_scalar_mul(cols[:, 0:1], cols[:, 0:1], -1.0),
                 reads=[P+"cols"], writes=[P+"cols"])
            em.sc(lambda e: e.activation(out=prow[:], in_=ps_log[:, :], func=AF.Exp,
                                         bias=cols[:, 0:1], accum_out=cols[:, 9:10]),
                  reads=[P+"cols"], psreads=[P+"pslog"], writes=[P+"prow", P+"cols"])
            em.v(lambda e: e.reciprocal(cols[:, 10:11], cols[:, 9:10]),
                 reads=[P+"cols"], writes=[P+"cols"])
            em.v(lambda e: e.tensor_scalar_mul(probs[:], prow[:], cols[:, 10:11]),
                 reads=[P+"prow", P+"cols"], writes=[P+"probs"])
            em.v(lambda e: e.reduce_max(cols[:, 3:4], probs[:], axis=AX.X),
                 reads=[P+"probs"], writes=[P+"cols"])
            em.v(lambda e: e.tensor_single_scalar(scr[:], probs[:], cols[:, 3:4], ALU.is_equal),
                 reads=[P+"probs", P+"cols"], writes=[P+"scr"])
            em.v(lambda e: e.scalar_tensor_tensor(scr[:], scr[:], -BIGP, io8B[:], ALU.mult, ALU.add),
                 reads=[P+"scr", P+"io8B"], writes=[P+"scr"])
            em.v(lambda e: e.tensor_reduce(cols[:, 1:2], scr[:], axis=AX.X, op=ALU.min),
                 reads=[P+"scr"], writes=[P+"cols"])
            em.v(lambda e: e.tensor_single_scalar(eqf[0][:], io8[:], cols[:, 1:2], ALU.is_equal),
                 reads=[P+"io8", P+"cols"], writes=[P+"eqf0"])
            em.v(lambda e: e.scalar_tensor_tensor(prow[:], eqf[0][:], -2.0, probs[:], ALU.mult, ALU.add),
                 reads=[P+"eqf0", P+"probs"], writes=[P+"prow"])
            em.v(lambda e: e.reduce_max(cols[:, 4:5], prow[:], axis=AX.X),
                 reads=[P+"prow"], writes=[P+"cols"])
            em.v(lambda e: e.tensor_single_scalar(scr[:], prow[:], cols[:, 4:5], ALU.is_equal),
                 reads=[P+"prow", P+"cols"], writes=[P+"scr"])
            em.v(lambda e: e.scalar_tensor_tensor(scr[:], scr[:], -BIGP, io8B[:], ALU.mult, ALU.add),
                 reads=[P+"scr", P+"io8B"], writes=[P+"scr"])
            em.v(lambda e: e.tensor_reduce(cols[:, 2:3], scr[:], axis=AX.X, op=ALU.min),
                 reads=[P+"scr"], writes=[P+"cols"])
            em.v(lambda e: e.tensor_single_scalar(eqf[1][:], io8[:], cols[:, 2:3], ALU.is_equal),
                 reads=[P+"io8", P+"cols"], writes=[P+"eqf1"])
            em.v(lambda e, tt=tt: e.tensor_copy(pbuf[:, tt, :], cols[:, 3:5]),
                 reads=[P+"cols"], writes=[P+"pbuf"])
            em.v(lambda e: e.tensor_add(rsum2[:], eqf[0][:], eqf[1][:]),
                 reads=[P+"eqf0", P+"eqf1"], writes=[P+"rsum2"])
            em.t(lambda e: e.matmul(ps_cum[:, :], lstr[:], rsum2[:], start=True, stop=True),
                 reads=[P+"lstr", P+"rsum2"], writes=[P+"pscum"], tag="cum")
            em.v(lambda e: e.tensor_add(posx[:], ps_cum[:, :], carry[:]),
                 reads=[P+"carry"], psreads=[P+"pscum"], writes=[P+"posx"])
            em.t(lambda e: e.matmul(ps_row[:, :], onesb1[:], rsum2[:], start=True, stop=True),
                 reads=[P+"onesb1", P+"rsum2"], writes=[P+"psrow"], tag="rowsum")
            em.v(lambda e: e.tensor_copy(row_sb[:], ps_row[:, :]),
                 psreads=[P+"psrow"], writes=[P+"rowsb"])
            em.t(lambda e: e.matmul(ps_car[:, :], onesf1[:], row_sb[:], start=True, stop=True),
                 reads=[P+"onesf1", P+"rowsb"], writes=[P+"pscar"], tag="carbc")
            em.v(lambda e: e.tensor_add(carry[:], carry[:], ps_car[:, :]),
                 reads=[P+"carry"], psreads=[P+"pscar"], writes=[P+"carry"])
            for k in range(K):
                em.v(lambda e, k=k: e.tensor_mul(scr[:], posx[:], eqf[k][:]),
                     reads=[P+"posx", f"{P}eqf{k}"], writes=[P+"scr"])
                em.v(lambda e, k=k: e.reduce_sum(cols[:, 5 + k:6 + k], scr[:], axis=AX.X),
                     reads=[P+"scr"], writes=[P+"cols"])
                em.v(lambda e, k=k: e.scalar_tensor_tensor(
                    addrf[:, k:k + 1], cols[:, 1 + k:2 + k], float(CAP), cols[:, 5 + k:6 + k],
                    ALU.mult, ALU.add),
                    reads=[P+"cols"], writes=[P+"addrf"])
                em.v(lambda e, k=k: e.tensor_scalar(
                    cols[:, 9:10], cols[:, 5 + k:6 + k], float(CAP), 8192.0, ALU.is_ge, ALU.mult),
                    reads=[P+"cols"], writes=[P+"cols"])
                em.v(lambda e, k=k: e.tensor_add(addrf[:, k:k + 1], addrf[:, k:k + 1], cols[:, 9:10]),
                     reads=[P+"cols", P+"addrf"], writes=[P+"addrf"])
            em.v(lambda e, tt=tt: e.tensor_copy(addr_sb[:, tt, :], addrf[:]),
                 reads=[P+"addrf"], writes=[P+"addr"])
        bigt = sbr("bigt", [128, 40], U32)
        em.v(lambda e: e.memset(bigt[:], 1 << 20), writes=[P+"bigt"])
        em.dma("gpsimd", slot_dram.rearrange("(a b) c -> a (b c)", a=128), bigt[:],
               reads=[P+"bigt"], writes=[P+"slotd"], tag="slotinit")
        for tt in range(NTT):
            for k in range(K):
                em.op("gpsimd", lambda e, tt=tt, k=k: e.indirect_dma_start(
                    out=slot_dram[:, :],
                    out_offset=bass.IndirectOffsetOnAxis(addr_sb[:, tt, k:k + 1], 0),
                    in_=tok_sb[:, tt:tt + 1],
                    in_offset=None,
                    bounds_check=bc_slot,
                    oob_is_err=False),
                    reads=[P+"addr", P+"tok", P+"slotd"], writes=[P+"slotd"],
                    dma=True, tag="scat")
        em.barrier()

    ctxF = contextlib.ExitStack()
    sbf = mksb(ctxF)
    with ctxF:
        psf = mkps(ctxF)
        ps_h = [psf(f"psh{i}", [128, 320]) for i in range(2)]
        ps_y = [psf(f"psy{i}", [128, 512]) for i in range(2)]
        ps_tr = [psf(f"pstr{i}", [128, 128], BF16) for i in range(2)]
        bufE = sbf("bufE", [128, 8, CAP], BF16)
        hfm = sbf("hfm", [128, 32, CAP], BF16)
        w1t = [sbf(f"w1t{i}", [128, 8, 1024], BF16) for i in range(2)]
        w2t = sbf("w2t", [128, 32, 1024], BF16)
        slot_ids = sbf("slotids", [128, 5], U32)
        gstag = [sbf(f"gst{i}", [128, D], BF16) for i in range(2)]
        b2row = sbf("b2row", [1, D], F32)
        b2hi = sbf("b2hi", [1, D], BF16)
        b2lo = sbf("b2lo", [1, D], BF16)
        b2lof = sbf("b2lof", [1, D], F32)
        ytile = sbf("ytile", [128, D], BF16)

        for e_i in range(E):
            for st in range(5):
                em.dma("sync", slot_ids[:, st:st + 1],
                       slot_dram[e_i * CAP + st * 128:e_i * CAP + (st + 1) * 128, :],
                       reads=[P+"slotd"], writes=[P+"slotids"], tag="sids")
            em.dma("sync", b2row[:], dw["b2"][e_i:e_i + 1, :], writes=[P+"b2row"])
            em.v(lambda e: e.tensor_copy(b2hi[:], b2row[:]), reads=[P+"b2row"], writes=[P+"b2hi"])
            em.v(lambda e: e.tensor_copy(b2lof[:], b2hi[:]), reads=[P+"b2hi"], writes=[P+"b2lof"])
            em.v(lambda e: e.tensor_sub(b2lof[:], b2row[:], b2lof[:]),
                 reads=[P+"b2row", P+"b2lof"], writes=[P+"b2lof"])
            em.v(lambda e: e.tensor_copy(b2lo[:], b2lof[:]), reads=[P+"b2lof"], writes=[P+"b2lo"])
            for st in range(5):
                i = st % 2
                kg = f"{P}gst{i}"
                em.v(lambda e, i=i: e.memset(gstag[i][:], 0.0), writes=[kg])
                em.op("gpsimd", lambda e, st=st, i=i: e.indirect_dma_start(
                    out=gstag[i][:, :],
                    out_offset=None,
                    in_=x_in[:, :],
                    in_offset=bass.IndirectOffsetOnAxis(slot_ids[:, st:st + 1], 0),
                    bounds_check=bc_tok,
                    oob_is_err=False),
                    reads=[P+"slotids", kg], writes=[kg], dma=True, tag="gath")
                for dc in range(8):
                    sl = dc % 2
                    em.t(lambda e, i=i, dc=dc, sl=sl: e.transpose(
                        ps_tr[sl][:, :], gstag[i][:, dc * 128:(dc + 1) * 128], cst.ident_b[:]),
                        reads=[kg, "identb"], writes=[f"{P}pstr{sl}"])
                    em.sc(lambda e, st=st, dc=dc, sl=sl: e.copy(
                        bufE[:, dc, st * 128:(st + 1) * 128], ps_tr[sl][:, :]),
                        psreads=[f"{P}pstr{sl}"], writes=[P+"bufE"])
            for q in range(4):
                iw = q % 2
                kw = f"{P}w1t{iw}"
                em.dma("gpsimd", w1t[iw][:],
                       dw["w1"][e_i, :, q * 1024:(q + 1) * 1024].rearrange("(c p) m -> p c m", p=128),
                       writes=[kw], tag="w1")
                for hh in range(8):
                    ht = q * 8 + hh
                    for w in range(2):
                        jp = (ht * 2 + w) % 2
                        kp = f"{P}psh{jp}"
                        for dc in range(8):
                            em.t(lambda e, iw=iw, hh=hh, w=w, dc=dc, jp=jp: e.matmul(
                                ps_h[jp][:, :], w1t[iw][:, dc, hh * 128:(hh + 1) * 128],
                                bufE[:, dc, w * 320:(w + 1) * 320],
                                start=(dc == 0), stop=(dc == 7)),
                                reads=[kw, P+"bufE"], writes=[kp], tag="w1mm")
                        em.sc(lambda e, ht=ht, w=w, e_i=e_i, jp=jp: e.activation(
                            out=hfm[:, ht, w * 320:(w + 1) * 320], in_=ps_h[jp][:, :], func=AF.Relu,
                            bias=b1_s[:, e_i, ht:ht + 1]),
                            reads=[P+"b1"], psreads=[kp], writes=[P+"hfm"])
            for q in range(4):
                em.dma("gpsimd", w2t[:, q * 8:(q + 1) * 8, :],
                       dw["w2"][e_i, q * 1024:(q + 1) * 1024, :].rearrange("(c p) m -> p c m", p=128),
                       writes=[P+"w2t"], tag="w2")
            for st in range(5):
                for cw in range(2):
                    jp = (st * 2 + cw) % 2
                    kp = f"{P}psy{jp}"
                    em.t(lambda e, cw=cw, jp=jp: e.matmul(
                        ps_y[jp][:, :], onesbr[:], b2hi[:, cw * 512:(cw + 1) * 512],
                        start=True, stop=False),
                        reads=[P+"onesbr", P+"b2hi"], writes=[kp], tag="b2a")
                    em.t(lambda e, cw=cw, jp=jp: e.matmul(
                        ps_y[jp][:, :], onesbr[:], b2lo[:, cw * 512:(cw + 1) * 512],
                        start=False, stop=False),
                        reads=[P+"onesbr", P+"b2lo"], writes=[kp], tag="b2b")
                    for hc in range(32):
                        em.t(lambda e, st=st, cw=cw, hc=hc, jp=jp: e.matmul(
                            ps_y[jp][:, :], hfm[:, hc, st * 128:(st + 1) * 128],
                            w2t[:, hc, cw * 512:(cw + 1) * 512],
                            start=False, stop=(hc == 31)),
                            reads=[P+"hfm", P+"w2t"], writes=[kp], tag="w2mm")
                    em.sc(lambda e, cw=cw, jp=jp: e.copy(
                        ytile[:, cw * 512:(cw + 1) * 512], ps_y[jp][:, :]),
                        psreads=[kp], writes=[P+"ytile"])
                em.dma("scalar", y_dram[e_i * CAP + st * 128:e_i * CAP + (st + 1) * 128, :],
                       ytile[:], reads=[P+"ytile"], writes=[P+"ydram"], tag="yst")
        em.barrier()

    ctxC = contextlib.ExitStack()
    sbc = mksb(ctxC)
    with ctxC:
        ybt = [sbc(f"ybt{i}", [128, D], BF16) for i in range(2)]
        xres = [sbc(f"xres{i}", [128, D], F32) for i in range(2)]
        s_t = [sbc(f"sc{i}", [128, D], F32) for i in range(2)]
        outt = [sbc(f"outc{i}", [128, D], F32) for i in range(2)]
        for tt in range(NTT):
            iq = tt % 2
            rows = slice(tt * 128, (tt + 1) * 128)
            em.dma("sync", xres[iq][:], x_in[rows, :], writes=[f"{P}xres{iq}"])
            em.v(lambda e, iq=iq: e.tensor_copy(s_t[iq][:], xres[iq][:]),
                 reads=[f"{P}xres{iq}"], writes=[f"{P}sc{iq}"])
            for k in range(K):
                em.v(lambda e, iq=iq: e.memset(ybt[iq][:], 0.0), writes=[f"{P}ybt{iq}"])
                em.op("gpsimd", lambda e, tt=tt, k=k, iq=iq: e.indirect_dma_start(
                    out=ybt[iq][:, :],
                    out_offset=None,
                    in_=y_dram[:, :],
                    in_offset=bass.IndirectOffsetOnAxis(addr_sb[:, tt, k:k + 1], 0),
                    bounds_check=bc_slot,
                    oob_is_err=False),
                    reads=[P+"addr", P+"ydram", f"{P}ybt{iq}"], writes=[f"{P}ybt{iq}"],
                    dma=True, tag="gathy")
                em.v(lambda e, tt=tt, k=k, iq=iq: e.scalar_tensor_tensor(
                    s_t[iq][:], ybt[iq][:], pbuf[:, tt, k:k + 1], s_t[iq][:], ALU.mult, ALU.add),
                    reads=[f"{P}ybt{iq}", P+"pbuf", f"{P}sc{iq}"], writes=[f"{P}sc{iq}"])
            emit_ln(em, P, s_t[iq], f"{P}sc{iq}", sq_t, red, outt[iq], f"{P}outc{iq}")
            em.dma("scalar", x_out[rows, :], outt[iq][:], reads=[f"{P}outc{iq}"], tag="xout")
        em.barrier()




T = 4096
D = 1024
NH = 16
HD = 64
W2 = 128
NTT = T // 128
EPS = 1e-5
NEG = -30000.0


def l1a_host_prep(w, pref=""):
    d = {
        "wqkT": np.ascontiguousarray(w["w_qkv"][0:2048, :].T),
        "wvT": np.ascontiguousarray(w["w_qkv"][2048:3072, :].T),
        "woT": np.ascontiguousarray(w["w_out"].T),
        "gamma": w["gamma"].reshape(1, D),
        "beta": w["beta"].reshape(1, D),
    }
    return {pref + k: v for k, v in d.items()}


def host_consts3():
    p = np.arange(128)[:, None]
    c = np.arange(384)[None, :]
    mi = np.where(np.abs(c - 128 - p) <= W2, 0.0, NEG).astype(np.float32)
    c2 = np.arange(256)[None, :]
    mf = np.where(c2 - p <= W2, 0.0, NEG).astype(np.float32)
    ml = np.where(c2 >= p, 0.0, NEG).astype(np.float32)
    return {"maski": mi, "maskf": mf, "maskl": ml}


def declare_consts3(nc):
    return {
        "maski": nc.dram_tensor("maski", [128, 384], F32, kind="ExternalInput").ap(),
        "maskf": nc.dram_tensor("maskf", [128, 256], F32, kind="ExternalInput").ap(),
        "maskl": nc.dram_tensor("maskl", [128, 256], F32, kind="ExternalInput").ap(),
    }


def l1a_declare_dram(nc, pref=""):
    d = {}
    def inp(name, shape, dt=F32):
        d[name] = nc.dram_tensor(pref + name, shape, dt, kind="ExternalInput").ap()
    inp("wqkT", [D, 2048])
    inp("wvT", [D, D])
    inp("woT", [D, D])
    inp("gamma", [1, D])
    inp("beta", [1, D])
    return d


def l1a_emit(em, nc, ctx, cst, cst3, dw, x_in, x_out, pref="l1a"):
    P = pref
    def mksb(c):
        def sb(name, shape, dtype=F32):
            return c.enter_context(nc.sbuf_tensor(P + "sb_" + name, shape, dtype))
        return sb
    def mkps(c):
        def ps(name, shape, dtype=F32):
            return c.enter_context(nc.psum_tensor(P + "pp_" + name, shape, dtype))
        return ps
    sbp = mksb(ctx)

    xn_dram = nc.dram_tensor(P + "xn_scr", [T, D], BF16, kind="Internal").ap()

    wqk_b = sbp("wqk", [128, 8, 2048], BF16)
    wv_b = sbp("wv", [128, 8, D], BF16)
    wo_b = sbp("wo", [128, 8, D], BF16)
    gbc = sbp("gbc", [128, D])
    bbc = sbp("bbc", [128, D])
    grow = sbp("grow", [1, D])
    brow = sbp("brow", [1, D])
    onesf1 = sbp("onesf1", [1, 128], F32)
    maski = sbp("maski", [128, 384])
    maskf = sbp("maskf", [128, 256])
    maskl = sbp("maskl", [128, 256])
    red = sbp("red", [128, 4], F32)
    sq_t = sbp("sq", [128, D], F32)

    em.dma("gpsimd", wqk_b[:], dw["wqkT"].rearrange("(c p) m -> p c m", p=128), writes=[P+"wqk"])
    em.dma("gpsimd", wv_b[:], dw["wvT"].rearrange("(c p) m -> p c m", p=128), writes=[P+"wv"])
    em.dma("gpsimd", wo_b[:], dw["woT"].rearrange("(c p) m -> p c m", p=128), writes=[P+"wo"])
    em.dma("sync", grow[:], dw["gamma"][:, :], writes=[P+"grow"])
    em.dma("sync", brow[:], dw["beta"][:, :], writes=[P+"brow"])
    em.dma("sync", maski[:], cst3["maski"][:, :], writes=[P+"maski"])
    em.dma("sync", maskf[:], cst3["maskf"][:, :], writes=[P+"maskf"])
    em.dma("sync", maskl[:], cst3["maskl"][:, :], writes=[P+"maskl"])
    em.v(lambda e: e.memset(onesf1[:], 1.0), writes=[P+"onesf1"])

    ctxN = contextlib.ExitStack()
    sbn = mksb(ctxN)
    with ctxN:
        psn = mkps(ctxN)
        ps_g = psn("psg", [128, 512])
        ps_g2 = psn("psg2", [128, 512])
        xld = [sbn(f"xld{i}", [128, D], F32) for i in range(2)]
        xnb = [sbn(f"xnb{i}", [128, D], BF16) for i in range(2)]
        tmp = sbn("tmp", [128, D], F32)
        for half in range(2):
            row = grow if half == 0 else brow
            krow = P + ("grow" if half == 0 else "brow")
            dst = gbc if half == 0 else bbc
            kdst = P + ("gbc" if half == 0 else "bbc")
            for hw in range(2):
                pg = ps_g if hw == 0 else ps_g2
                kpg = P + ("psg0" if hw == 0 else "psg1")
                em.t(lambda e, pg=pg, row=row, hw=hw: e.matmul(
                    pg[:, :], onesf1[:], row[:, hw * 512:(hw + 1) * 512],
                    start=True, stop=True),
                    reads=[P+"onesf1", krow], writes=[kpg], tag="gb")
                em.v(lambda e, dst=dst, hw=hw, pg=pg: e.tensor_copy(
                    dst[:, hw * 512:(hw + 1) * 512], pg[:, :]),
                    psreads=[kpg], writes=[kdst])
        for tt in range(NTT):
            i = tt % 2
            kxl = f"{P}xld{i}"
            rows = slice(tt * 128, (tt + 1) * 128)
            em.dma("sync", xld[i][:], x_in[rows, :], writes=[kxl])
            em.v(lambda e, i=i: e.reduce_sum(red[:, 0:1], xld[i][:], axis=AX.X),
                 reads=[kxl], writes=[P+"red"])
            em.v(lambda e: e.tensor_scalar_mul(red[:, 1:2], red[:, 0:1], -1.0 / D),
                 reads=[P+"red"], writes=[P+"red"])
            em.v(lambda e, i=i: e.tensor_scalar_add(tmp[:], xld[i][:], red[:, 1:2]),
                 reads=[kxl, P+"red"], writes=[P+"tmp"])
            em.sc(lambda e: e.activation(out=sq_t[:], in_=tmp[:], func=AF.Square,
                                         accum_out=red[:, 2:3]),
                  reads=[P+"tmp"], writes=[P+"sq", P+"red"])
            em.v(lambda e: e.tensor_scalar(red[:, 3:4], red[:, 2:3], 1.0 / D, EPS, ALU.mult, ALU.add),
                 reads=[P+"red"], writes=[P+"red"])
            em.sc(lambda e: e.activation(out=red[:, 3:4], in_=red[:, 3:4], func=AF.Ln),
                  reads=[P+"red"], writes=[P+"red"])
            em.sc(lambda e: e.activation(out=red[:, 3:4], in_=red[:, 3:4], func=AF.Exp, scale=-0.5),
                  reads=[P+"red"], writes=[P+"red"])
            em.v(lambda e: e.tensor_scalar_mul(tmp[:], tmp[:], red[:, 3:4]),
                 reads=[P+"tmp", P+"red"], writes=[P+"tmp"])
            em.v(lambda e: e.tensor_mul(tmp[:], tmp[:], gbc[:]),
                 reads=[P+"tmp", P+"gbc"], writes=[P+"tmp"])
            em.v(lambda e, i=i: e.tensor_add(xnb[i][:], tmp[:], bbc[:]),
                 reads=[P+"tmp", P+"bbc"], writes=[f"{P}xnb{i}"])
            em.dma("scalar", xn_dram[rows, :], xnb[i][:], reads=[f"{P}xnb{i}"],
                   writes=[P+"xnd"], tag="xnst")
        em.barrier()

    ctxK = contextlib.ExitStack()
    sbk = mksb(ctxK)
    kfm = ctx.enter_context(nc.sbuf_tensor(P + "sb_kfm", [128, 8, T], BF16))
    with ctxK:
        psk = mkps(ctxK)
        ps_trK = [psk(f"pstr{i}", [128, 128], BF16) for i in range(2)]
        ps_kK = [psk(f"psk{i}", [128, 128]) for i in range(2)]
        xldbK = [sbk(f"xldbK{i}", [128, D], BF16) for i in range(2)]
        xfmtK = sbk("xfmtK", [128, 8, 128], BF16)
        for tt in range(NTT):
            i = tt % 2
            kxl = f"{P}xldbK{i}"
            em.dma("sync", xldbK[i][:], xn_dram[tt * 128:(tt + 1) * 128, :],
                   reads=[P+"xnd"], writes=[kxl])
            for dc in range(8):
                sl = dc % 2
                em.t(lambda e, i=i, dc=dc, sl=sl: e.transpose(
                    ps_trK[sl][:, :], xldbK[i][:, dc * 128:(dc + 1) * 128], cst.ident_b[:]),
                    reads=[kxl, "identb"], writes=[f"{P}pstr{sl}"])
                em.v(lambda e, dc=dc, sl=sl: e.tensor_copy(xfmtK[:, dc, :], ps_trK[sl][:, :]),
                     psreads=[f"{P}pstr{sl}"], writes=[P+"xfmtK"])
            for kf in range(8):
                sl = kf % 2
                kp = f"{P}psk{sl}"
                for dc in range(8):
                    em.t(lambda e, kf=kf, dc=dc, sl=sl: e.matmul(
                        ps_kK[sl][:, :], wqk_b[:, dc, 1024 + kf * 128:1024 + (kf + 1) * 128],
                        xfmtK[:, dc, :], start=(dc == 0), stop=(dc == 7)),
                        reads=[P+"wqk", P+"xfmtK"], writes=[kp], tag="kmm")
                em.sc(lambda e, kf=kf, tt=tt, sl=sl: e.copy(
                    kfm[:, kf, tt * 128:(tt + 1) * 128], ps_kK[sl][:, :]),
                    psreads=[kp], writes=[P+"kfm"])
        em.barrier()

    ctxA = contextlib.ExitStack()
    sba = mksb(ctxA)
    with ctxA:
        psa = mkps(ctxA)
        ps_q = [psa(f"psq{i}", [128, 128]) for i in range(2)]
        ps_v = [psa(f"psv{i}", [128, 512]) for i in range(2)]
        ps_s = psa("pss", [128, 384])
        ps_trA = psa("pstrA", [128, 128], BF16)
        ps_av = psa("psav", [128, 128])
        ps_o = psa("pso", [128, 512])
        xnr = sba("xnr", [128, 2, 8, 128], BF16)
        vring = sba("vring", [128, 3, D], BF16)
        q_sb = sba("qsb", [128, 8, 128], BF16)
        att = sba("att", [128, 384])
        attb = sba("attb", [128, 384], BF16)
        attT = sba("attT", [128, 3, 128], BF16)
        aout = sba("aout", [128, 8, 128], BF16)
        xres = [sba(f"xres{i}", [128, D], F32) for i in range(2)]
        s_t = [sba(f"sa{i}", [128, D], F32) for i in range(2)]
        outt = [sba(f"outa{i}", [128, D], F32) for i in range(2)]
        xldbA = [sba(f"xldc{i}", [128, D], BF16) for i in range(2)]

        def load_xn_tile(tile, slot):
            i = tile % 2
            kxl = f"{P}xldc{i}"
            em.dma("sync", xldbA[i][:], xn_dram[tile * 128:(tile + 1) * 128, :],
                   reads=[P+"xnd"], writes=[kxl])
            for dc in range(8):
                sl = dc % 2
                em.t(lambda e, i=i, dc=dc: e.transpose(
                    ps_trA[:, :], xldbA[i][:, dc * 128:(dc + 1) * 128], cst.ident_b[:]),
                    reads=[kxl, "identb"], writes=[P+"pstrA"])
                em.v(lambda e, dc=dc, slot=slot: e.tensor_copy(xnr[:, slot, dc, :], ps_trA[:, :]),
                     psreads=[P+"pstrA"], writes=[P+"xnr"])

        def compute_v_tile(tile, vslot, xslot):
            for hw in range(2):
                kp = f"{P}psv{hw}"
                for dc in range(8):
                    em.t(lambda e, hw=hw, dc=dc, xslot=xslot: e.matmul(
                        ps_v[hw][:, :], xnr[:, xslot, dc, :], wv_b[:, dc, hw * 512:(hw + 1) * 512],
                        start=(dc == 0), stop=(dc == 7)),
                        reads=[P+"xnr", P+"wv"], writes=[kp], tag="vmm")
                em.sc(lambda e, hw=hw, vslot=vslot: e.copy(
                    vring[:, vslot, hw * 512:(hw + 1) * 512], ps_v[hw][:, :]),
                    psreads=[kp], writes=[P+"vring"])

        load_xn_tile(0, 0)
        compute_v_tile(0, 0, 0)
        load_xn_tile(1, 1)
        compute_v_tile(1, 1, 1)

        for qt in range(NTT):
            if qt + 1 < NTT and qt > 0:
                xslot = (qt + 1) % 2
                load_xn_tile(qt + 1, xslot)
                compute_v_tile(qt + 1, (qt + 1) % 3, xslot)
            for qf in range(8):
                sl = qf % 2
                kp = f"{P}psq{sl}"
                for dc in range(8):
                    em.t(lambda e, qf=qf, dc=dc, sl=sl, qt=qt: e.matmul(
                        ps_q[sl][:, :], wqk_b[:, dc, qf * 128:(qf + 1) * 128],
                        xnr[:, qt % 2, dc, :], start=(dc == 0), stop=(dc == 7)),
                        reads=[P+"wqk", P+"xnr"], writes=[kp], tag="qmm")
                em.sc(lambda e, qf=qf, sl=sl: e.activation(
                    out=q_sb[:, qf, :], in_=ps_q[sl][:, :], func=AF.Copy, scale=0.125),
                    psreads=[kp], writes=[P+"qsb"])
            ks = [t for t in (qt - 1, qt, qt + 1) if 0 <= t < NTT]
            kw = len(ks) * 128
            k0 = ks[0] * 128
            mask = maski if len(ks) == 3 else (maskf if qt == 0 else maskl)
            kmask = P + ("maski" if len(ks) == 3 else ("maskf" if qt == 0 else "maskl"))
            for h in range(NH):
                hp = (h % 2) * 64
                hf = h // 2
                em.t(lambda e, hp=hp, hf=hf, k0=k0, kw=kw: e.matmul(
                    ps_s[:, 0:kw], q_sb[hp:hp + 64, hf, :], kfm[hp:hp + 64, hf, k0:k0 + kw],
                    start=True, stop=True),
                    reads=[P+"qsb", P+"kfm"], writes=[P+"pss"], tag="smm")
                em.v(lambda e, kw=kw, mask=mask: e.tensor_add(att[:, 0:kw], ps_s[:, 0:kw], mask[:, 0:kw]),
                     reads=[kmask], psreads=[P+"pss"], writes=[P+"att"])
                em.v(lambda e, kw=kw: e.reduce_max(red[:, 0:1], att[:, 0:kw], axis=AX.X),
                     reads=[P+"att"], writes=[P+"red"])
                em.v(lambda e: e.tensor_scalar_mul(red[:, 1:2], red[:, 0:1], -1.0),
                     reads=[P+"red"], writes=[P+"red"])
                em.sc(lambda e, kw=kw: e.activation(out=att[:, 0:kw], in_=att[:, 0:kw], func=AF.Exp,
                                                    bias=red[:, 1:2], accum_out=red[:, 2:3]),
                      reads=[P+"att", P+"red"], writes=[P+"att", P+"red"])
                em.v(lambda e: e.reciprocal(red[:, 3:4], red[:, 2:3]),
                     reads=[P+"red"], writes=[P+"red"])
                em.sc(lambda e, kw=kw: e.mul(attb[:, 0:kw], att[:, 0:kw], red[:, 3:4]),
                      reads=[P+"att", P+"red"], writes=[P+"attb"])
                for j in range(len(ks)):
                    em.t(lambda e, j=j: e.transpose(ps_trA[:, :], attb[:, j * 128:(j + 1) * 128],
                                                    cst.ident_b[:]),
                         reads=[P+"attb", "identb"], writes=[P+"pstrA"])
                    em.sc(lambda e, j=j: e.copy(attT[:, j, :], ps_trA[:, :]),
                          psreads=[P+"pstrA"], writes=[P+"attT"])
                for j, kt in enumerate(ks):
                    em.t(lambda e, j=j, kt=kt, hp=hp, hf=hf, nk=len(ks): e.matmul(
                        ps_av[0:64, :], vring[:, kt % 3, (hf * 2 * 64 + hp):(hf * 2 * 64 + hp) + 64],
                        attT[:, j, :], start=(j == 0), stop=(j == nk - 1)),
                        reads=[P+"vring", P+"attT"], writes=[P+"psav"], tag="avmm")
                em.v(lambda e, hp=hp, hf=hf: e.tensor_copy(aout[hp:hp + 64, hf, :], ps_av[0:64, :]),
                     psreads=[P+"psav"], writes=[P+"aout"])
            iq = qt % 2
            rows = slice(qt * 128, (qt + 1) * 128)
            em.dma("sync", xres[iq][:], x_in[rows, :], writes=[f"{P}xres{iq}"])
            for hw in range(2):
                for dc in range(8):
                    em.t(lambda e, hw=hw, dc=dc: e.matmul(
                        ps_o[:, :], aout[:, dc, :], wo_b[:, dc, hw * 512:(hw + 1) * 512],
                        start=(dc == 0), stop=(dc == 7)),
                        reads=[P+"aout", P+"wo"], writes=[P+"pso"], tag="omm")
                em.v(lambda e, iq=iq, hw=hw: e.scalar_tensor_tensor(
                    s_t[iq][:, hw * 512:(hw + 1) * 512], xres[iq][:, hw * 512:(hw + 1) * 512],
                    2.0, ps_o[:, :], ALU.mult, ALU.add),
                    reads=[f"{P}xres{iq}"], psreads=[P+"pso"], writes=[f"{P}sa{iq}"])
            emit_ln(em, P, s_t[iq], f"{P}sa{iq}", sq_t, red, outt[iq], f"{P}outa{iq}")
            em.dma("scalar", x_out[rows, :], outt[iq][:], reads=[f"{P}outa{iq}"], tag="xout")
        em.barrier()




T = 4096
D = 1024
E = 8
HH = 1024
NTT = T // 128
BIGP = 1.0e6
EPS = 1e-5


def l1f_host_prep(w, pref=""):
    d = {
        "wr": np.ascontiguousarray(w["Wr"]),
        "wi": np.ascontiguousarray(w["Wi"]),
        "wo": np.ascontiguousarray(w["Wo"]),
    }
    return {pref + k: v for k, v in d.items()}


def host_consts4():
    sel = np.zeros((8, 8, 128), np.float32)
    for e in range(8):
        sel[e, e, :] = 1.0
    return {"sel8f": sel.reshape(8, 8 * 128)}


def declare_consts4(nc):
    return {"sel8f": nc.dram_tensor("sel8f", [8, 8 * 128], F32, kind="ExternalInput").ap()}


def l1f_declare_dram(nc, pref=""):
    d = {}
    def inp(name, shape, dt=F32):
        d[name] = nc.dram_tensor(pref + name, shape, dt, kind="ExternalInput").ap()
    inp("wr", [D, E])
    inp("wi", [E, D, HH])
    inp("wo", [E, HH, D])
    return d


def l1f_emit(em, nc, ctx, cst, cst4, dw, x_in, x_out, pref="l1f"):
    P = pref
    def mksb(c):
        def sb(name, shape, dtype=F32):
            return c.enter_context(nc.sbuf_tensor(P + "sb_" + name, shape, dtype))
        return sb
    def mkps(c):
        def ps(name, shape, dtype=F32):
            return c.enter_context(nc.psum_tensor(P + "pp_" + name, shape, dtype))
        return ps
    sbp = mksb(ctx)

    xbf_dram = nc.dram_tensor(P + "xbf_scr", [T, D], BF16, kind="Internal").ap()

    wr_s = sbp("wr", [128, 8, E], F32)
    cfm = sbp("cfm", [8, T], F32)
    io8 = sbp("io8", [128, E], F32)
    io8B = sbp("io8B", [128, E], F32)
    sel8f = sbp("sel8f", [8, 8 * 128], F32)
    red = sbp("red", [128, 4], F32)
    sq_t = sbp("sq", [128, D], F32)

    em.dma("sync", wr_s[:], dw["wr"].rearrange("(c p) m -> p c m", p=128), writes=[P+"wr"])
    em.dma("sync", sel8f[:], cst4["sel8f"][:, :], writes=[P+"sel8f"])
    em.g(lambda e: e.iota(io8[:], pattern=[[1, E]], base=0, channel_multiplier=0,
                          allow_small_or_imprecise_dtypes=True), writes=[P+"io8"])
    em.g(lambda e: e.iota(io8B[:], pattern=[[1, E]], base=int(BIGP), channel_multiplier=0,
                          allow_small_or_imprecise_dtypes=True), writes=[P+"io8B"])

    ctxR = contextlib.ExitStack()
    sbr = mksb(ctxR)
    with ctxR:
        psr = mkps(ctxR)
        ps_log = psr("pslog", [128, E])
        ps_ct = psr("psct", [128, 128])
        pstrf = [psr(f"pstrf{i}", [128, 128]) for i in range(2)]
        xld = [sbr(f"xld{i}", [128, D], F32) for i in range(2)]
        xbfs = [sbr(f"xbfs{i}", [128, D], BF16) for i in range(2)]
        xfmR = sbr("xfmR", [128, 8, 128], F32)
        probs = sbr("probs", [128, E])
        prow = sbr("prow", [128, E])
        scr = sbr("scr", [128, E])
        eqf = [sbr(f"eqf{i}", [128, E]) for i in range(2)]
        csb = sbr("csb", [128, E])
        cols = sbr("cols", [128, 12])

        for tt in range(NTT):
            i = tt % 2
            kxl = f"{P}xld{i}"
            rows = slice(tt * 128, (tt + 1) * 128)
            em.dma("sync", xld[i][:], x_in[rows, :], writes=[kxl])
            em.v(lambda e, i=i: e.tensor_copy(xbfs[i][:], xld[i][:]),
                 reads=[kxl], writes=[f"{P}xbfs{i}"])
            em.dma("scalar", xbf_dram[rows, :], xbfs[i][:], reads=[f"{P}xbfs{i}"],
                   writes=[P+"xbfd"], tag="xbst")
            for dc in range(8):
                sl = dc % 2
                em.t(lambda e, i=i, dc=dc, sl=sl: e.transpose(
                    pstrf[sl][:, :], xld[i][:, dc * 128:(dc + 1) * 128], cst.ident_f[:]),
                    reads=[kxl, "identf"], writes=[f"{P}pstrf{sl}"])
                em.v(lambda e, dc=dc, sl=sl: e.tensor_copy(xfmR[:, dc, :], pstrf[sl][:, :]),
                     psreads=[f"{P}pstrf{sl}"], writes=[P+"xfmR"])
            for dc in range(8):
                em.t(lambda e, dc=dc: e.matmul(
                    ps_log[:, :], xfmR[:, dc, :], wr_s[:, dc, :],
                    start=(dc == 0), stop=(dc == 7)),
                    reads=[P+"xfmR", P+"wr"], writes=[P+"pslog"], tag="router")
            em.v(lambda e: e.reduce_max(cols[:, 0:1], ps_log[:, :], axis=AX.X),
                 psreads=[P+"pslog"], writes=[P+"cols"])
            em.v(lambda e: e.tensor_scalar_mul(cols[:, 0:1], cols[:, 0:1], -1.0),
                 reads=[P+"cols"], writes=[P+"cols"])
            em.sc(lambda e: e.activation(out=prow[:], in_=ps_log[:, :], func=AF.Exp,
                                         bias=cols[:, 0:1], accum_out=cols[:, 9:10]),
                  reads=[P+"cols"], psreads=[P+"pslog"], writes=[P+"prow", P+"cols"])
            em.v(lambda e: e.reciprocal(cols[:, 10:11], cols[:, 9:10]),
                 reads=[P+"cols"], writes=[P+"cols"])
            em.v(lambda e: e.tensor_scalar_mul(probs[:], prow[:], cols[:, 10:11]),
                 reads=[P+"prow", P+"cols"], writes=[P+"probs"])
            em.v(lambda e: e.reduce_max(cols[:, 3:4], probs[:], axis=AX.X),
                 reads=[P+"probs"], writes=[P+"cols"])
            em.v(lambda e: e.tensor_single_scalar(scr[:], probs[:], cols[:, 3:4], ALU.is_equal),
                 reads=[P+"probs", P+"cols"], writes=[P+"scr"])
            em.v(lambda e: e.scalar_tensor_tensor(scr[:], scr[:], -BIGP, io8B[:], ALU.mult, ALU.add),
                 reads=[P+"scr", P+"io8B"], writes=[P+"scr"])
            em.v(lambda e: e.tensor_reduce(cols[:, 1:2], scr[:], axis=AX.X, op=ALU.min),
                 reads=[P+"scr"], writes=[P+"cols"])
            em.v(lambda e: e.tensor_single_scalar(eqf[0][:], io8[:], cols[:, 1:2], ALU.is_equal),
                 reads=[P+"io8", P+"cols"], writes=[P+"eqf0"])
            em.v(lambda e: e.scalar_tensor_tensor(prow[:], eqf[0][:], -2.0, probs[:], ALU.mult, ALU.add),
                 reads=[P+"eqf0", P+"probs"], writes=[P+"prow"])
            em.v(lambda e: e.reduce_max(cols[:, 4:5], prow[:], axis=AX.X),
                 reads=[P+"prow"], writes=[P+"cols"])
            em.v(lambda e: e.tensor_single_scalar(scr[:], prow[:], cols[:, 4:5], ALU.is_equal),
                 reads=[P+"prow", P+"cols"], writes=[P+"scr"])
            em.v(lambda e: e.scalar_tensor_tensor(scr[:], scr[:], -BIGP, io8B[:], ALU.mult, ALU.add),
                 reads=[P+"scr", P+"io8B"], writes=[P+"scr"])
            em.v(lambda e: e.tensor_reduce(cols[:, 2:3], scr[:], axis=AX.X, op=ALU.min),
                 reads=[P+"scr"], writes=[P+"cols"])
            em.v(lambda e: e.tensor_single_scalar(eqf[1][:], io8[:], cols[:, 2:3], ALU.is_equal),
                 reads=[P+"io8", P+"cols"], writes=[P+"eqf1"])
            em.v(lambda e: e.tensor_add(scr[:], eqf[0][:], eqf[1][:]),
                 reads=[P+"eqf0", P+"eqf1"], writes=[P+"scr"])
            em.v(lambda e: e.tensor_mul(csb[:], probs[:], scr[:]),
                 reads=[P+"probs", P+"scr"], writes=[P+"csb"])
            em.t(lambda e: e.transpose(ps_ct[0:8, :], csb[:], cst.ident_f[:]),
                 reads=[P+"csb", "identf"], writes=[P+"psct"])
            em.v(lambda e, tt=tt: e.tensor_copy(cfm[:, tt * 128:(tt + 1) * 128], ps_ct[0:8, :]),
                 psreads=[P+"psct"], writes=[P+"cfm"])
        em.barrier()

    ctxF = contextlib.ExitStack()
    sbf = mksb(ctxF)
    with ctxF:
        psf = mkps(ctxF)
        ps_cb = psf("pscb", [128, 512])
        ps_he = [psf(f"pshe{i}", [128, 512]) for i in range(2)]
        ps_o = [psf(f"pso{i}", [128, 512]) for i in range(2)]
        xfmH = sbf("xfmH", [128, 8, 2048], BF16)
        out_acc = sbf("oacc", [128, 16, D])
        wi_b = [sbf("wib0", [128, 8, HH], BF16)]
        wo_b = [sbf("wob0", [128, 8, D], BF16)]
        cb_sb = sbf("cbsb", [128, 512])
        g_fm = sbf("gfm", [128, 8, 512], BF16)
        xres = [sbf(f"xres{i}", [128, D], F32) for i in range(2)]
        s_t = [sbf(f"sf{i}", [128, D], F32) for i in range(2)]
        outt = [sbf(f"outf{i}", [128, D], F32) for i in range(2)]

        for half in range(2):
            th0 = half * 2048
            for ch in range(4):
                for dc in range(8):
                    em.dma("sync", xfmH[:, dc, ch * 512:(ch + 1) * 512],
                           xbf_dram[th0 + ch * 512:th0 + (ch + 1) * 512, dc * 128:(dc + 1) * 128],
                           reads=[P+"xbfd"], writes=[P+"xfmH"], transpose=True, tag="xtr")
            em.v(lambda e: e.memset(out_acc[:], 0.0), writes=[P+"oacc"])
            for e_i in range(E):
                ib = 0
                kwi, kwo = f"{P}wib0", f"{P}wob0"
                em.dma("gpsimd", wi_b[ib][:], dw["wi"][e_i].rearrange("(c p) m -> p c m", p=128),
                       writes=[kwi], tag="wi")
                em.dma("gpsimd", wo_b[ib][:], dw["wo"][e_i].rearrange("(c p) m -> p c m", p=128),
                       writes=[kwo], tag="wo")
                for tch in range(4):
                    tw = slice(tch * 512, (tch + 1) * 512)
                    em.t(lambda e, e_i=e_i, tch=tch, th0=th0: e.matmul(
                        ps_cb[:, :], sel8f[:, e_i * 128:(e_i + 1) * 128],
                        cfm[:, th0 + tch * 512:th0 + (tch + 1) * 512],
                        start=True, stop=True),
                        reads=[P+"sel8f", P+"cfm"], writes=[P+"pscb"], tag="cb")
                    em.v(lambda e: e.tensor_copy(cb_sb[:], ps_cb[:, :]),
                         psreads=[P+"pscb"], writes=[P+"cbsb"])
                    for ht in range(8):
                        sl = ht % 2
                        kp = f"{P}pshe{sl}"
                        for dc in range(8):
                            em.t(lambda e, ht=ht, dc=dc, sl=sl, ib=ib, tch=tch: e.matmul(
                                ps_he[sl][:, :], wi_b[ib][:, dc, ht * 128:(ht + 1) * 128],
                                xfmH[:, dc, tch * 512:(tch + 1) * 512],
                                start=(dc == 0), stop=(dc == 7)),
                                reads=[kwi, P+"xfmH"], writes=[kp], tag="hemm")
                        em.v(lambda e, ht=ht, sl=sl: e.tensor_mul(g_fm[:, ht, :], ps_he[sl][:, :], cb_sb[:]),
                             reads=[P+"cbsb"], psreads=[kp], writes=[P+"gfm"])
                    for tt2 in range(4):
                        for cw in range(2):
                            sl = (tt2 * 2 + cw) % 2
                            kp = f"{P}pso{sl}"
                            for hc in range(8):
                                em.t(lambda e, tt2=tt2, cw=cw, hc=hc, sl=sl, ib=ib: e.matmul(
                                    ps_o[sl][:, :], g_fm[:, hc, tt2 * 128:(tt2 + 1) * 128],
                                    wo_b[ib][:, hc, cw * 512:(cw + 1) * 512],
                                    start=(hc == 0), stop=(hc == 7)),
                                    reads=[P+"gfm", kwo], writes=[kp], tag="womm")
                            em.v(lambda e, tch=tch, tt2=tt2, cw=cw, sl=sl: e.tensor_add(
                                out_acc[:, tch * 4 + tt2, cw * 512:(cw + 1) * 512],
                                out_acc[:, tch * 4 + tt2, cw * 512:(cw + 1) * 512],
                                ps_o[sl][:, :]),
                                reads=[P+"oacc"], psreads=[kp], writes=[P+"oacc"])
            for tl in range(16):
                tt = half * 16 + tl
                iq = tl % 2
                rows = slice(tt * 128, (tt + 1) * 128)
                em.dma("sync", xres[iq][:], x_in[rows, :], writes=[f"{P}xres{iq}"])
                em.v(lambda e, iq=iq, tl=tl: e.tensor_add(s_t[iq][:], out_acc[:, tl, :], xres[iq][:]),
                     reads=[P+"oacc", f"{P}xres{iq}"], writes=[f"{P}sf{iq}"])
                emit_ln(em, P, s_t[iq], f"{P}sf{iq}", sq_t, red, outt[iq], f"{P}outf{iq}")
                em.dma("scalar", x_out[rows, :], outt[iq][:], reads=[f"{P}outf{iq}"], tag="xout")
        em.barrier()







def _build_fused():
    nc = bass.Bass("TRN2", target_bir_lowering=False, debug=False)
    ctx = contextlib.ExitStack()
    with ctx:
        x_in = nc.dram_tensor("x", [T, D], F32, kind="ExternalInput").ap()
        x_out = nc.dram_tensor("x4", [T, D], F32, kind="ExternalOutput").ap()
        x1s = nc.dram_tensor("x1s", [T, D], F32, kind="Internal").ap()
        x2s = nc.dram_tensor("x2s", [T, D], F32, kind="Internal").ap()
        x3s = nc.dram_tensor("x3s", [T, D], F32, kind="Internal").ap()
        dcst = declare_consts(nc)
        dcst2 = declare_consts2(nc)
        dcst3 = declare_consts3(nc)
        dcst4 = declare_consts4(nc)
        dw0a = l0a_declare_dram(nc, "l0a")
        dw0f = l0f_declare_dram(nc, "l0f")
        dw1a = l1a_declare_dram(nc, "l1a")
        dw1f = l1f_declare_dram(nc, "l1f")
        em = Em(nc, ctx)
        cst = Consts(em, nc, ctx, dcst)
        with contextlib.ExitStack() as c1:
            l0a_emit(em, nc, c1, cst, dw0a, x_in, x1s, pref="l0a")
            em.barrier()
        with contextlib.ExitStack() as c2:
            l0f_emit(em, nc, c2, cst, dcst2, dw0f, x1s, x2s, pref="l0f")
            em.barrier()
        with contextlib.ExitStack() as c3:
            l1a_emit(em, nc, c3, cst, dcst3, dw1a, x2s, x3s, pref="l1a")
            em.barrier()
        with contextlib.ExitStack() as c4:
            l1f_emit(em, nc, c4, cst, dcst4, dw1f, x3s, x_out, pref="l1f")
            em.barrier()
        em.barrier_reads_of_all()
        block = ctx.enter_context(nc.Block())
        em.replay(block)
    return nc


def kernel(**inputs):
    def grp(pref):
        return {k[len(pref):]: np.ascontiguousarray(np.asarray(v, np.float32))
                for k, v in inputs.items() if k.startswith(pref)}

    cst_all = host_consts()
    hp = {}
    hp.update(l0a_host_prep(grp("L0_attn_"), "l0a"))
    hp.update(l0f_host_prep(grp("L0_ffn_"), "l0f"))
    hp.update(l1a_host_prep(grp("L1_attn_"), "l1a"))
    hp.update(l1f_host_prep(grp("L1_ffn_"), "l1f"))
    hp.update(cst_all)
    hp.update(host_consts2())
    hp.update(host_consts3())
    hp.update(host_consts4())

    x = np.ascontiguousarray(np.asarray(inputs["L0_attn_hidden_states"], np.float32))
    nc = _build_fused()
    maps = [{**hp, "x": x[b]} for b in range(8)]
    res = run_bass_kernel_spmd(nc, maps, core_ids=list(range(8)))
    out = np.stack([np.asarray(res.results[b]["x4"]) for b in range(8)])
    return np.ascontiguousarray(out.astype(np.float32))


# version 04 (final)
# speedup vs baseline: 1.0353x; 1.0016x over previous
import os
import contextlib
import numpy as np
import ml_dtypes
import concourse.bass as bass
import concourse.mybir as mybir
from concourse.bass_utils import run_bass_kernel_spmd





F32 = mybir.dt.float32
BF16 = mybir.dt.bfloat16
U32 = mybir.dt.uint32
I32 = mybir.dt.int32
U8 = mybir.dt.uint8
AF = mybir.ActivationFunctionType
ALU = mybir.AluOpType
AX = mybir.AxisListType

STREAMS = ("sync", "tensor", "vector", "scalar", "gpsimd")


class Em:
    def __init__(self, nc, ctx):
        self.nc = nc
        self.ctx = ctx
        self.ops = {s: [] for s in STREAMS}
        self.csem = {s: ctx.enter_context(nc.semaphore("c_" + s)) for s in STREAMS}
        self.R = 8
        self.dsem = {s: [ctx.enter_context(nc.semaphore(f"d_{s}{r}")) for r in range(self.R)]
                     for s in ("sync", "scalar", "gpsimd")}
        self.ccount = {s: 0 for s in STREAMS}
        self.dcount = {s: 0 for s in ("sync", "scalar", "gpsimd")}
        self.kn = {s: {} for s in STREAMS}
        self.snaps = {("c", s): [] for s in STREAMS}
        for s in ("sync", "scalar", "gpsimd"):
            self.snaps[("d", s)] = []
        self.last_writer = {}
        self.readers = {}
        self.nwaits = 0

    def _tok_needed(self, s, toks):
        need = {}
        for tok in toks:
            if tok is None:
                continue
            kind, s2, n = tok
            if kind == "c" and s2 == "tensor" and s == "tensor":
                continue
            key = (kind, s2)
            if need.get(key, 0) < n:
                need[key] = n
        return need

    def op(self, s, fn, reads=(), writes=(), psreads=(), dma=False, tag=""):
        if psreads:
            reads = tuple(reads) + tuple(psreads)
            writes = tuple(writes) + tuple(psreads)
        toks = []
        for k in reads:
            toks.append(self.last_writer.get(k))
        for k in writes:
            toks.append(self.last_writer.get(k))
            toks.extend(self.readers.get(k, ()))
        need = self._tok_needed(s, toks)
        waits = []
        kn = self.kn[s]
        for key, n in sorted(need.items()):
            kind, s2 = key
            if kind == "c":
                if kn.get(key, 0) >= n:
                    continue
                waits.append((self.csem[s2], n))
                snap = self.snaps[key][n - 1]
                for kk, vv in snap.items():
                    if kn.get(kk, 0) < vv:
                        kn[kk] = vv
                if kn.get(key, 0) < n:
                    kn[key] = n
            else:
                slot = (n - 1) % self.R
                per = (n - 1) // self.R + 1
                kslot = ("ds", s2, slot)
                if kn.get(kslot, 0) >= per:
                    continue
                waits.append((self.dsem[s2][slot], 16 * per))
                snap = self.snaps[("d", s2)][n - 1]
                for kk, vv in snap.items():
                    if kn.get(kk, 0) < vv:
                        kn[kk] = vv
                if kn.get(kslot, 0) < per:
                    kn[kslot] = per
        self.nwaits += len(waits)
        if dma:
            assert s in self.dsem, f"stream {s} cannot issue DMAs"
            i0 = self.dcount[s] + 1
            slot = (i0 - 1) % self.R
            prev_per = (i0 - 1) // self.R
            kslot = ("ds", s, slot)
            if prev_per and kn.get(kslot, 0) < prev_per:
                waits.append((self.dsem[s][slot], 16 * prev_per))
                pi = prev_per * self.R + slot
                snap = self.snaps[("d", s)][pi - 1]
                for kk, vv in snap.items():
                    if kn.get(kk, 0) < vv:
                        kn[kk] = vv
                kn[kslot] = prev_per
                self.nwaits += 1
            self.dcount[s] += 1
            tok = ("d", s, self.dcount[s])
            inc = (self.dsem[s][slot], 16)
        else:
            self.ccount[s] += 1
            tok = ("c", s, self.ccount[s])
            inc = (self.csem[s], 1)
        snap = dict(kn)
        if tok[0] == "d":
            slot = (tok[2] - 1) % self.R
            snap[("ds", s, slot)] = (tok[2] - 1) // self.R + 1
        else:
            snap[("c", s)] = tok[2]
        self.snaps[(tok[0], s)].append(snap)
        self.ops[s].append((waits, fn, inc, tag))
        for k in reads:
            self.readers.setdefault(k, []).append(tok)
        for k in writes:
            self.last_writer[k] = tok
            self.readers[k] = []
        return tok

    def raw(self, s, fn, tag="raw"):
        self.ops[s].append(([], fn, None, tag))

    def v(self, fn, reads=(), writes=(), psreads=(), tag=""):
        return self.op("vector", fn, reads, writes, psreads, tag=tag)

    def sc(self, fn, reads=(), writes=(), psreads=(), tag=""):
        return self.op("scalar", fn, reads, writes, psreads, tag=tag)

    def t(self, fn, reads=(), writes=(), psreads=(), tag=""):
        return self.op("tensor", fn, reads, writes, psreads, tag=tag)

    def g(self, fn, reads=(), writes=(), psreads=(), tag=""):
        return self.op("gpsimd", fn, reads, writes, psreads, tag=tag)

    def dma(self, q, out, in_, reads=(), writes=(), tag="", **kw):
        return self.op(q, lambda e: e.dma_start(out=out, in_=in_, **kw),
                       reads, writes, dma=True, tag=tag)

    def barrier(self):
        counts_c = dict(self.ccount)
        counts_d = dict(self.dcount)
        for s in STREAMS:
            waits = []
            kn = self.kn[s]
            for s2 in STREAMS:
                if s2 == s:
                    continue
                n = counts_c[s2]
                if n and kn.get(("c", s2), 0) < n:
                    waits.append((self.csem[s2], n))
                    snap = self.snaps[("c", s2)][n - 1]
                    for kk, vv in snap.items():
                        if kn.get(kk, 0) < vv:
                            kn[kk] = vv
                    kn[("c", s2)] = max(kn.get(("c", s2), 0), n)
            for s2 in ("sync", "scalar", "gpsimd"):
                n = counts_d[s2]
                for slot in range(self.R):
                    per = (n - slot + self.R - 1) // self.R if n > slot else 0
                    if per and kn.get(("ds", s2, slot), 0) < per:
                        waits.append((self.dsem[s2][slot], 16 * per))
                        kn[("ds", s2, slot)] = per
                if n:
                    snap = self.snaps[("d", s2)][n - 1]
                    for kk, vv in snap.items():
                        if kn.get(kk, 0) < vv:
                            kn[kk] = vv
            n_own = counts_c[s]
            if n_own and kn.get(("c", s), 0) < n_own:
                waits.append((self.csem[s], n_own))
                kn[("c", s)] = n_own
            if s in self.dcount:
                n = counts_d[s]
                for slot in range(self.R):
                    per = (n - slot + self.R - 1) // self.R if n > slot else 0
                    if per and kn.get(("ds", s, slot), 0) < per:
                        waits.append((self.dsem[s][slot], 16 * per))
                        kn[("ds", s, slot)] = per
            self.ccount[s] += 1
            tok = ("c", s, self.ccount[s])
            snap = dict(kn)
            snap[("c", s)] = tok[2]
            self.snaps[("c", s)].append(snap)
            self.ops[s].append((waits, (lambda e: e.nop()), (self.csem[s], 1), "barrier"))
            self.nwaits += len(waits)

    def barrier_reads_of_all(self):
        def fin(eng):
            return eng.nop()
        waits = []
        kn = self.kn["sync"]
        for s in STREAMS:
            n = self.ccount[s]
            if n and kn.get(("c", s), 0) < n:
                waits.append((self.csem[s], n))
        for s in ("sync", "scalar", "gpsimd"):
            n = self.dcount[s]
            for slot in range(self.R):
                per = (n - slot + self.R - 1) // self.R if n > slot else 0
                if per and kn.get(("ds", s, slot), 0) < per:
                    waits.append((self.dsem[s][slot], 16 * per))
        n_own = self.ccount["sync"]
        if n_own and kn.get(("c", "sync"), 0) < n_own:
            waits.append((self.csem["sync"], n_own))
            kn[("c", "sync")] = n_own
        self.ccount["sync"] += 1
        self.ops["sync"].append((waits, fin, (self.csem["sync"], 1), "final"))

    def replay(self, block):
        em = self

        def make(stream):
            oplist = em.ops[stream]

            def runner(eng):
                for idx, (waits, fn, inc, tag) in enumerate(oplist):
                    for sem, val in waits:
                        eng.wait_ge(sem, val)
                    try:
                        inst = fn(eng)
                    except Exception:
                        print(f"EMIT FAIL stream={stream} idx={idx} tag={tag} "
                              f"waits={[(s.name, v) for s, v in waits]}")
                        raise
                    if inc is not None:
                        inst.then_inc(inc[0], inc[1])

            return runner

        block.sync(make("sync"))
        block.tensor(make("tensor"))
        block.vector(make("vector"))
        block.scalar(make("scalar"))
        block.gpsimd(make("gpsimd"))


def ceil_div(a, b):
    return (a + b - 1) // b


T = 4096
D = 1024
DI = 2048
NS = 16
RK = 64
TC = 512
NCH = T // TC
NDT = DI // 128
NFT = 2 * DI // 128
NTT = T // 128
EPS = 1e-5


def l0a_host_prep(w, pref=""):
    cw = w["conv_w"].reshape(NDT, 128, 4).transpose(1, 0, 2).copy()
    cb = w["conv_b"].reshape(NDT, 128).T.copy()
    dtb = w["dt_proj_b"].reshape(NDT, 128).T.copy()
    alog = w["A_log"].reshape(NDT, 128, NS).transpose(1, 0, 2).copy()
    dsk = w["D_skip"].reshape(NDT, 128).T.copy()
    d = {
        "ipw": np.ascontiguousarray(w["in_proj_w"]),
        "convw": cw, "convb": cb,
        "xpw": np.ascontiguousarray(w["x_proj_w"]),
        "dtw": np.ascontiguousarray(w["dt_proj_w"]),
        "dtb": dtb, "alog": alog, "dskip": dsk,
        "opw": np.ascontiguousarray(w["out_proj_w"]),
    }
    return {pref + k: v for k, v in d.items()}


def host_consts():
    ident = np.eye(128, dtype=np.float32)
    sel = np.zeros((32, 32, 128), np.float32)
    for n in range(32):
        sel[n, n, :] = 1.0
    return {
        "identf": ident,
        "identb": ident.astype(ml_dtypes.bfloat16),
        "sel32": sel.reshape(32, 32 * 128).astype(ml_dtypes.bfloat16),
    }


def declare_consts(nc):
    d = {}
    d["identf"] = nc.dram_tensor("identf", [128, 128], F32, kind="ExternalInput").ap()
    d["identb"] = nc.dram_tensor("identb", [128, 128], BF16, kind="ExternalInput").ap()
    d["sel32"] = nc.dram_tensor("sel32", [32, 32 * 128], BF16, kind="ExternalInput").ap()
    return d


def l0a_declare_dram(nc, pref=""):
    d = {}
    def inp(name, shape, dt=F32):
        d[name] = nc.dram_tensor(pref + name, shape, dt, kind="ExternalInput").ap()
    inp("ipw", [D, 2 * DI])
    inp("convw", [128, NDT, 4])
    inp("convb", [128, NDT])
    inp("xpw", [DI, 96])
    inp("dtw", [RK, DI])
    inp("dtb", [128, NDT])
    inp("alog", [128, NDT, NS])
    inp("dskip", [128, NDT])
    inp("opw", [DI, D])
    return d


class Consts:
    def __init__(self, em, nc, ctx, dc):
        def sb(name, shape, dtype=F32):
            return ctx.enter_context(nc.sbuf_tensor("sb_" + name, shape, dtype))
        self.ident_b = sb("identb", [128, 128], BF16)
        self.ident_f = sb("identf", [128, 128], F32)
        self.sel32 = sb("sel32", [32, 32 * 128], BF16)
        em.dma("sync", self.ident_b[:], dc["identb"][:, :], writes=["identb"])
        em.dma("sync", self.ident_f[:], dc["identf"][:, :], writes=["identf"])
        em.dma("sync", self.sel32[:], dc["sel32"][:, :], writes=["sel32"])


def emit_ln(em, P, src, ksrc, sq_t, red, dst, kdst):
    em.v(lambda e: e.reduce_sum(red[:, 0:1], src[:], axis=AX.X),
         reads=[ksrc], writes=[P + "red"])
    em.v(lambda e: e.tensor_scalar_mul(red[:, 1:2], red[:, 0:1], -1.0 / D),
         reads=[P + "red"], writes=[P + "red"])
    em.v(lambda e: e.tensor_scalar_add(src[:], src[:], red[:, 1:2]),
         reads=[ksrc, P + "red"], writes=[ksrc])
    em.sc(lambda e: e.activation(out=sq_t[:], in_=src[:], func=AF.Square,
                                 accum_out=red[:, 2:3]),
          reads=[ksrc], writes=[P + "sq", P + "red"])
    em.v(lambda e: e.tensor_scalar(red[:, 3:4], red[:, 2:3], 1.0 / D, EPS, ALU.mult, ALU.add),
         reads=[P + "red"], writes=[P + "red"])
    em.sc(lambda e: e.activation(out=red[:, 3:4], in_=red[:, 3:4], func=AF.Ln),
          reads=[P + "red"], writes=[P + "red"])
    em.sc(lambda e: e.activation(out=red[:, 3:4], in_=red[:, 3:4], func=AF.Exp, scale=-0.5),
          reads=[P + "red"], writes=[P + "red"])
    em.v(lambda e: e.tensor_scalar_mul(dst[:], src[:], red[:, 3:4]),
         reads=[ksrc, P + "red"], writes=[kdst])


def l0a_emit(em, nc, ctx, cst, dw, x_in, x_out, pref="l0a", stage=99, dbg=False):
    P = pref
    def mksb(c):
        def sb(name, shape, dtype=F32):
            return c.enter_context(nc.sbuf_tensor(P + "sb_" + name, shape, dtype))
        return sb
    sbp = mksb(ctx)
    def psum(name, shape, dtype=F32):
        return ctx.enter_context(nc.psum_tensor(P + "pp_" + name, shape, dtype))

    kind = "ExternalOutput" if dbg else "Internal"
    u_dram = nc.dram_tensor(P + "u_scr", [NDT, 128, T], F32, kind=kind).ap()
    sres_dram = nc.dram_tensor(P + "sres_scr", [NDT, 128, T], BF16, kind=kind).ap()

    xdbl = sbp("xdbl", [128, NTT, 96], F32)
    dtfm = sbp("dtfm", [RK, T], BF16)
    bmcm = sbp("bmcm", [32, T], BF16)
    xpw_b = sbp("xpw", [128, NDT, 96], BF16)
    dtw_b = sbp("dtw", [RK, DI], BF16)
    convw_s = sbp("convw", [128, NDT, 4], F32)
    convb_s = sbp("convb", [128, NDT], F32)
    dtb_s = sbp("dtb", [128, NDT], F32)
    acol_s = sbp("acol", [128, NDT * NS], F32)
    dskip_s = sbp("dskip", [128, NDT], F32)
    state = sbp("state", [128, NDT * NS], F32)
    xres = [sbp(f"xres{i}", [128, D], F32) for i in range(2)]
    s_t = [sbp(f"st{i}", [128, D], F32) for i in range(2)]
    sq_t = sbp("sq", [128, D], F32)
    red = sbp("red", [128, 4], F32)
    outt = [sbp(f"outt{i}", [128, D], F32) for i in range(2)]


    em.dma("sync", convw_s[:], dw["convw"][:, :, :], writes=[P+"convw"])
    em.dma("sync", convb_s[:], dw["convb"][:, :], writes=[P+"convb"])
    em.dma("sync", dtb_s[:], dw["dtb"][:, :], writes=[P+"dtb"])
    em.dma("sync", acol_s[:], dw["alog"].rearrange("p a b -> p (a b)"), writes=[P+"acol"])
    em.dma("sync", dskip_s[:], dw["dskip"][:, :], writes=[P+"dskip"])
    em.dma("gpsimd", xpw_b[:], dw["xpw"].rearrange("(c p) m -> p c m", p=128), writes=[P+"xpw"])
    em.dma("gpsimd", dtw_b[:], dw["dtw"][:, :], writes=[P+"dtw"])
    em.sc(lambda e: e.activation(out=acol_s[:], in_=acol_s[:], func=AF.Exp),
          reads=[P+"acol"], writes=[P+"acol"])
    em.v(lambda e: e.tensor_scalar_mul(acol_s[:], acol_s[:], -1.0),
         reads=[P+"acol"], writes=[P+"acol"])
    em.v(lambda e: e.memset(state[:], 0.0), writes=[P+"state"])
    em.v(lambda e: e.memset(xdbl[:], 0.0), writes=[P+"xdbl"])

    ctxA = contextlib.ExitStack()
    sba = mksb(ctxA)
    def mkps(c):
        def ps(name, shape, dtype=F32):
            return c.enter_context(nc.psum_tensor(P + "pp_" + name, shape, dtype))
        return ps
    with ctxA:
        psa_ = mkps(ctxA)
        ps_a = psa_("psa", [128, 512])
        ps_tr = [psa_(f"pstr{i}", [128, 128], BF16) for i in range(2)]
        ps_96 = [psa_(f"ps96{i}", [128, 96]) for i in range(2)]
        ps_tf = [psa_(f"pstf{i}", [128, 128]) for i in range(2)]
        xfm = sba("xfm", [128, 8, T], BF16)
        xld = [sba(f"xld{i}", [128, D], F32) for i in range(2)]
        xbf = [sba(f"xbf{i}", [128, D], BF16) for i in range(2)]
        ipw_t = [sba(f"ipwt{i}", [128, 8, 128], BF16) for i in range(2)]
        xz_t = sba("xz", [128, T], F32)
        cacc = sba("cacc", [128, T], F32)
        ubf_t = sba("ubf", [128, T], BF16)
        sres_t = sba("srest", [128, T], BF16)

        import os as _os
        _cap = int(_os.environ.get("NTTCAP", str(NTT)))
        for tt in range(_cap):
            i = tt % 2
            kxl, kxb = f"{P}xld{i}", f"{P}xbf{i}"
            em.dma("sync", xld[i][:], x_in[tt * 128:(tt + 1) * 128, :], writes=[kxl], tag="p0ld")
            em.v(lambda e, i=i: e.tensor_copy(xbf[i][:], xld[i][:]), reads=[kxl], writes=[kxb])
            for dc in range(8):
                slot = dc % 2
                kps = f"{P}pstr{slot}"
                em.t(lambda e, i=i, dc=dc, slot=slot: e.transpose(
                    ps_tr[slot][:, :], xbf[i][:, dc * 128:(dc + 1) * 128], cst.ident_b[:]),
                    reads=[kxb, "identb"], writes=[kps])
                if dc % 2:
                    em.sc(lambda e, tt=tt, dc=dc, slot=slot: e.copy(
                        xfm[:, dc, tt * 128:(tt + 1) * 128], ps_tr[slot][:, :]),
                        reads=[], psreads=[kps], writes=[P+"xfm"])
                else:
                    em.v(lambda e, tt=tt, dc=dc, slot=slot: e.tensor_copy(
                        xfm[:, dc, tt * 128:(tt + 1) * 128], ps_tr[slot][:, :]),
                        reads=[], psreads=[kps], writes=[P+"xfm"])

        for ft in (range(NFT) if stage >= 2 else []):
            i = ft % 2
            kw = f"{P}ipwt{i}"
            em.dma("gpsimd", ipw_t[i][:], dw["ipw"][:, ft * 128:(ft + 1) * 128].rearrange("(c p) m -> p c m", p=128),
                   writes=[kw], tag="p1w")
            for w in range(8):
                for dc in range(8):
                    em.t(lambda e, i=i, w=w, dc=dc: e.matmul(
                        ps_a[:, :], ipw_t[i][:, dc, :], xfm[:, dc, w * 512:(w + 1) * 512],
                        start=(dc == 0), stop=(dc == 7)),
                        reads=[kw, P+"xfm"], writes=[P+"psa"], tag="p1mm")
                em.sc(lambda e, w=w: e.copy(xz_t[:, w * 512:(w + 1) * 512], ps_a[:, :]),
                      psreads=[P+"psa"], writes=[P+"xz"])
            if ft < NDT:
                dt_i = ft
                em.v(lambda e, dt_i=dt_i: e.tensor_scalar_mul(cacc[:], xz_t[:], convw_s[:, dt_i, 3:4]),
                     reads=[P+"xz", P+"convw"], writes=[P+"cacc"])
                for s in range(1, 4):
                    em.v(lambda e, s=s, dt_i=dt_i: e.scalar_tensor_tensor(
                        cacc[:, s:T], xz_t[:, 0:T - s], convw_s[:, dt_i, 3 - s:4 - s], cacc[:, s:T],
                        ALU.mult, ALU.add),
                        reads=[P+"xz", P+"convw", P+"cacc"], writes=[P+"cacc"])
                em.sc(lambda e, dt_i=dt_i: e.activation(out=cacc[:], in_=cacc[:], func=AF.Silu,
                                                        bias=convb_s[:, dt_i:dt_i + 1]),
                      reads=[P+"cacc", P+"convb"], writes=[P+"cacc"])
                em.dma("scalar", u_dram[dt_i, :, :], cacc[:], reads=[P+"cacc"], tag="p1us")
                em.v(lambda e: e.tensor_copy(ubf_t[:], cacc[:]), reads=[P+"cacc"], writes=[P+"ubf"])
                for tt in range(NTT):
                    j = tt % 2
                    kp = f"{P}ps96{j}"
                    em.t(lambda e, tt=tt, dt_i=dt_i, j=j: e.matmul(
                        ps_96[j][:, :], ubf_t[:, tt * 128:(tt + 1) * 128], xpw_b[:, dt_i, :],
                        start=True, stop=True),
                        reads=[P+"ubf", P+"xpw"], writes=[kp])
                    em.v(lambda e, tt=tt, j=j: e.tensor_add(xdbl[:, tt, :], xdbl[:, tt, :], ps_96[j][:, :]),
                         reads=[P+"xdbl"], psreads=[kp], writes=[P+"xdbl"])
            else:
                dt_i = ft - NDT
                em.sc(lambda e: e.activation(out=sres_t[:], in_=xz_t[:], func=AF.Silu),
                      reads=[P+"xz"], writes=[P+"srest"])
                em.dma("scalar", sres_dram[dt_i, :, :], sres_t[:], reads=[P+"srest"], tag="p1ss")

        for tt in (range(NTT) if stage >= 3 else []):
            em.t(lambda e, tt=tt: e.transpose(ps_tf[0][0:64, :], xdbl[:, tt, 0:64], cst.ident_f[:]),
                 reads=[P+"xdbl", "identf"], writes=[P+"pstf0"])
            em.v(lambda e, tt=tt: e.tensor_copy(dtfm[:, tt * 128:(tt + 1) * 128], ps_tf[0][0:64, :]),
                 psreads=[P+"pstf0"], writes=[P+"dtfm"])
            em.t(lambda e, tt=tt: e.transpose(ps_tf[1][0:32, :], xdbl[:, tt, 64:96], cst.ident_f[:]),
                 reads=[P+"xdbl", "identf"], writes=[P+"pstf1"])
            em.v(lambda e, tt=tt: e.tensor_copy(bmcm[:, tt * 128:(tt + 1) * 128], ps_tf[1][0:32, :]),
                 psreads=[P+"pstf1"], writes=[P+"bmcm"])

        em.barrier()

    if stage < 4:
        em.v(lambda e: e.memset(outt[0][:], 0.0), writes=[P+"outt0"])
        for tt in range(NTT):
            em.dma("scalar", x_out[tt * 128:(tt + 1) * 128, :], outt[0][:],
                   reads=[P+"outt0"], tag="dummy")
        return
    ctxB = contextlib.ExitStack()
    sbb = mksb(ctxB)
    with ctxB:
        psb_ = mkps(ctxB)
        ps_a = psb_("psdt", [128, 512])
        ps_b = psb_("psbc", [128, 512])
        ps_y = psb_("psy", [128, 512])
        ps_o = [psb_(f"pso{i}", [128, 512]) for i in range(2)]
        opw_b = sbb("opw", [128, NDT, D], BF16)
        bcB = sbb("bcB", [128, NS, TC], BF16)
        bcC = sbb("bcC", [128, NS, TC], BF16)
        sp1 = sbb("sp1", [128, TC], F32)
        spt = sbb("spt", [128, TC], F32)
        delta_t = sbb("delta", [128, TC], F32)
        uch = [sbb(f"uch{i}", [128, TC], F32) for i in range(2)]
        du_t = sbb("du", [128, TC], F32)
        uDb_t = sbb("uDb", [128, TC], BF16)
        dA = [sbb(f"dA{i}", [128, TC], F32) for i in range(2)]
        dBu = [sbb(f"dBu{i}", [128, TC], F32) for i in range(2)]
        h_t = [sbb(f"h{i}", [128, TC], F32) for i in range(2)]
        tmpb = [sbb(f"tmpb{i}", [128, TC], BF16) for i in range(2)]
        sresc = [sbb(f"sresc{i}", [128, TC], BF16) for i in range(2)]
        yfin = sbb("yfin", [128, NDT, TC], BF16)

        em.dma("gpsimd", opw_b[:], dw["opw"].rearrange("(c p) m -> p c m", p=128), writes=[P+"opw"])

        SS = os.environ.get("SCANSUB", "d")
        _nchcap = int(os.environ.get("NCHCAP", str(NCH)))
        _ndtcap = int(os.environ.get("NDTCAP", str(NDT)))
        for ch in range(_nchcap):
            t0 = ch * TC
            if SS == "a":
                em.v(lambda e: e.memset(yfin[:], 0.0), writes=[P+"yfin"])
            for n in (range(NS) if SS != "z" else []):
                for half in range(2):
                    row = n if half == 0 else NS + n
                    dst = bcB if half == 0 else bcC
                    kd = (P + "bcB") if half == 0 else (P + "bcC")
                    em.t(lambda e, row=row, t0=t0: e.matmul(
                        ps_b[:, :], cst.sel32[:, row * 128:(row + 1) * 128],
                        bmcm[:, t0:t0 + TC], start=True, stop=True),
                        reads=["sel32", P+"bmcm"], writes=[P+"psb"], tag="bc")
                    em.v(lambda e, dst=dst, n=n: e.tensor_copy(dst[:, n, :], ps_b[:, :]),
                         psreads=[P+"psb"], writes=[kd])
            DOPS = os.environ.get("DTOPS", "m12345uyrf")
            for dt_i in (range(_ndtcap) if SS not in ("a", "z") else []):
              if "m" in DOPS:
                em.t(lambda e, dt_i=dt_i, t0=t0: e.matmul(
                    ps_a[:, :], dtw_b[:, dt_i * 128:(dt_i + 1) * 128], dtfm[:, t0:t0 + TC],
                    start=True, stop=True),
                    reads=[P+"dtw", P+"dtfm"], writes=[P+"psa"], tag="dt")
              if "1" in DOPS:
                em.v(lambda e, dt_i=dt_i: e.tensor_scalar(
                    sp1[:], ps_a[:, :], dtb_s[:, dt_i:dt_i + 1], 0.0, ALU.add, ALU.max),
                    reads=[P+"dtb"], psreads=[P+"psa"], writes=[P+"sp1"])
              if "2" in DOPS:
                em.sc(lambda e, dt_i=dt_i: e.activation(out=spt[:], in_=ps_a[:, :], func=AF.Abs,
                                                        bias=dtb_s[:, dt_i:dt_i + 1]),
                      reads=[P+"dtb"], psreads=[P+"psa"], writes=[P+"spt"])
              if "A" in DOPS:
                em.sc(lambda e: e.activation(out=spt[:], in_=ps_a[:, :], func=AF.Abs),
                      psreads=[P+"psa"], writes=[P+"spt"])
              if "B" in DOPS:
                em.sc(lambda e, dt_i=dt_i: e.activation(out=spt[:], in_=ps_a[:, :], func=AF.Identity,
                                                        bias=dtb_s[:, dt_i:dt_i + 1]),
                      reads=[P+"dtb"], psreads=[P+"psa"], writes=[P+"spt"])
              if "C" in DOPS:
                em.sc(lambda e, dt_i=dt_i: e.activation(out=spt[:], in_=sp1[:], func=AF.Abs,
                                                        bias=dtb_s[:, dt_i:dt_i + 1]),
                      reads=[P+"sp1", P+"dtb"], writes=[P+"spt"])
              if "3" in DOPS:
                em.sc(lambda e: e.activation(out=spt[:], in_=spt[:], func=AF.Exp, scale=-1.0),
                      reads=[P+"spt"], writes=[P+"spt"])
              if "4" in DOPS:
                em.sc(lambda e: e.activation(out=spt[:], in_=spt[:], func=AF.Ln, bias=1.0),
                      reads=[P+"spt"], writes=[P+"spt"])
              if "5" in DOPS:
                em.v(lambda e: e.tensor_add(delta_t[:], sp1[:], spt[:]),
                     reads=[P+"sp1", P+"spt"], writes=[P+"delta"])
              iu = dt_i % 2
              if "u" in DOPS:
                ku = f"{P}uch{iu}"
                em.dma("sync", uch[iu][:], u_dram[dt_i, :, t0:t0 + TC], writes=[ku], tag="p3u")
                em.v(lambda e, iu=iu: e.tensor_mul(du_t[:], delta_t[:], uch[iu][:]),
                     reads=[P+"delta", ku], writes=[P+"du"])
                em.v(lambda e, iu=iu, dt_i=dt_i: e.tensor_scalar_mul(uDb_t[:], uch[iu][:], dskip_s[:, dt_i:dt_i + 1]),
                     reads=[ku, P+"dskip"], writes=[P+"uDb"])
              if "y" in DOPS:
                em.t(lambda e, _st=(SS == "b"): e.matmul(ps_y[:, :], cst.ident_b[:], uDb_t[:], start=True, stop=_st),
                     reads=["identb", P+"uDb"], writes=[P+"psy"], tag="yacc")
              if True:
                for n in (range(NS) if SS != "b" else []):
                    j = n % 2
                    col = dt_i * NS + n
                    em.sc(lambda e, j=j, col=col: e.activation(
                        out=dA[j][:], in_=delta_t[:], func=AF.Exp, scale=acol_s[:, col:col + 1]),
                        reads=[P+"delta", P+"acol"], writes=[f"{P}dA{j}"])
                    em.v(lambda e, j=j, n=n: e.tensor_mul(dBu[j][:], du_t[:], bcB[:, n, :]),
                         reads=[P+"du", P+"bcB"], writes=[f"{P}dBu{j}"])
                    if SS == "c":
                        em.v(lambda e, j=j: e.tensor_copy(h_t[j][:], dBu[j][:]),
                             reads=[f"{P}dBu{j}"], writes=[f"{P}h{j}"])
                    else:
                        em.v(lambda e, j=j, col=col: e.tensor_tensor_scan(
                            h_t[j][:], dA[j][:], dBu[j][:], state[:, col:col + 1], ALU.mult, ALU.add),
                            reads=[f"{P}dA{j}", f"{P}dBu{j}", P+"state"], writes=[f"{P}h{j}"])
                    em.sc(lambda e, j=j, col=col: e.copy(state[:, col:col + 1], h_t[j][:, TC - 1:TC]),
                          reads=[f"{P}h{j}"], writes=[P+"state"])
                    em.g(lambda e, j=j, n=n: e.tensor_mul(tmpb[j][:], h_t[j][:], bcC[:, n, :]),
                         reads=[f"{P}h{j}", P+"bcC"], writes=[f"{P}tmpb{j}"])
                    em.t(lambda e, j=j, n=n: e.matmul(
                        ps_y[:, :], cst.ident_b[:], tmpb[j][:], start=False, stop=(n == NS - 1)),
                        reads=["identb", f"{P}tmpb{j}"], writes=[P+"psy"], tag="yacc")
              isr = dt_i % 2
              ksr = f"{P}sresc{isr}"
              if "r" in DOPS:
                em.dma("sync", sresc[isr][:], sres_dram[dt_i, :, t0:t0 + TC], writes=[ksr], tag="p3s")
              if "f" in DOPS:
                em.v(lambda e, isr=isr, dt_i=dt_i: e.tensor_mul(yfin[:, dt_i, :], ps_y[:, :], sresc[isr][:]),
                     reads=[ksr], psreads=[P+"psy"], writes=[P+"yfin"])
            for q in range(TC // 128):
                iq = q % 2
                rows = slice(t0 + q * 128, t0 + (q + 1) * 128)
                em.dma("sync", xres[iq][:], x_in[rows, :], writes=[f"{P}xres{iq}"], tag="p3x")
                for half in range(2):
                    for dt_i in range(NDT):
                        em.t(lambda e, q=q, dt_i=dt_i, half=half: e.matmul(
                            ps_o[half][:, :],
                            yfin[:, dt_i, q * 128:(q + 1) * 128],
                            opw_b[:, dt_i, half * 512:(half + 1) * 512],
                            start=(dt_i == 0), stop=(dt_i == NDT - 1)),
                            reads=[P+"yfin", P+"opw"], writes=[f"{P}pso{half}"], tag="op")
                for half in range(2):
                    em.v(lambda e, iq=iq, half=half: e.tensor_add(
                        s_t[iq][:, half * 512:(half + 1) * 512], ps_o[half][:, :],
                        xres[iq][:, half * 512:(half + 1) * 512]),
                        reads=[f"{P}xres{iq}"], psreads=[f"{P}pso{half}"], writes=[f"{P}st{iq}"])
                emit_ln(em, P, s_t[iq], f"{P}st{iq}", sq_t, red, outt[iq], f"{P}outt{iq}")
                em.dma("scalar", x_out[rows, :], outt[iq][:], reads=[f"{P}outt{iq}"], tag="p3o")
        em.barrier()




T = 4096
D = 1024
E = 8
K = 2
H = 4096
CAP = 640
NTT = T // 128
NST = E * CAP // 128
BIGP = 1.0e6
EPS = 1e-5


def l0f_host_prep(w, pref=""):
    b1 = w["b1"].reshape(E, 32, 128).transpose(2, 0, 1).copy()
    d = {
        "wr": np.ascontiguousarray(w["Wr"]),
        "w1": np.ascontiguousarray(w["W1"]),
        "b1": b1,
        "w2": np.ascontiguousarray(w["W2"]),
        "b2": np.ascontiguousarray(w["b2"]),
    }
    return {pref + k: v for k, v in d.items()}


def host_consts2():
    lstrict = np.tril(np.ones((128, 128), np.float32), -1).T
    return {"lstrict": lstrict.astype(ml_dtypes.bfloat16)}


def declare_consts2(nc):
    return {"lstrict": nc.dram_tensor("lstrict", [128, 128], BF16, kind="ExternalInput").ap()}


def l0f_declare_dram(nc, pref=""):
    d = {}
    def inp(name, shape, dt=F32):
        d[name] = nc.dram_tensor(pref + name, shape, dt, kind="ExternalInput").ap()
    inp("wr", [D, E])
    inp("w1", [E, D, H])
    inp("b1", [128, E, 32])
    inp("w2", [E, H, D])
    inp("b2", [E, D])
    return d


def l0f_emit(em, nc, ctx, cst, cst2, dw, x_in, x_out, pref="l0f"):
    P = pref
    def mksb(c):
        def sb(name, shape, dtype=F32):
            return c.enter_context(nc.sbuf_tensor(P + "sb_" + name, shape, dtype))
        return sb
    def mkps(c):
        def ps(name, shape, dtype=F32):
            return c.enter_context(nc.psum_tensor(P + "pp_" + name, shape, dtype))
        return ps
    sbp = mksb(ctx)

    slot_dram = nc.dram_tensor(P + "slot_scr", [E * CAP, 1], U32, kind="Internal").ap()
    y_dram = nc.dram_tensor(P + "y_scr", [E * CAP, D], BF16, kind="Internal").ap()

    lstr = sbp("lstr", [128, 128], BF16)
    wr_s = sbp("wr", [128, 8, E], F32)
    b1_s = sbp("b1", [128, E, 32], F32)
    addr_sb = sbp("addr", [128, NTT, K], U32)
    pbuf = sbp("pbuf", [128, NTT, K], F32)
    io8 = sbp("io8", [128, E], F32)
    io8B = sbp("io8B", [128, E], F32)
    tok_sb = sbp("tok", [128, NTT], U32)
    onesf1 = sbp("onesf1", [1, 128], F32)
    onesb1 = sbp("onesb1", [128, 1], BF16)
    onesbr = sbp("onesbr", [1, 128], BF16)
    red = sbp("red", [128, 4], F32)
    sq_t = sbp("sq", [128, D], F32)

    em.dma("sync", lstr[:], cst2["lstrict"][:, :], writes=[P+"lstr"])
    em.dma("sync", wr_s[:], dw["wr"].rearrange("(c p) m -> p c m", p=128), writes=[P+"wr"])
    em.dma("sync", b1_s[:], dw["b1"][:, :, :], writes=[P+"b1"])
    em.g(lambda e: e.iota(io8[:], pattern=[[1, E]], base=0, channel_multiplier=0,
                          allow_small_or_imprecise_dtypes=True), writes=[P+"io8"])
    em.g(lambda e: e.iota(io8B[:], pattern=[[1, E]], base=int(BIGP), channel_multiplier=0,
                          allow_small_or_imprecise_dtypes=True), writes=[P+"io8B"])
    em.g(lambda e: e.iota(tok_sb[:], pattern=[[128, NTT]], base=0, channel_multiplier=1),
         writes=[P+"tok"])
    bc_slot = nc.gpsimd.alloc_register(P + "bcslot")
    bc_tok = nc.gpsimd.alloc_register(P + "bctok")
    em.raw("gpsimd", lambda e: e.reg_mov(bc_slot, E * CAP - 1))
    em.raw("gpsimd", lambda e: e.reg_mov(bc_tok, T - 1))
    em.v(lambda e: e.memset(onesf1[:], 1.0), writes=[P+"onesf1"])
    em.v(lambda e: e.memset(onesb1[:], 1.0), writes=[P+"onesb1"])
    em.v(lambda e: e.memset(onesbr[:], 1.0), writes=[P+"onesbr"])

    ctxR = contextlib.ExitStack()
    sbr = mksb(ctxR)
    with ctxR:
        psr = mkps(ctxR)
        ps_log = psr("pslog", [128, E])
        ps_cum = psr("pscum", [128, E])
        ps_row = psr("psrow", [1, E])
        ps_car = psr("pscar", [128, E])
        xld = [sbr(f"xld{i}", [128, D], F32) for i in range(2)]
        xfm = sbr("xfm", [128, 8, 128], F32)
        pstrf = [psr(f"pstrf{i}", [128, 128]) for i in range(2)]
        probs = sbr("probs", [128, E])
        prow = sbr("prow", [128, E])
        scr = sbr("scr", [128, E])
        eqf = [sbr(f"eqf{i}", [128, E]) for i in range(2)]
        rsum2 = sbr("rsum2", [128, E], BF16)
        carry = sbr("carry", [128, E])
        posx = sbr("posx", [128, E])
        row_sb = sbr("rowsb", [1, E])
        cols = sbr("cols", [128, 12])
        addrf = sbr("addrf", [128, 2])
        em.v(lambda e: e.memset(carry[:], 0.0), writes=[P+"carry"])

        for tt in range(NTT):
            i = tt % 2
            kxl = f"{P}xld{i}"
            em.dma("sync", xld[i][:], x_in[tt * 128:(tt + 1) * 128, :], writes=[kxl])
            for dc in range(8):
                sl = dc % 2
                em.t(lambda e, i=i, dc=dc, sl=sl: e.transpose(
                    pstrf[sl][:, :], xld[i][:, dc * 128:(dc + 1) * 128], cst.ident_f[:]),
                    reads=[kxl, "identf"], writes=[f"{P}pstrf{sl}"])
                em.v(lambda e, dc=dc, sl=sl: e.tensor_copy(xfm[:, dc, :], pstrf[sl][:, :]),
                     psreads=[f"{P}pstrf{sl}"], writes=[P+"xfm"])
            for dc in range(8):
                em.t(lambda e, dc=dc: e.matmul(
                    ps_log[:, :], xfm[:, dc, :], wr_s[:, dc, :],
                    start=(dc == 0), stop=(dc == 7)),
                    reads=[P+"xfm", P+"wr"], writes=[P+"pslog"], tag="router")
            em.v(lambda e: e.reduce_max(cols[:, 0:1], ps_log[:, :], axis=AX.X),
                 psreads=[P+"pslog"], writes=[P+"cols"])
            em.v(lambda e: e.tensor_scalar_mul(cols[:, 0:1], cols[:, 0:1], -1.0),
                 reads=[P+"cols"], writes=[P+"cols"])
            em.sc(lambda e: e.activation(out=prow[:], in_=ps_log[:, :], func=AF.Exp,
                                         bias=cols[:, 0:1], accum_out=cols[:, 9:10]),
                  reads=[P+"cols"], psreads=[P+"pslog"], writes=[P+"prow", P+"cols"])
            em.v(lambda e: e.reciprocal(cols[:, 10:11], cols[:, 9:10]),
                 reads=[P+"cols"], writes=[P+"cols"])
            em.v(lambda e: e.tensor_scalar_mul(probs[:], prow[:], cols[:, 10:11]),
                 reads=[P+"prow", P+"cols"], writes=[P+"probs"])
            em.v(lambda e: e.reduce_max(cols[:, 3:4], probs[:], axis=AX.X),
                 reads=[P+"probs"], writes=[P+"cols"])
            em.v(lambda e: e.tensor_single_scalar(scr[:], probs[:], cols[:, 3:4], ALU.is_equal),
                 reads=[P+"probs", P+"cols"], writes=[P+"scr"])
            em.v(lambda e: e.scalar_tensor_tensor(scr[:], scr[:], -BIGP, io8B[:], ALU.mult, ALU.add),
                 reads=[P+"scr", P+"io8B"], writes=[P+"scr"])
            em.v(lambda e: e.tensor_reduce(cols[:, 1:2], scr[:], axis=AX.X, op=ALU.min),
                 reads=[P+"scr"], writes=[P+"cols"])
            em.v(lambda e: e.tensor_single_scalar(eqf[0][:], io8[:], cols[:, 1:2], ALU.is_equal),
                 reads=[P+"io8", P+"cols"], writes=[P+"eqf0"])
            em.v(lambda e: e.scalar_tensor_tensor(prow[:], eqf[0][:], -2.0, probs[:], ALU.mult, ALU.add),
                 reads=[P+"eqf0", P+"probs"], writes=[P+"prow"])
            em.v(lambda e: e.reduce_max(cols[:, 4:5], prow[:], axis=AX.X),
                 reads=[P+"prow"], writes=[P+"cols"])
            em.v(lambda e: e.tensor_single_scalar(scr[:], prow[:], cols[:, 4:5], ALU.is_equal),
                 reads=[P+"prow", P+"cols"], writes=[P+"scr"])
            em.v(lambda e: e.scalar_tensor_tensor(scr[:], scr[:], -BIGP, io8B[:], ALU.mult, ALU.add),
                 reads=[P+"scr", P+"io8B"], writes=[P+"scr"])
            em.v(lambda e: e.tensor_reduce(cols[:, 2:3], scr[:], axis=AX.X, op=ALU.min),
                 reads=[P+"scr"], writes=[P+"cols"])
            em.v(lambda e: e.tensor_single_scalar(eqf[1][:], io8[:], cols[:, 2:3], ALU.is_equal),
                 reads=[P+"io8", P+"cols"], writes=[P+"eqf1"])
            em.v(lambda e, tt=tt: e.tensor_copy(pbuf[:, tt, :], cols[:, 3:5]),
                 reads=[P+"cols"], writes=[P+"pbuf"])
            em.v(lambda e: e.tensor_add(rsum2[:], eqf[0][:], eqf[1][:]),
                 reads=[P+"eqf0", P+"eqf1"], writes=[P+"rsum2"])
            em.t(lambda e: e.matmul(ps_cum[:, :], lstr[:], rsum2[:], start=True, stop=True),
                 reads=[P+"lstr", P+"rsum2"], writes=[P+"pscum"], tag="cum")
            em.v(lambda e: e.tensor_add(posx[:], ps_cum[:, :], carry[:]),
                 reads=[P+"carry"], psreads=[P+"pscum"], writes=[P+"posx"])
            em.t(lambda e: e.matmul(ps_row[:, :], onesb1[:], rsum2[:], start=True, stop=True),
                 reads=[P+"onesb1", P+"rsum2"], writes=[P+"psrow"], tag="rowsum")
            em.v(lambda e: e.tensor_copy(row_sb[:], ps_row[:, :]),
                 psreads=[P+"psrow"], writes=[P+"rowsb"])
            em.t(lambda e: e.matmul(ps_car[:, :], onesf1[:], row_sb[:], start=True, stop=True),
                 reads=[P+"onesf1", P+"rowsb"], writes=[P+"pscar"], tag="carbc")
            em.v(lambda e: e.tensor_add(carry[:], carry[:], ps_car[:, :]),
                 reads=[P+"carry"], psreads=[P+"pscar"], writes=[P+"carry"])
            for k in range(K):
                em.v(lambda e, k=k: e.tensor_mul(scr[:], posx[:], eqf[k][:]),
                     reads=[P+"posx", f"{P}eqf{k}"], writes=[P+"scr"])
                em.v(lambda e, k=k: e.reduce_sum(cols[:, 5 + k:6 + k], scr[:], axis=AX.X),
                     reads=[P+"scr"], writes=[P+"cols"])
                em.v(lambda e, k=k: e.scalar_tensor_tensor(
                    addrf[:, k:k + 1], cols[:, 1 + k:2 + k], float(CAP), cols[:, 5 + k:6 + k],
                    ALU.mult, ALU.add),
                    reads=[P+"cols"], writes=[P+"addrf"])
                em.v(lambda e, k=k: e.tensor_scalar(
                    cols[:, 9:10], cols[:, 5 + k:6 + k], float(CAP), 8192.0, ALU.is_ge, ALU.mult),
                    reads=[P+"cols"], writes=[P+"cols"])
                em.v(lambda e, k=k: e.tensor_add(addrf[:, k:k + 1], addrf[:, k:k + 1], cols[:, 9:10]),
                     reads=[P+"cols", P+"addrf"], writes=[P+"addrf"])
            em.v(lambda e, tt=tt: e.tensor_copy(addr_sb[:, tt, :], addrf[:]),
                 reads=[P+"addrf"], writes=[P+"addr"])
        bigt = sbr("bigt", [128, 40], U32)
        em.v(lambda e: e.memset(bigt[:], 1 << 20), writes=[P+"bigt"])
        em.dma("gpsimd", slot_dram.rearrange("(a b) c -> a (b c)", a=128), bigt[:],
               reads=[P+"bigt"], writes=[P+"slotd"], tag="slotinit")
        for tt in range(NTT):
            for k in range(K):
                em.op("gpsimd", lambda e, tt=tt, k=k: e.indirect_dma_start(
                    out=slot_dram[:, :],
                    out_offset=bass.IndirectOffsetOnAxis(addr_sb[:, tt, k:k + 1], 0),
                    in_=tok_sb[:, tt:tt + 1],
                    in_offset=None,
                    bounds_check=bc_slot,
                    oob_is_err=False),
                    reads=[P+"addr", P+"tok", P+"slotd"], writes=[P+"slotd"],
                    dma=True, tag="scat")
        em.barrier()

    ctxF = contextlib.ExitStack()
    sbf = mksb(ctxF)
    with ctxF:
        psf = mkps(ctxF)
        ps_h = [psf(f"psh{i}", [128, 320]) for i in range(2)]
        ps_y = [psf(f"psy{i}", [128, 512]) for i in range(2)]
        ps_tr = [psf(f"pstr{i}", [128, 128], BF16) for i in range(2)]
        bufE = sbf("bufE", [128, 8, CAP], BF16)
        hfm = sbf("hfm", [128, 32, CAP], BF16)
        w1t = [sbf(f"w1t{i}", [128, 8, 1024], BF16) for i in range(2)]
        w2t = sbf("w2t", [128, 32, 1024], BF16)
        slot_ids = sbf("slotids", [128, 5], U32)
        gstag = [sbf(f"gst{i}", [128, D], BF16) for i in range(2)]
        b2row = sbf("b2row", [1, D], F32)
        b2hi = sbf("b2hi", [1, D], BF16)
        b2lo = sbf("b2lo", [1, D], BF16)
        b2lof = sbf("b2lof", [1, D], F32)
        ytile = sbf("ytile", [128, D], BF16)

        for e_i in range(E):
            for st in range(5):
                em.dma("sync", slot_ids[:, st:st + 1],
                       slot_dram[e_i * CAP + st * 128:e_i * CAP + (st + 1) * 128, :],
                       reads=[P+"slotd"], writes=[P+"slotids"], tag="sids")
            em.dma("sync", b2row[:], dw["b2"][e_i:e_i + 1, :], writes=[P+"b2row"])
            em.v(lambda e: e.tensor_copy(b2hi[:], b2row[:]), reads=[P+"b2row"], writes=[P+"b2hi"])
            em.v(lambda e: e.tensor_copy(b2lof[:], b2hi[:]), reads=[P+"b2hi"], writes=[P+"b2lof"])
            em.v(lambda e: e.tensor_sub(b2lof[:], b2row[:], b2lof[:]),
                 reads=[P+"b2row", P+"b2lof"], writes=[P+"b2lof"])
            em.v(lambda e: e.tensor_copy(b2lo[:], b2lof[:]), reads=[P+"b2lof"], writes=[P+"b2lo"])
            for st in range(5):
                i = st % 2
                kg = f"{P}gst{i}"
                em.v(lambda e, i=i: e.memset(gstag[i][:], 0.0), writes=[kg])
                em.op("gpsimd", lambda e, st=st, i=i: e.indirect_dma_start(
                    out=gstag[i][:, :],
                    out_offset=None,
                    in_=x_in[:, :],
                    in_offset=bass.IndirectOffsetOnAxis(slot_ids[:, st:st + 1], 0),
                    bounds_check=bc_tok,
                    oob_is_err=False),
                    reads=[P+"slotids", kg], writes=[kg], dma=True, tag="gath")
                for dc in range(8):
                    sl = dc % 2
                    em.t(lambda e, i=i, dc=dc, sl=sl: e.transpose(
                        ps_tr[sl][:, :], gstag[i][:, dc * 128:(dc + 1) * 128], cst.ident_b[:]),
                        reads=[kg, "identb"], writes=[f"{P}pstr{sl}"])
                    em.sc(lambda e, st=st, dc=dc, sl=sl: e.copy(
                        bufE[:, dc, st * 128:(st + 1) * 128], ps_tr[sl][:, :]),
                        psreads=[f"{P}pstr{sl}"], writes=[P+"bufE"])
            for q in range(4):
                iw = q % 2
                kw = f"{P}w1t{iw}"
                em.dma("gpsimd", w1t[iw][:],
                       dw["w1"][e_i, :, q * 1024:(q + 1) * 1024].rearrange("(c p) m -> p c m", p=128),
                       writes=[kw], tag="w1")
                for hh in range(8):
                    ht = q * 8 + hh
                    for w in range(2):
                        jp = (ht * 2 + w) % 2
                        kp = f"{P}psh{jp}"
                        for dc in range(8):
                            em.t(lambda e, iw=iw, hh=hh, w=w, dc=dc, jp=jp: e.matmul(
                                ps_h[jp][:, :], w1t[iw][:, dc, hh * 128:(hh + 1) * 128],
                                bufE[:, dc, w * 320:(w + 1) * 320],
                                start=(dc == 0), stop=(dc == 7)),
                                reads=[kw, P+"bufE"], writes=[kp], tag="w1mm")
                        em.sc(lambda e, ht=ht, w=w, e_i=e_i, jp=jp: e.activation(
                            out=hfm[:, ht, w * 320:(w + 1) * 320], in_=ps_h[jp][:, :], func=AF.Relu,
                            bias=b1_s[:, e_i, ht:ht + 1]),
                            reads=[P+"b1"], psreads=[kp], writes=[P+"hfm"])
            for q in range(4):
                em.dma("gpsimd", w2t[:, q * 8:(q + 1) * 8, :],
                       dw["w2"][e_i, q * 1024:(q + 1) * 1024, :].rearrange("(c p) m -> p c m", p=128),
                       writes=[P+"w2t"], tag="w2")
            for st in range(5):
                for cw in range(2):
                    jp = (st * 2 + cw) % 2
                    kp = f"{P}psy{jp}"
                    em.t(lambda e, cw=cw, jp=jp: e.matmul(
                        ps_y[jp][:, :], onesbr[:], b2hi[:, cw * 512:(cw + 1) * 512],
                        start=True, stop=False),
                        reads=[P+"onesbr", P+"b2hi"], writes=[kp], tag="b2a")
                    em.t(lambda e, cw=cw, jp=jp: e.matmul(
                        ps_y[jp][:, :], onesbr[:], b2lo[:, cw * 512:(cw + 1) * 512],
                        start=False, stop=False),
                        reads=[P+"onesbr", P+"b2lo"], writes=[kp], tag="b2b")
                    for hc in range(32):
                        em.t(lambda e, st=st, cw=cw, hc=hc, jp=jp: e.matmul(
                            ps_y[jp][:, :], hfm[:, hc, st * 128:(st + 1) * 128],
                            w2t[:, hc, cw * 512:(cw + 1) * 512],
                            start=False, stop=(hc == 31)),
                            reads=[P+"hfm", P+"w2t"], writes=[kp], tag="w2mm")
                    em.sc(lambda e, cw=cw, jp=jp: e.copy(
                        ytile[:, cw * 512:(cw + 1) * 512], ps_y[jp][:, :]),
                        psreads=[kp], writes=[P+"ytile"])
                em.dma("scalar", y_dram[e_i * CAP + st * 128:e_i * CAP + (st + 1) * 128, :],
                       ytile[:], reads=[P+"ytile"], writes=[P+"ydram"], tag="yst")
        em.barrier()

    ctxC = contextlib.ExitStack()
    sbc = mksb(ctxC)
    with ctxC:
        ybt = [sbc(f"ybt{i}", [128, D], BF16) for i in range(2)]
        xres = [sbc(f"xres{i}", [128, D], F32) for i in range(2)]
        s_t = [sbc(f"sc{i}", [128, D], F32) for i in range(2)]
        outt = [sbc(f"outc{i}", [128, D], F32) for i in range(2)]
        for tt in range(NTT):
            iq = tt % 2
            rows = slice(tt * 128, (tt + 1) * 128)
            em.dma("sync", xres[iq][:], x_in[rows, :], writes=[f"{P}xres{iq}"])
            em.v(lambda e, iq=iq: e.tensor_copy(s_t[iq][:], xres[iq][:]),
                 reads=[f"{P}xres{iq}"], writes=[f"{P}sc{iq}"])
            for k in range(K):
                em.v(lambda e, iq=iq: e.memset(ybt[iq][:], 0.0), writes=[f"{P}ybt{iq}"])
                em.op("gpsimd", lambda e, tt=tt, k=k, iq=iq: e.indirect_dma_start(
                    out=ybt[iq][:, :],
                    out_offset=None,
                    in_=y_dram[:, :],
                    in_offset=bass.IndirectOffsetOnAxis(addr_sb[:, tt, k:k + 1], 0),
                    bounds_check=bc_slot,
                    oob_is_err=False),
                    reads=[P+"addr", P+"ydram", f"{P}ybt{iq}"], writes=[f"{P}ybt{iq}"],
                    dma=True, tag="gathy")
                em.v(lambda e, tt=tt, k=k, iq=iq: e.scalar_tensor_tensor(
                    s_t[iq][:], ybt[iq][:], pbuf[:, tt, k:k + 1], s_t[iq][:], ALU.mult, ALU.add),
                    reads=[f"{P}ybt{iq}", P+"pbuf", f"{P}sc{iq}"], writes=[f"{P}sc{iq}"])
            emit_ln(em, P, s_t[iq], f"{P}sc{iq}", sq_t, red, outt[iq], f"{P}outc{iq}")
            em.dma("scalar", x_out[rows, :], outt[iq][:], reads=[f"{P}outc{iq}"], tag="xout")
        em.barrier()




T = 4096
D = 1024
NH = 16
HD = 64
W2 = 128
NTT = T // 128
EPS = 1e-5
NEG = -30000.0


def l1a_host_prep(w, pref=""):
    d = {
        "wqkT": np.ascontiguousarray(w["w_qkv"][0:2048, :].T),
        "wvT": np.ascontiguousarray(w["w_qkv"][2048:3072, :].T),
        "woT": np.ascontiguousarray(w["w_out"].T),
        "gamma": w["gamma"].reshape(1, D),
        "beta": w["beta"].reshape(1, D),
    }
    return {pref + k: v for k, v in d.items()}


def host_consts3():
    p = np.arange(128)[:, None]
    c = np.arange(384)[None, :]
    mi = np.where(np.abs(c - 128 - p) <= W2, 0.0, NEG).astype(np.float32)
    c2 = np.arange(256)[None, :]
    mf = np.where(c2 - p <= W2, 0.0, NEG).astype(np.float32)
    ml = np.where(c2 >= p, 0.0, NEG).astype(np.float32)
    return {"maski": mi, "maskf": mf, "maskl": ml}


def declare_consts3(nc):
    return {
        "maski": nc.dram_tensor("maski", [128, 384], F32, kind="ExternalInput").ap(),
        "maskf": nc.dram_tensor("maskf", [128, 256], F32, kind="ExternalInput").ap(),
        "maskl": nc.dram_tensor("maskl", [128, 256], F32, kind="ExternalInput").ap(),
    }


def l1a_declare_dram(nc, pref=""):
    d = {}
    def inp(name, shape, dt=F32):
        d[name] = nc.dram_tensor(pref + name, shape, dt, kind="ExternalInput").ap()
    inp("wqkT", [D, 2048])
    inp("wvT", [D, D])
    inp("woT", [D, D])
    inp("gamma", [1, D])
    inp("beta", [1, D])
    return d


def l1a_emit(em, nc, ctx, cst, cst3, dw, x_in, x_out, pref="l1a"):
    P = pref
    def mksb(c):
        def sb(name, shape, dtype=F32):
            return c.enter_context(nc.sbuf_tensor(P + "sb_" + name, shape, dtype))
        return sb
    def mkps(c):
        def ps(name, shape, dtype=F32):
            return c.enter_context(nc.psum_tensor(P + "pp_" + name, shape, dtype))
        return ps
    sbp = mksb(ctx)

    xn_dram = nc.dram_tensor(P + "xn_scr", [T, D], BF16, kind="Internal").ap()

    wqk_b = sbp("wqk", [128, 8, 2048], BF16)
    wv_b = sbp("wv", [128, 8, D], BF16)
    wo_b = sbp("wo", [128, 8, D], BF16)
    gbc = sbp("gbc", [128, D])
    bbc = sbp("bbc", [128, D])
    grow = sbp("grow", [1, D])
    brow = sbp("brow", [1, D])
    onesf1 = sbp("onesf1", [1, 128], F32)
    maski = sbp("maski", [128, 384])
    maskf = sbp("maskf", [128, 256])
    maskl = sbp("maskl", [128, 256])
    red = sbp("red", [128, 4], F32)
    sq_t = sbp("sq", [128, D], F32)

    em.dma("gpsimd", wqk_b[:], dw["wqkT"].rearrange("(c p) m -> p c m", p=128), writes=[P+"wqk"])
    em.dma("gpsimd", wv_b[:], dw["wvT"].rearrange("(c p) m -> p c m", p=128), writes=[P+"wv"])
    em.dma("gpsimd", wo_b[:], dw["woT"].rearrange("(c p) m -> p c m", p=128), writes=[P+"wo"])
    em.dma("sync", grow[:], dw["gamma"][:, :], writes=[P+"grow"])
    em.dma("sync", brow[:], dw["beta"][:, :], writes=[P+"brow"])
    em.dma("sync", maski[:], cst3["maski"][:, :], writes=[P+"maski"])
    em.dma("sync", maskf[:], cst3["maskf"][:, :], writes=[P+"maskf"])
    em.dma("sync", maskl[:], cst3["maskl"][:, :], writes=[P+"maskl"])
    em.v(lambda e: e.memset(onesf1[:], 1.0), writes=[P+"onesf1"])

    ctxN = contextlib.ExitStack()
    sbn = mksb(ctxN)
    with ctxN:
        psn = mkps(ctxN)
        ps_g = psn("psg", [128, 512])
        ps_g2 = psn("psg2", [128, 512])
        xld = [sbn(f"xld{i}", [128, D], F32) for i in range(2)]
        xnb = [sbn(f"xnb{i}", [128, D], BF16) for i in range(2)]
        tmp = sbn("tmp", [128, D], F32)
        for half in range(2):
            row = grow if half == 0 else brow
            krow = P + ("grow" if half == 0 else "brow")
            dst = gbc if half == 0 else bbc
            kdst = P + ("gbc" if half == 0 else "bbc")
            for hw in range(2):
                pg = ps_g if hw == 0 else ps_g2
                kpg = P + ("psg0" if hw == 0 else "psg1")
                em.t(lambda e, pg=pg, row=row, hw=hw: e.matmul(
                    pg[:, :], onesf1[:], row[:, hw * 512:(hw + 1) * 512],
                    start=True, stop=True),
                    reads=[P+"onesf1", krow], writes=[kpg], tag="gb")
                em.v(lambda e, dst=dst, hw=hw, pg=pg: e.tensor_copy(
                    dst[:, hw * 512:(hw + 1) * 512], pg[:, :]),
                    psreads=[kpg], writes=[kdst])
        for tt in range(NTT):
            i = tt % 2
            kxl = f"{P}xld{i}"
            rows = slice(tt * 128, (tt + 1) * 128)
            em.dma("sync", xld[i][:], x_in[rows, :], writes=[kxl])
            em.v(lambda e, i=i: e.reduce_sum(red[:, 0:1], xld[i][:], axis=AX.X),
                 reads=[kxl], writes=[P+"red"])
            em.v(lambda e: e.tensor_scalar_mul(red[:, 1:2], red[:, 0:1], -1.0 / D),
                 reads=[P+"red"], writes=[P+"red"])
            em.v(lambda e, i=i: e.tensor_scalar_add(tmp[:], xld[i][:], red[:, 1:2]),
                 reads=[kxl, P+"red"], writes=[P+"tmp"])
            em.sc(lambda e: e.activation(out=sq_t[:], in_=tmp[:], func=AF.Square,
                                         accum_out=red[:, 2:3]),
                  reads=[P+"tmp"], writes=[P+"sq", P+"red"])
            em.v(lambda e: e.tensor_scalar(red[:, 3:4], red[:, 2:3], 1.0 / D, EPS, ALU.mult, ALU.add),
                 reads=[P+"red"], writes=[P+"red"])
            em.sc(lambda e: e.activation(out=red[:, 3:4], in_=red[:, 3:4], func=AF.Ln),
                  reads=[P+"red"], writes=[P+"red"])
            em.sc(lambda e: e.activation(out=red[:, 3:4], in_=red[:, 3:4], func=AF.Exp, scale=-0.5),
                  reads=[P+"red"], writes=[P+"red"])
            em.v(lambda e: e.tensor_scalar_mul(tmp[:], tmp[:], red[:, 3:4]),
                 reads=[P+"tmp", P+"red"], writes=[P+"tmp"])
            em.v(lambda e: e.tensor_mul(tmp[:], tmp[:], gbc[:]),
                 reads=[P+"tmp", P+"gbc"], writes=[P+"tmp"])
            em.v(lambda e, i=i: e.tensor_add(xnb[i][:], tmp[:], bbc[:]),
                 reads=[P+"tmp", P+"bbc"], writes=[f"{P}xnb{i}"])
            em.dma("scalar", xn_dram[rows, :], xnb[i][:], reads=[f"{P}xnb{i}"],
                   writes=[P+"xnd"], tag="xnst")
        em.barrier()

    ctxK = contextlib.ExitStack()
    sbk = mksb(ctxK)
    kfm = ctx.enter_context(nc.sbuf_tensor(P + "sb_kfm", [128, 8, T], BF16))
    with ctxK:
        psk = mkps(ctxK)
        ps_trK = [psk(f"pstr{i}", [128, 128], BF16) for i in range(2)]
        ps_kK = [psk(f"psk{i}", [128, 128]) for i in range(2)]
        xldbK = [sbk(f"xldbK{i}", [128, D], BF16) for i in range(2)]
        xfmtK = sbk("xfmtK", [128, 8, 128], BF16)
        for tt in range(NTT):
            i = tt % 2
            kxl = f"{P}xldbK{i}"
            em.dma("sync", xldbK[i][:], xn_dram[tt * 128:(tt + 1) * 128, :],
                   reads=[P+"xnd"], writes=[kxl])
            for dc in range(8):
                sl = dc % 2
                em.t(lambda e, i=i, dc=dc, sl=sl: e.transpose(
                    ps_trK[sl][:, :], xldbK[i][:, dc * 128:(dc + 1) * 128], cst.ident_b[:]),
                    reads=[kxl, "identb"], writes=[f"{P}pstr{sl}"])
                em.v(lambda e, dc=dc, sl=sl: e.tensor_copy(xfmtK[:, dc, :], ps_trK[sl][:, :]),
                     psreads=[f"{P}pstr{sl}"], writes=[P+"xfmtK"])
            for kf in range(8):
                sl = kf % 2
                kp = f"{P}psk{sl}"
                for dc in range(8):
                    em.t(lambda e, kf=kf, dc=dc, sl=sl: e.matmul(
                        ps_kK[sl][:, :], wqk_b[:, dc, 1024 + kf * 128:1024 + (kf + 1) * 128],
                        xfmtK[:, dc, :], start=(dc == 0), stop=(dc == 7)),
                        reads=[P+"wqk", P+"xfmtK"], writes=[kp], tag="kmm")
                em.sc(lambda e, kf=kf, tt=tt, sl=sl: e.copy(
                    kfm[:, kf, tt * 128:(tt + 1) * 128], ps_kK[sl][:, :]),
                    psreads=[kp], writes=[P+"kfm"])
        em.barrier()

    ctxA = contextlib.ExitStack()
    sba = mksb(ctxA)
    with ctxA:
        psa = mkps(ctxA)
        ps_q = [psa(f"psq{i}", [128, 128]) for i in range(2)]
        ps_v = [psa(f"psv{i}", [128, 512]) for i in range(2)]
        ps_s = psa("pss", [128, 384])
        ps_trA = psa("pstrA", [128, 128], BF16)
        ps_av = psa("psav", [128, 128])
        ps_o = psa("pso", [128, 512])
        xnr = sba("xnr", [128, 2, 8, 128], BF16)
        vring = sba("vring", [128, 3, D], BF16)
        q_sb = sba("qsb", [128, 8, 128], BF16)
        att = sba("att", [128, 384])
        attb = sba("attb", [128, 384], BF16)
        attT = sba("attT", [128, 3, 128], BF16)
        aout = sba("aout", [128, 8, 128], BF16)
        xres = [sba(f"xres{i}", [128, D], F32) for i in range(2)]
        s_t = [sba(f"sa{i}", [128, D], F32) for i in range(2)]
        outt = [sba(f"outa{i}", [128, D], F32) for i in range(2)]
        xldbA = [sba(f"xldc{i}", [128, D], BF16) for i in range(2)]

        def load_xn_tile(tile, slot):
            i = tile % 2
            kxl = f"{P}xldc{i}"
            em.dma("sync", xldbA[i][:], xn_dram[tile * 128:(tile + 1) * 128, :],
                   reads=[P+"xnd"], writes=[kxl])
            for dc in range(8):
                sl = dc % 2
                em.t(lambda e, i=i, dc=dc: e.transpose(
                    ps_trA[:, :], xldbA[i][:, dc * 128:(dc + 1) * 128], cst.ident_b[:]),
                    reads=[kxl, "identb"], writes=[P+"pstrA"])
                em.v(lambda e, dc=dc, slot=slot: e.tensor_copy(xnr[:, slot, dc, :], ps_trA[:, :]),
                     psreads=[P+"pstrA"], writes=[P+"xnr"])

        def compute_v_tile(tile, vslot, xslot):
            for hw in range(2):
                kp = f"{P}psv{hw}"
                for dc in range(8):
                    em.t(lambda e, hw=hw, dc=dc, xslot=xslot: e.matmul(
                        ps_v[hw][:, :], xnr[:, xslot, dc, :], wv_b[:, dc, hw * 512:(hw + 1) * 512],
                        start=(dc == 0), stop=(dc == 7)),
                        reads=[P+"xnr", P+"wv"], writes=[kp], tag="vmm")
                em.sc(lambda e, hw=hw, vslot=vslot: e.copy(
                    vring[:, vslot, hw * 512:(hw + 1) * 512], ps_v[hw][:, :]),
                    psreads=[kp], writes=[P+"vring"])

        load_xn_tile(0, 0)
        compute_v_tile(0, 0, 0)
        load_xn_tile(1, 1)
        compute_v_tile(1, 1, 1)

        for qt in range(NTT):
            if qt + 1 < NTT and qt > 0:
                xslot = (qt + 1) % 2
                load_xn_tile(qt + 1, xslot)
                compute_v_tile(qt + 1, (qt + 1) % 3, xslot)
            for qf in range(8):
                sl = qf % 2
                kp = f"{P}psq{sl}"
                for dc in range(8):
                    em.t(lambda e, qf=qf, dc=dc, sl=sl, qt=qt: e.matmul(
                        ps_q[sl][:, :], wqk_b[:, dc, qf * 128:(qf + 1) * 128],
                        xnr[:, qt % 2, dc, :], start=(dc == 0), stop=(dc == 7)),
                        reads=[P+"wqk", P+"xnr"], writes=[kp], tag="qmm")
                em.sc(lambda e, qf=qf, sl=sl: e.activation(
                    out=q_sb[:, qf, :], in_=ps_q[sl][:, :], func=AF.Copy, scale=0.125),
                    psreads=[kp], writes=[P+"qsb"])
            ks = [t for t in (qt - 1, qt, qt + 1) if 0 <= t < NTT]
            kw = len(ks) * 128
            k0 = ks[0] * 128
            mask = maski if len(ks) == 3 else (maskf if qt == 0 else maskl)
            kmask = P + ("maski" if len(ks) == 3 else ("maskf" if qt == 0 else "maskl"))
            for h in range(NH):
                hp = (h % 2) * 64
                hf = h // 2
                em.t(lambda e, hp=hp, hf=hf, k0=k0, kw=kw: e.matmul(
                    ps_s[:, 0:kw], q_sb[hp:hp + 64, hf, :], kfm[hp:hp + 64, hf, k0:k0 + kw],
                    start=True, stop=True),
                    reads=[P+"qsb", P+"kfm"], writes=[P+"pss"], tag="smm")
                em.v(lambda e, kw=kw, mask=mask: e.tensor_add(att[:, 0:kw], ps_s[:, 0:kw], mask[:, 0:kw]),
                     reads=[kmask], psreads=[P+"pss"], writes=[P+"att"])
                em.v(lambda e, kw=kw: e.reduce_max(red[:, 0:1], att[:, 0:kw], axis=AX.X),
                     reads=[P+"att"], writes=[P+"red"])
                em.v(lambda e: e.tensor_scalar_mul(red[:, 1:2], red[:, 0:1], -1.0),
                     reads=[P+"red"], writes=[P+"red"])
                em.sc(lambda e, kw=kw: e.activation(out=att[:, 0:kw], in_=att[:, 0:kw], func=AF.Exp,
                                                    bias=red[:, 1:2], accum_out=red[:, 2:3]),
                      reads=[P+"att", P+"red"], writes=[P+"att", P+"red"])
                em.v(lambda e: e.reciprocal(red[:, 3:4], red[:, 2:3]),
                     reads=[P+"red"], writes=[P+"red"])
                em.sc(lambda e, kw=kw: e.mul(attb[:, 0:kw], att[:, 0:kw], red[:, 3:4]),
                      reads=[P+"att", P+"red"], writes=[P+"attb"])
                for j in range(len(ks)):
                    em.t(lambda e, j=j: e.transpose(ps_trA[:, :], attb[:, j * 128:(j + 1) * 128],
                                                    cst.ident_b[:]),
                         reads=[P+"attb", "identb"], writes=[P+"pstrA"])
                    em.sc(lambda e, j=j: e.copy(attT[:, j, :], ps_trA[:, :]),
                          psreads=[P+"pstrA"], writes=[P+"attT"])
                for j, kt in enumerate(ks):
                    em.t(lambda e, j=j, kt=kt, hp=hp, hf=hf, nk=len(ks): e.matmul(
                        ps_av[0:64, :], vring[:, kt % 3, (hf * 2 * 64 + hp):(hf * 2 * 64 + hp) + 64],
                        attT[:, j, :], start=(j == 0), stop=(j == nk - 1)),
                        reads=[P+"vring", P+"attT"], writes=[P+"psav"], tag="avmm")
                em.v(lambda e, hp=hp, hf=hf: e.tensor_copy(aout[hp:hp + 64, hf, :], ps_av[0:64, :]),
                     psreads=[P+"psav"], writes=[P+"aout"])
            iq = qt % 2
            rows = slice(qt * 128, (qt + 1) * 128)
            em.dma("sync", xres[iq][:], x_in[rows, :], writes=[f"{P}xres{iq}"])
            for hw in range(2):
                for dc in range(8):
                    em.t(lambda e, hw=hw, dc=dc: e.matmul(
                        ps_o[:, :], aout[:, dc, :], wo_b[:, dc, hw * 512:(hw + 1) * 512],
                        start=(dc == 0), stop=(dc == 7)),
                        reads=[P+"aout", P+"wo"], writes=[P+"pso"], tag="omm")
                em.v(lambda e, iq=iq, hw=hw: e.scalar_tensor_tensor(
                    s_t[iq][:, hw * 512:(hw + 1) * 512], xres[iq][:, hw * 512:(hw + 1) * 512],
                    2.0, ps_o[:, :], ALU.mult, ALU.add),
                    reads=[f"{P}xres{iq}"], psreads=[P+"pso"], writes=[f"{P}sa{iq}"])
            emit_ln(em, P, s_t[iq], f"{P}sa{iq}", sq_t, red, outt[iq], f"{P}outa{iq}")
            em.dma("scalar", x_out[rows, :], outt[iq][:], reads=[f"{P}outa{iq}"], tag="xout")
        em.barrier()




T = 4096
D = 1024
E = 8
HH = 1024
NTT = T // 128
BIGP = 1.0e6
EPS = 1e-5


def l1f_host_prep(w, pref=""):
    d = {
        "wr": np.ascontiguousarray(w["Wr"]),
        "wi": np.ascontiguousarray(w["Wi"]),
        "wo": np.ascontiguousarray(w["Wo"]),
    }
    return {pref + k: v for k, v in d.items()}


def host_consts4():
    sel = np.zeros((8, 8, 128), np.float32)
    for e in range(8):
        sel[e, e, :] = 1.0
    return {"sel8f": sel.reshape(8, 8 * 128)}


def declare_consts4(nc):
    return {"sel8f": nc.dram_tensor("sel8f", [8, 8 * 128], F32, kind="ExternalInput").ap()}


def l1f_declare_dram(nc, pref=""):
    d = {}
    def inp(name, shape, dt=F32):
        d[name] = nc.dram_tensor(pref + name, shape, dt, kind="ExternalInput").ap()
    inp("wr", [D, E])
    inp("wi", [E, D, HH])
    inp("wo", [E, HH, D])
    return d


def l1f_emit(em, nc, ctx, cst, cst4, dw, x_in, x_out, pref="l1f"):
    P = pref
    def mksb(c):
        def sb(name, shape, dtype=F32):
            return c.enter_context(nc.sbuf_tensor(P + "sb_" + name, shape, dtype))
        return sb
    def mkps(c):
        def ps(name, shape, dtype=F32):
            return c.enter_context(nc.psum_tensor(P + "pp_" + name, shape, dtype))
        return ps
    sbp = mksb(ctx)

    xbf_dram = nc.dram_tensor(P + "xbf_scr", [T, D], BF16, kind="Internal").ap()

    wr_s = sbp("wr", [128, 8, E], F32)
    cfm = sbp("cfm", [8, T], F32)
    io8 = sbp("io8", [128, E], F32)
    io8B = sbp("io8B", [128, E], F32)
    sel8f = sbp("sel8f", [8, 8 * 128], F32)
    red = sbp("red", [128, 4], F32)
    sq_t = sbp("sq", [128, D], F32)

    em.dma("sync", wr_s[:], dw["wr"].rearrange("(c p) m -> p c m", p=128), writes=[P+"wr"])
    em.dma("sync", sel8f[:], cst4["sel8f"][:, :], writes=[P+"sel8f"])
    em.g(lambda e: e.iota(io8[:], pattern=[[1, E]], base=0, channel_multiplier=0,
                          allow_small_or_imprecise_dtypes=True), writes=[P+"io8"])
    em.g(lambda e: e.iota(io8B[:], pattern=[[1, E]], base=int(BIGP), channel_multiplier=0,
                          allow_small_or_imprecise_dtypes=True), writes=[P+"io8B"])

    ctxR = contextlib.ExitStack()
    sbr = mksb(ctxR)
    with ctxR:
        psr = mkps(ctxR)
        ps_log = psr("pslog", [128, E])
        ps_ct = psr("psct", [128, 128])
        pstrf = [psr(f"pstrf{i}", [128, 128]) for i in range(2)]
        xld = [sbr(f"xld{i}", [128, D], F32) for i in range(2)]
        xbfs = [sbr(f"xbfs{i}", [128, D], BF16) for i in range(2)]
        xfmR = sbr("xfmR", [128, 8, 128], F32)
        probs = sbr("probs", [128, E])
        prow = sbr("prow", [128, E])
        scr = sbr("scr", [128, E])
        eqf = [sbr(f"eqf{i}", [128, E]) for i in range(2)]
        csb = sbr("csb", [128, E])
        cols = sbr("cols", [128, 12])

        for tt in range(NTT):
            i = tt % 2
            kxl = f"{P}xld{i}"
            rows = slice(tt * 128, (tt + 1) * 128)
            em.dma("sync", xld[i][:], x_in[rows, :], writes=[kxl])
            em.v(lambda e, i=i: e.tensor_copy(xbfs[i][:], xld[i][:]),
                 reads=[kxl], writes=[f"{P}xbfs{i}"])
            em.dma("scalar", xbf_dram[rows, :], xbfs[i][:], reads=[f"{P}xbfs{i}"],
                   writes=[P+"xbfd"], tag="xbst")
            for dc in range(8):
                sl = dc % 2
                em.t(lambda e, i=i, dc=dc, sl=sl: e.transpose(
                    pstrf[sl][:, :], xld[i][:, dc * 128:(dc + 1) * 128], cst.ident_f[:]),
                    reads=[kxl, "identf"], writes=[f"{P}pstrf{sl}"])
                em.v(lambda e, dc=dc, sl=sl: e.tensor_copy(xfmR[:, dc, :], pstrf[sl][:, :]),
                     psreads=[f"{P}pstrf{sl}"], writes=[P+"xfmR"])
            for dc in range(8):
                em.t(lambda e, dc=dc: e.matmul(
                    ps_log[:, :], xfmR[:, dc, :], wr_s[:, dc, :],
                    start=(dc == 0), stop=(dc == 7)),
                    reads=[P+"xfmR", P+"wr"], writes=[P+"pslog"], tag="router")
            em.v(lambda e: e.reduce_max(cols[:, 0:1], ps_log[:, :], axis=AX.X),
                 psreads=[P+"pslog"], writes=[P+"cols"])
            em.v(lambda e: e.tensor_scalar_mul(cols[:, 0:1], cols[:, 0:1], -1.0),
                 reads=[P+"cols"], writes=[P+"cols"])
            em.sc(lambda e: e.activation(out=prow[:], in_=ps_log[:, :], func=AF.Exp,
                                         bias=cols[:, 0:1], accum_out=cols[:, 9:10]),
                  reads=[P+"cols"], psreads=[P+"pslog"], writes=[P+"prow", P+"cols"])
            em.v(lambda e: e.reciprocal(cols[:, 10:11], cols[:, 9:10]),
                 reads=[P+"cols"], writes=[P+"cols"])
            em.v(lambda e: e.tensor_scalar_mul(probs[:], prow[:], cols[:, 10:11]),
                 reads=[P+"prow", P+"cols"], writes=[P+"probs"])
            em.v(lambda e: e.reduce_max(cols[:, 3:4], probs[:], axis=AX.X),
                 reads=[P+"probs"], writes=[P+"cols"])
            em.v(lambda e: e.tensor_single_scalar(scr[:], probs[:], cols[:, 3:4], ALU.is_equal),
                 reads=[P+"probs", P+"cols"], writes=[P+"scr"])
            em.v(lambda e: e.scalar_tensor_tensor(scr[:], scr[:], -BIGP, io8B[:], ALU.mult, ALU.add),
                 reads=[P+"scr", P+"io8B"], writes=[P+"scr"])
            em.v(lambda e: e.tensor_reduce(cols[:, 1:2], scr[:], axis=AX.X, op=ALU.min),
                 reads=[P+"scr"], writes=[P+"cols"])
            em.v(lambda e: e.tensor_single_scalar(eqf[0][:], io8[:], cols[:, 1:2], ALU.is_equal),
                 reads=[P+"io8", P+"cols"], writes=[P+"eqf0"])
            em.v(lambda e: e.scalar_tensor_tensor(prow[:], eqf[0][:], -2.0, probs[:], ALU.mult, ALU.add),
                 reads=[P+"eqf0", P+"probs"], writes=[P+"prow"])
            em.v(lambda e: e.reduce_max(cols[:, 4:5], prow[:], axis=AX.X),
                 reads=[P+"prow"], writes=[P+"cols"])
            em.v(lambda e: e.tensor_single_scalar(scr[:], prow[:], cols[:, 4:5], ALU.is_equal),
                 reads=[P+"prow", P+"cols"], writes=[P+"scr"])
            em.v(lambda e: e.scalar_tensor_tensor(scr[:], scr[:], -BIGP, io8B[:], ALU.mult, ALU.add),
                 reads=[P+"scr", P+"io8B"], writes=[P+"scr"])
            em.v(lambda e: e.tensor_reduce(cols[:, 2:3], scr[:], axis=AX.X, op=ALU.min),
                 reads=[P+"scr"], writes=[P+"cols"])
            em.v(lambda e: e.tensor_single_scalar(eqf[1][:], io8[:], cols[:, 2:3], ALU.is_equal),
                 reads=[P+"io8", P+"cols"], writes=[P+"eqf1"])
            em.v(lambda e: e.tensor_add(scr[:], eqf[0][:], eqf[1][:]),
                 reads=[P+"eqf0", P+"eqf1"], writes=[P+"scr"])
            em.v(lambda e: e.tensor_mul(csb[:], probs[:], scr[:]),
                 reads=[P+"probs", P+"scr"], writes=[P+"csb"])
            em.t(lambda e: e.transpose(ps_ct[0:8, :], csb[:], cst.ident_f[:]),
                 reads=[P+"csb", "identf"], writes=[P+"psct"])
            em.v(lambda e, tt=tt: e.tensor_copy(cfm[:, tt * 128:(tt + 1) * 128], ps_ct[0:8, :]),
                 psreads=[P+"psct"], writes=[P+"cfm"])
        em.barrier()

    ctxF = contextlib.ExitStack()
    sbf = mksb(ctxF)
    with ctxF:
        psf = mkps(ctxF)
        ps_cb = psf("pscb", [128, 512])
        ps_he = [psf(f"pshe{i}", [128, 512]) for i in range(2)]
        ps_o = [psf(f"pso{i}", [128, 512]) for i in range(2)]
        xfmH = sbf("xfmH", [128, 8, 2048], BF16)
        out_acc = sbf("oacc", [128, 16, D])
        wi_b = [sbf("wib0", [128, 8, HH], BF16)]
        wo_b = [sbf("wob0", [128, 8, D], BF16)]
        cb_sb = sbf("cbsb", [128, 512])
        g_fm = sbf("gfm", [128, 8, 512], BF16)
        xres = [sbf(f"xres{i}", [128, D], F32) for i in range(2)]
        s_t = [sbf(f"sf{i}", [128, D], F32) for i in range(2)]
        outt = [sbf(f"outf{i}", [128, D], F32) for i in range(2)]

        for half in range(2):
            th0 = half * 2048
            for ch in range(4):
                for dc in range(8):
                    em.dma("sync", xfmH[:, dc, ch * 512:(ch + 1) * 512],
                           xbf_dram[th0 + ch * 512:th0 + (ch + 1) * 512, dc * 128:(dc + 1) * 128],
                           reads=[P+"xbfd"], writes=[P+"xfmH"], transpose=True, tag="xtr")
            em.v(lambda e: e.memset(out_acc[:], 0.0), writes=[P+"oacc"])
            for e_i in range(E):
                ib = 0
                kwi, kwo = f"{P}wib0", f"{P}wob0"
                em.dma("gpsimd", wi_b[ib][:], dw["wi"][e_i].rearrange("(c p) m -> p c m", p=128),
                       writes=[kwi], tag="wi")
                em.dma("gpsimd", wo_b[ib][:], dw["wo"][e_i].rearrange("(c p) m -> p c m", p=128),
                       writes=[kwo], tag="wo")
                for tch in range(4):
                    tw = slice(tch * 512, (tch + 1) * 512)
                    em.t(lambda e, e_i=e_i, tch=tch, th0=th0: e.matmul(
                        ps_cb[:, :], sel8f[:, e_i * 128:(e_i + 1) * 128],
                        cfm[:, th0 + tch * 512:th0 + (tch + 1) * 512],
                        start=True, stop=True),
                        reads=[P+"sel8f", P+"cfm"], writes=[P+"pscb"], tag="cb")
                    em.v(lambda e: e.tensor_copy(cb_sb[:], ps_cb[:, :]),
                         psreads=[P+"pscb"], writes=[P+"cbsb"])
                    for ht in range(8):
                        sl = ht % 2
                        kp = f"{P}pshe{sl}"
                        for dc in range(8):
                            em.t(lambda e, ht=ht, dc=dc, sl=sl, ib=ib, tch=tch: e.matmul(
                                ps_he[sl][:, :], wi_b[ib][:, dc, ht * 128:(ht + 1) * 128],
                                xfmH[:, dc, tch * 512:(tch + 1) * 512],
                                start=(dc == 0), stop=(dc == 7)),
                                reads=[kwi, P+"xfmH"], writes=[kp], tag="hemm")
                        em.v(lambda e, ht=ht, sl=sl: e.tensor_mul(g_fm[:, ht, :], ps_he[sl][:, :], cb_sb[:]),
                             reads=[P+"cbsb"], psreads=[kp], writes=[P+"gfm"])
                    for tt2 in range(4):
                        for cw in range(2):
                            sl = (tt2 * 2 + cw) % 2
                            kp = f"{P}pso{sl}"
                            for hc in range(8):
                                em.t(lambda e, tt2=tt2, cw=cw, hc=hc, sl=sl, ib=ib: e.matmul(
                                    ps_o[sl][:, :], g_fm[:, hc, tt2 * 128:(tt2 + 1) * 128],
                                    wo_b[ib][:, hc, cw * 512:(cw + 1) * 512],
                                    start=(hc == 0), stop=(hc == 7)),
                                    reads=[P+"gfm", kwo], writes=[kp], tag="womm")
                            em.v(lambda e, tch=tch, tt2=tt2, cw=cw, sl=sl: e.tensor_add(
                                out_acc[:, tch * 4 + tt2, cw * 512:(cw + 1) * 512],
                                out_acc[:, tch * 4 + tt2, cw * 512:(cw + 1) * 512],
                                ps_o[sl][:, :]),
                                reads=[P+"oacc"], psreads=[kp], writes=[P+"oacc"])
            for tl in range(16):
                tt = half * 16 + tl
                iq = tl % 2
                rows = slice(tt * 128, (tt + 1) * 128)
                em.dma("sync", xres[iq][:], x_in[rows, :], writes=[f"{P}xres{iq}"])
                em.v(lambda e, iq=iq, tl=tl: e.tensor_add(s_t[iq][:], out_acc[:, tl, :], xres[iq][:]),
                     reads=[P+"oacc", f"{P}xres{iq}"], writes=[f"{P}sf{iq}"])
                emit_ln(em, P, s_t[iq], f"{P}sf{iq}", sq_t, red, outt[iq], f"{P}outf{iq}")
                em.dma("scalar", x_out[rows, :], outt[iq][:], reads=[f"{P}outf{iq}"], tag="xout")
        em.barrier()







def _build_fused():
    nc = bass.Bass("TRN2", target_bir_lowering=False, debug=False)
    ctx = contextlib.ExitStack()
    with ctx:
        x_in = nc.dram_tensor("x", [T, D], F32, kind="ExternalInput").ap()
        x_out = nc.dram_tensor("x4", [T, D], F32, kind="ExternalOutput").ap()
        x1s = nc.dram_tensor("x1s", [T, D], F32, kind="Internal").ap()
        x2s = nc.dram_tensor("x2s", [T, D], F32, kind="Internal").ap()
        x3s = nc.dram_tensor("x3s", [T, D], F32, kind="Internal").ap()
        dcst = declare_consts(nc)
        dcst2 = declare_consts2(nc)
        dcst3 = declare_consts3(nc)
        dcst4 = declare_consts4(nc)
        dw0a = l0a_declare_dram(nc, "l0a")
        dw0f = l0f_declare_dram(nc, "l0f")
        dw1a = l1a_declare_dram(nc, "l1a")
        dw1f = l1f_declare_dram(nc, "l1f")
        em = Em(nc, ctx)
        cst = Consts(em, nc, ctx, dcst)
        with contextlib.ExitStack() as c1:
            l0a_emit(em, nc, c1, cst, dw0a, x_in, x1s, pref="l0a")
            em.barrier()
        with contextlib.ExitStack() as c2:
            l0f_emit(em, nc, c2, cst, dcst2, dw0f, x1s, x2s, pref="l0f")
            em.barrier()
        with contextlib.ExitStack() as c3:
            l1a_emit(em, nc, c3, cst, dcst3, dw1a, x2s, x3s, pref="l1a")
            em.barrier()
        with contextlib.ExitStack() as c4:
            l1f_emit(em, nc, c4, cst, dcst4, dw1f, x3s, x_out, pref="l1f")
            em.barrier()
        em.barrier_reads_of_all()
        block = ctx.enter_context(nc.Block())
        em.replay(block)
    return nc


def kernel(**inputs):
    def grp(pref):
        return {k[len(pref):]: np.ascontiguousarray(np.asarray(v, np.float32))
                for k, v in inputs.items() if k.startswith(pref)}

    cst_all = host_consts()
    hp = {}
    hp.update(l0a_host_prep(grp("L0_attn_"), "l0a"))
    hp.update(l0f_host_prep(grp("L0_ffn_"), "l0f"))
    hp.update(l1a_host_prep(grp("L1_attn_"), "l1a"))
    hp.update(l1f_host_prep(grp("L1_ffn_"), "l1f"))
    hp.update(cst_all)
    hp.update(host_consts2())
    hp.update(host_consts3())
    hp.update(host_consts4())

    x = np.ascontiguousarray(np.asarray(inputs["L0_attn_hidden_states"], np.float32))
    nc = _build_fused()
    maps = [{**hp, "x": x[b]} for b in range(8)]
    res = run_bass_kernel_spmd(nc, maps, core_ids=list(range(8)))
    out = np.stack([np.asarray(res.results[b]["x4"]) for b in range(8)])
    return np.ascontiguousarray(out.astype(np.float32))


# version 06
# speedup vs baseline: 1.0353x; 1.0001x over previous
import os
import contextlib
import numpy as np
import ml_dtypes
import concourse.bass as bass
import concourse.mybir as mybir
from concourse.bass_utils import run_bass_kernel_spmd





F32 = mybir.dt.float32
BF16 = mybir.dt.bfloat16
U32 = mybir.dt.uint32
I32 = mybir.dt.int32
U8 = mybir.dt.uint8
AF = mybir.ActivationFunctionType
ALU = mybir.AluOpType
AX = mybir.AxisListType

STREAMS = ("sync", "tensor", "vector", "scalar", "gpsimd")


class Em:
    def __init__(self, nc, ctx):
        self.nc = nc
        self.ctx = ctx
        self.ops = {s: [] for s in STREAMS}
        self.csem = {s: ctx.enter_context(nc.semaphore("c_" + s)) for s in STREAMS}
        self.R = 8
        self.dsem = {s: [ctx.enter_context(nc.semaphore(f"d_{s}{r}")) for r in range(self.R)]
                     for s in ("sync", "scalar", "gpsimd")}
        self.ccount = {s: 0 for s in STREAMS}
        self.dcount = {s: 0 for s in ("sync", "scalar", "gpsimd")}
        self.kn = {s: {} for s in STREAMS}
        self.snaps = {("c", s): [] for s in STREAMS}
        for s in ("sync", "scalar", "gpsimd"):
            self.snaps[("d", s)] = []
        self.last_writer = {}
        self.readers = {}
        self.nwaits = 0

    def _tok_needed(self, s, toks):
        need = {}
        for tok in toks:
            if tok is None:
                continue
            kind, s2, n = tok
            if kind == "c" and s2 == "tensor" and s == "tensor":
                continue
            key = (kind, s2)
            if need.get(key, 0) < n:
                need[key] = n
        return need

    def op(self, s, fn, reads=(), writes=(), psreads=(), dma=False, tag=""):
        if psreads:
            reads = tuple(reads) + tuple(psreads)
            writes = tuple(writes) + tuple(psreads)
        toks = []
        for k in reads:
            toks.append(self.last_writer.get(k))
        for k in writes:
            toks.append(self.last_writer.get(k))
            toks.extend(self.readers.get(k, ()))
        need = self._tok_needed(s, toks)
        waits = []
        kn = self.kn[s]
        for key, n in sorted(need.items()):
            kind, s2 = key
            if kind == "c":
                if kn.get(key, 0) >= n:
                    continue
                waits.append((self.csem[s2], n))
                snap = self.snaps[key][n - 1]
                for kk, vv in snap.items():
                    if kn.get(kk, 0) < vv:
                        kn[kk] = vv
                if kn.get(key, 0) < n:
                    kn[key] = n
            else:
                slot = (n - 1) % self.R
                per = (n - 1) // self.R + 1
                kslot = ("ds", s2, slot)
                if kn.get(kslot, 0) >= per:
                    continue
                waits.append((self.dsem[s2][slot], 16 * per))
                snap = self.snaps[("d", s2)][n - 1]
                for kk, vv in snap.items():
                    if kn.get(kk, 0) < vv:
                        kn[kk] = vv
                if kn.get(kslot, 0) < per:
                    kn[kslot] = per
        self.nwaits += len(waits)
        if dma:
            assert s in self.dsem, f"stream {s} cannot issue DMAs"
            i0 = self.dcount[s] + 1
            slot = (i0 - 1) % self.R
            prev_per = (i0 - 1) // self.R
            kslot = ("ds", s, slot)
            if prev_per and kn.get(kslot, 0) < prev_per:
                waits.append((self.dsem[s][slot], 16 * prev_per))
                pi = prev_per * self.R + slot
                snap = self.snaps[("d", s)][pi - 1]
                for kk, vv in snap.items():
                    if kn.get(kk, 0) < vv:
                        kn[kk] = vv
                kn[kslot] = prev_per
                self.nwaits += 1
            self.dcount[s] += 1
            tok = ("d", s, self.dcount[s])
            inc = (self.dsem[s][slot], 16)
        else:
            self.ccount[s] += 1
            tok = ("c", s, self.ccount[s])
            inc = (self.csem[s], 1)
        snap = dict(kn)
        if tok[0] == "d":
            slot = (tok[2] - 1) % self.R
            snap[("ds", s, slot)] = (tok[2] - 1) // self.R + 1
        else:
            snap[("c", s)] = tok[2]
        self.snaps[(tok[0], s)].append(snap)
        self.ops[s].append((waits, fn, inc, tag))
        for k in reads:
            self.readers.setdefault(k, []).append(tok)
        for k in writes:
            self.last_writer[k] = tok
            self.readers[k] = []
        return tok

    def raw(self, s, fn, tag="raw"):
        self.ops[s].append(([], fn, None, tag))

    def v(self, fn, reads=(), writes=(), psreads=(), tag=""):
        return self.op("vector", fn, reads, writes, psreads, tag=tag)

    def sc(self, fn, reads=(), writes=(), psreads=(), tag=""):
        return self.op("scalar", fn, reads, writes, psreads, tag=tag)

    def t(self, fn, reads=(), writes=(), psreads=(), tag=""):
        return self.op("tensor", fn, reads, writes, psreads, tag=tag)

    def g(self, fn, reads=(), writes=(), psreads=(), tag=""):
        return self.op("gpsimd", fn, reads, writes, psreads, tag=tag)

    def dma(self, q, out, in_, reads=(), writes=(), tag="", **kw):
        return self.op(q, lambda e: e.dma_start(out=out, in_=in_, **kw),
                       reads, writes, dma=True, tag=tag)

    def barrier(self):
        counts_c = dict(self.ccount)
        counts_d = dict(self.dcount)
        for s in STREAMS:
            waits = []
            kn = self.kn[s]
            for s2 in STREAMS:
                if s2 == s:
                    continue
                n = counts_c[s2]
                if n and kn.get(("c", s2), 0) < n:
                    waits.append((self.csem[s2], n))
                    snap = self.snaps[("c", s2)][n - 1]
                    for kk, vv in snap.items():
                        if kn.get(kk, 0) < vv:
                            kn[kk] = vv
                    kn[("c", s2)] = max(kn.get(("c", s2), 0), n)
            for s2 in ("sync", "scalar", "gpsimd"):
                n = counts_d[s2]
                for slot in range(self.R):
                    per = (n - slot + self.R - 1) // self.R if n > slot else 0
                    if per and kn.get(("ds", s2, slot), 0) < per:
                        waits.append((self.dsem[s2][slot], 16 * per))
                        kn[("ds", s2, slot)] = per
                if n:
                    snap = self.snaps[("d", s2)][n - 1]
                    for kk, vv in snap.items():
                        if kn.get(kk, 0) < vv:
                            kn[kk] = vv
            n_own = counts_c[s]
            if n_own and kn.get(("c", s), 0) < n_own:
                waits.append((self.csem[s], n_own))
                kn[("c", s)] = n_own
            if s in self.dcount:
                n = counts_d[s]
                for slot in range(self.R):
                    per = (n - slot + self.R - 1) // self.R if n > slot else 0
                    if per and kn.get(("ds", s, slot), 0) < per:
                        waits.append((self.dsem[s][slot], 16 * per))
                        kn[("ds", s, slot)] = per
            self.ccount[s] += 1
            tok = ("c", s, self.ccount[s])
            snap = dict(kn)
            snap[("c", s)] = tok[2]
            self.snaps[("c", s)].append(snap)
            self.ops[s].append((waits, (lambda e: e.nop()), (self.csem[s], 1), "barrier"))
            self.nwaits += len(waits)

    def barrier_reads_of_all(self):
        def fin(eng):
            return eng.nop()
        waits = []
        kn = self.kn["sync"]
        for s in STREAMS:
            n = self.ccount[s]
            if n and kn.get(("c", s), 0) < n:
                waits.append((self.csem[s], n))
        for s in ("sync", "scalar", "gpsimd"):
            n = self.dcount[s]
            for slot in range(self.R):
                per = (n - slot + self.R - 1) // self.R if n > slot else 0
                if per and kn.get(("ds", s, slot), 0) < per:
                    waits.append((self.dsem[s][slot], 16 * per))
        n_own = self.ccount["sync"]
        if n_own and kn.get(("c", "sync"), 0) < n_own:
            waits.append((self.csem["sync"], n_own))
            kn[("c", "sync")] = n_own
        self.ccount["sync"] += 1
        self.ops["sync"].append((waits, fin, (self.csem["sync"], 1), "final"))

    def replay(self, block):
        em = self

        def make(stream):
            oplist = em.ops[stream]

            def runner(eng):
                for idx, (waits, fn, inc, tag) in enumerate(oplist):
                    for sem, val in waits:
                        eng.wait_ge(sem, val)
                    try:
                        inst = fn(eng)
                    except Exception:
                        print(f"EMIT FAIL stream={stream} idx={idx} tag={tag} "
                              f"waits={[(s.name, v) for s, v in waits]}")
                        raise
                    if inc is not None:
                        inst.then_inc(inc[0], inc[1])

            return runner

        block.sync(make("sync"))
        block.tensor(make("tensor"))
        block.vector(make("vector"))
        block.scalar(make("scalar"))
        block.gpsimd(make("gpsimd"))


def ceil_div(a, b):
    return (a + b - 1) // b


T = 4096
D = 1024
DI = 2048
NS = 16
RK = 64
TC = 512
NCH = T // TC
NDT = DI // 128
NFT = 2 * DI // 128
NTT = T // 128
EPS = 1e-5


def l0a_host_prep(w, pref=""):
    cw = w["conv_w"].reshape(NDT, 128, 4).transpose(1, 0, 2).copy()
    cb = w["conv_b"].reshape(NDT, 128).T.copy()
    dtb = w["dt_proj_b"].reshape(NDT, 128).T.copy()
    alog = w["A_log"].reshape(NDT, 128, NS).transpose(1, 0, 2).copy()
    dsk = w["D_skip"].reshape(NDT, 128).T.copy()
    d = {
        "ipw": np.ascontiguousarray(w["in_proj_w"]),
        "convw": cw, "convb": cb,
        "xpw": np.ascontiguousarray(w["x_proj_w"]),
        "dtw": np.ascontiguousarray(w["dt_proj_w"]),
        "dtb": dtb, "alog": alog, "dskip": dsk,
        "opw": np.ascontiguousarray(w["out_proj_w"]),
    }
    return {pref + k: v for k, v in d.items()}


def host_consts():
    ident = np.eye(128, dtype=np.float32)
    sel = np.zeros((32, 32, 128), np.float32)
    for n in range(32):
        sel[n, n, :] = 1.0
    return {
        "identf": ident,
        "identb": ident.astype(ml_dtypes.bfloat16),
        "sel32": sel.reshape(32, 32 * 128).astype(ml_dtypes.bfloat16),
    }


def declare_consts(nc):
    d = {}
    d["identf"] = nc.dram_tensor("identf", [128, 128], F32, kind="ExternalInput").ap()
    d["identb"] = nc.dram_tensor("identb", [128, 128], BF16, kind="ExternalInput").ap()
    d["sel32"] = nc.dram_tensor("sel32", [32, 32 * 128], BF16, kind="ExternalInput").ap()
    return d


def l0a_declare_dram(nc, pref=""):
    d = {}
    def inp(name, shape, dt=F32):
        d[name] = nc.dram_tensor(pref + name, shape, dt, kind="ExternalInput").ap()
    inp("ipw", [D, 2 * DI])
    inp("convw", [128, NDT, 4])
    inp("convb", [128, NDT])
    inp("xpw", [DI, 96])
    inp("dtw", [RK, DI])
    inp("dtb", [128, NDT])
    inp("alog", [128, NDT, NS])
    inp("dskip", [128, NDT])
    inp("opw", [DI, D])
    return d


class Consts:
    def __init__(self, em, nc, ctx, dc):
        def sb(name, shape, dtype=F32):
            return ctx.enter_context(nc.sbuf_tensor("sb_" + name, shape, dtype))
        self.ident_b = sb("identb", [128, 128], BF16)
        self.ident_f = sb("identf", [128, 128], F32)
        self.sel32 = sb("sel32", [32, 32 * 128], BF16)
        em.dma("sync", self.ident_b[:], dc["identb"][:, :], writes=["identb"])
        em.dma("sync", self.ident_f[:], dc["identf"][:, :], writes=["identf"])
        em.dma("sync", self.sel32[:], dc["sel32"][:, :], writes=["sel32"])


def emit_ln(em, P, src, ksrc, sq_t, red, dst, kdst):
    em.v(lambda e: e.reduce_sum(red[:, 0:1], src[:], axis=AX.X),
         reads=[ksrc], writes=[P + "red"])
    em.v(lambda e: e.tensor_scalar_mul(red[:, 1:2], red[:, 0:1], -1.0 / D),
         reads=[P + "red"], writes=[P + "red"])
    em.v(lambda e: e.tensor_scalar_add(src[:], src[:], red[:, 1:2]),
         reads=[ksrc, P + "red"], writes=[ksrc])
    em.sc(lambda e: e.activation(out=sq_t[:], in_=src[:], func=AF.Square,
                                 accum_out=red[:, 2:3]),
          reads=[ksrc], writes=[P + "sq", P + "red"])
    em.v(lambda e: e.tensor_scalar(red[:, 3:4], red[:, 2:3], 1.0 / D, EPS, ALU.mult, ALU.add),
         reads=[P + "red"], writes=[P + "red"])
    em.sc(lambda e: e.activation(out=red[:, 3:4], in_=red[:, 3:4], func=AF.Ln),
          reads=[P + "red"], writes=[P + "red"])
    em.sc(lambda e: e.activation(out=red[:, 3:4], in_=red[:, 3:4], func=AF.Exp, scale=-0.5),
          reads=[P + "red"], writes=[P + "red"])
    em.v(lambda e: e.tensor_scalar_mul(dst[:], src[:], red[:, 3:4]),
         reads=[ksrc, P + "red"], writes=[kdst])


def l0a_emit(em, nc, ctx, cst, dw, x_in, x_out, pref="l0a", stage=99, dbg=False):
    P = pref
    def mksb(c):
        def sb(name, shape, dtype=F32):
            return c.enter_context(nc.sbuf_tensor(P + "sb_" + name, shape, dtype))
        return sb
    sbp = mksb(ctx)
    def psum(name, shape, dtype=F32):
        return ctx.enter_context(nc.psum_tensor(P + "pp_" + name, shape, dtype))

    kind = "ExternalOutput" if dbg else "Internal"
    u_dram = nc.dram_tensor(P + "u_scr", [NDT, 128, T], F32, kind=kind).ap()
    sres_dram = nc.dram_tensor(P + "sres_scr", [NDT, 128, T], BF16, kind=kind).ap()

    xdbl = sbp("xdbl", [128, NTT, 96], F32)
    dtfm = sbp("dtfm", [RK, T], BF16)
    bmcm = sbp("bmcm", [32, T], BF16)
    xpw_b = sbp("xpw", [128, NDT, 96], BF16)
    dtw_b = sbp("dtw", [RK, DI], BF16)
    convw_s = sbp("convw", [128, NDT, 4], F32)
    convb_s = sbp("convb", [128, NDT], F32)
    dtb_s = sbp("dtb", [128, NDT], F32)
    acol_s = sbp("acol", [128, NDT * NS], F32)
    dskip_s = sbp("dskip", [128, NDT], F32)
    state = sbp("state", [128, NDT * NS], F32)
    xres = [sbp(f"xres{i}", [128, D], F32) for i in range(2)]
    s_t = [sbp(f"st{i}", [128, D], F32) for i in range(2)]
    sq_t = sbp("sq", [128, D], F32)
    red = sbp("red", [128, 4], F32)
    outt = [sbp(f"outt{i}", [128, D], F32) for i in range(2)]


    em.dma("sync", convw_s[:], dw["convw"][:, :, :], writes=[P+"convw"])
    em.dma("sync", convb_s[:], dw["convb"][:, :], writes=[P+"convb"])
    em.dma("sync", dtb_s[:], dw["dtb"][:, :], writes=[P+"dtb"])
    em.dma("sync", acol_s[:], dw["alog"].rearrange("p a b -> p (a b)"), writes=[P+"acol"])
    em.dma("sync", dskip_s[:], dw["dskip"][:, :], writes=[P+"dskip"])
    em.dma("gpsimd", xpw_b[:], dw["xpw"].rearrange("(c p) m -> p c m", p=128), writes=[P+"xpw"])
    em.dma("gpsimd", dtw_b[:], dw["dtw"][:, :], writes=[P+"dtw"])
    em.sc(lambda e: e.activation(out=acol_s[:], in_=acol_s[:], func=AF.Exp),
          reads=[P+"acol"], writes=[P+"acol"])
    em.v(lambda e: e.tensor_scalar_mul(acol_s[:], acol_s[:], -1.0),
         reads=[P+"acol"], writes=[P+"acol"])
    em.v(lambda e: e.memset(state[:], 0.0),
         writes=[f"{P}state{c}" for c in range(NDT * NS)])
    em.v(lambda e: e.memset(xdbl[:], 0.0), writes=[P+"xdbl"])

    ctxA = contextlib.ExitStack()
    sba = mksb(ctxA)
    def mkps(c):
        def ps(name, shape, dtype=F32):
            return c.enter_context(nc.psum_tensor(P + "pp_" + name, shape, dtype))
        return ps
    with ctxA:
        psa_ = mkps(ctxA)
        ps_a = psa_("psa", [128, 512])
        ps_tr = [psa_(f"pstr{i}", [128, 128], BF16) for i in range(2)]
        ps_96 = [psa_(f"ps96{i}", [128, 96]) for i in range(2)]
        ps_tf = [psa_(f"pstf{i}", [128, 128]) for i in range(2)]
        xfm = sba("xfm", [128, 8, T], BF16)
        xld = [sba(f"xld{i}", [128, D], F32) for i in range(2)]
        xbf = [sba(f"xbf{i}", [128, D], BF16) for i in range(2)]
        ipw_t = [sba(f"ipwt{i}", [128, 8, 128], BF16) for i in range(2)]
        xz_t = sba("xz", [128, T], F32)
        cacc = sba("cacc", [128, T], F32)
        ubf_t = sba("ubf", [128, T], BF16)
        sres_t = sba("srest", [128, T], BF16)

        import os as _os
        _cap = int(_os.environ.get("NTTCAP", str(NTT)))
        for tt in range(_cap):
            i = tt % 2
            kxl, kxb = f"{P}xld{i}", f"{P}xbf{i}"
            em.dma("sync", xld[i][:], x_in[tt * 128:(tt + 1) * 128, :], writes=[kxl], tag="p0ld")
            em.v(lambda e, i=i: e.tensor_copy(xbf[i][:], xld[i][:]), reads=[kxl], writes=[kxb])
            for dc in range(8):
                slot = dc % 2
                kps = f"{P}pstr{slot}"
                em.t(lambda e, i=i, dc=dc, slot=slot: e.transpose(
                    ps_tr[slot][:, :], xbf[i][:, dc * 128:(dc + 1) * 128], cst.ident_b[:]),
                    reads=[kxb, "identb"], writes=[kps])
                if dc % 2:
                    em.sc(lambda e, tt=tt, dc=dc, slot=slot: e.copy(
                        xfm[:, dc, tt * 128:(tt + 1) * 128], ps_tr[slot][:, :]),
                        reads=[], psreads=[kps], writes=[P+"xfm"])
                else:
                    em.v(lambda e, tt=tt, dc=dc, slot=slot: e.tensor_copy(
                        xfm[:, dc, tt * 128:(tt + 1) * 128], ps_tr[slot][:, :]),
                        reads=[], psreads=[kps], writes=[P+"xfm"])

        for ft in (range(NFT) if stage >= 2 else []):
            i = ft % 2
            kw = f"{P}ipwt{i}"
            em.dma("gpsimd", ipw_t[i][:], dw["ipw"][:, ft * 128:(ft + 1) * 128].rearrange("(c p) m -> p c m", p=128),
                   writes=[kw], tag="p1w")
            for w in range(8):
                for dc in range(8):
                    em.t(lambda e, i=i, w=w, dc=dc: e.matmul(
                        ps_a[:, :], ipw_t[i][:, dc, :], xfm[:, dc, w * 512:(w + 1) * 512],
                        start=(dc == 0), stop=(dc == 7)),
                        reads=[kw, P+"xfm"], writes=[P+"psa"], tag="p1mm")
                em.sc(lambda e, w=w: e.copy(xz_t[:, w * 512:(w + 1) * 512], ps_a[:, :]),
                      psreads=[P+"psa"], writes=[P+"xz"])
            if ft < NDT:
                dt_i = ft
                em.v(lambda e, dt_i=dt_i: e.tensor_scalar_mul(cacc[:], xz_t[:], convw_s[:, dt_i, 3:4]),
                     reads=[P+"xz", P+"convw"], writes=[P+"cacc"])
                for s in range(1, 4):
                    em.v(lambda e, s=s, dt_i=dt_i: e.scalar_tensor_tensor(
                        cacc[:, s:T], xz_t[:, 0:T - s], convw_s[:, dt_i, 3 - s:4 - s], cacc[:, s:T],
                        ALU.mult, ALU.add),
                        reads=[P+"xz", P+"convw", P+"cacc"], writes=[P+"cacc"])
                em.sc(lambda e, dt_i=dt_i: e.activation(out=cacc[:], in_=cacc[:], func=AF.Silu,
                                                        bias=convb_s[:, dt_i:dt_i + 1]),
                      reads=[P+"cacc", P+"convb"], writes=[P+"cacc"])
                em.dma("scalar", u_dram[dt_i, :, :], cacc[:], reads=[P+"cacc"], tag="p1us")
                em.v(lambda e: e.tensor_copy(ubf_t[:], cacc[:]), reads=[P+"cacc"], writes=[P+"ubf"])
                for tt in range(NTT):
                    j = tt % 2
                    kp = f"{P}ps96{j}"
                    em.t(lambda e, tt=tt, dt_i=dt_i, j=j: e.matmul(
                        ps_96[j][:, :], ubf_t[:, tt * 128:(tt + 1) * 128], xpw_b[:, dt_i, :],
                        start=True, stop=True),
                        reads=[P+"ubf", P+"xpw"], writes=[kp])
                    em.v(lambda e, tt=tt, j=j: e.tensor_add(xdbl[:, tt, :], xdbl[:, tt, :], ps_96[j][:, :]),
                         reads=[P+"xdbl"], psreads=[kp], writes=[P+"xdbl"])
            else:
                dt_i = ft - NDT
                em.sc(lambda e: e.activation(out=sres_t[:], in_=xz_t[:], func=AF.Silu),
                      reads=[P+"xz"], writes=[P+"srest"])
                em.dma("scalar", sres_dram[dt_i, :, :], sres_t[:], reads=[P+"srest"], tag="p1ss")

        for tt in (range(NTT) if stage >= 3 else []):
            em.t(lambda e, tt=tt: e.transpose(ps_tf[0][0:64, :], xdbl[:, tt, 0:64], cst.ident_f[:]),
                 reads=[P+"xdbl", "identf"], writes=[P+"pstf0"])
            em.v(lambda e, tt=tt: e.tensor_copy(dtfm[:, tt * 128:(tt + 1) * 128], ps_tf[0][0:64, :]),
                 psreads=[P+"pstf0"], writes=[P+"dtfm"])
            em.t(lambda e, tt=tt: e.transpose(ps_tf[1][0:32, :], xdbl[:, tt, 64:96], cst.ident_f[:]),
                 reads=[P+"xdbl", "identf"], writes=[P+"pstf1"])
            em.v(lambda e, tt=tt: e.tensor_copy(bmcm[:, tt * 128:(tt + 1) * 128], ps_tf[1][0:32, :]),
                 psreads=[P+"pstf1"], writes=[P+"bmcm"])

        em.barrier()

    if stage < 4:
        em.v(lambda e: e.memset(outt[0][:], 0.0), writes=[P+"outt0"])
        for tt in range(NTT):
            em.dma("scalar", x_out[tt * 128:(tt + 1) * 128, :], outt[0][:],
                   reads=[P+"outt0"], tag="dummy")
        return
    ctxB = contextlib.ExitStack()
    sbb = mksb(ctxB)
    with ctxB:
        psb_ = mkps(ctxB)
        ps_a = psb_("psdt", [128, 512])
        ps_b = psb_("psbc", [128, 512])
        ps_y = psb_("psy", [128, 512])
        ps_o = [psb_(f"pso{i}", [128, 512]) for i in range(2)]
        opw_b = sbb("opw", [128, NDT, D], BF16)
        bcB = sbb("bcB", [128, NS, TC], BF16)
        bcC = sbb("bcC", [128, NS, TC], BF16)
        sp1 = sbb("sp1", [128, TC], F32)
        spt = sbb("spt", [128, TC], F32)
        delta_t = sbb("delta", [128, TC], F32)
        uch = [sbb(f"uch{i}", [128, TC], F32) for i in range(2)]
        du_t = sbb("du", [128, TC], F32)
        uDb_t = sbb("uDb", [128, TC], BF16)
        dA = [sbb(f"dA{i}", [128, TC], F32) for i in range(4)]
        dBu = [sbb(f"dBu{i}", [128, TC], F32) for i in range(4)]
        h_t = [sbb(f"h{i}", [128, TC], F32) for i in range(4)]
        tmpb = [sbb(f"tmpb{i}", [128, TC], BF16) for i in range(4)]
        sresc = [sbb(f"sresc{i}", [128, TC], BF16) for i in range(2)]
        yfin = sbb("yfin", [128, NDT, TC], BF16)

        em.dma("gpsimd", opw_b[:], dw["opw"].rearrange("(c p) m -> p c m", p=128), writes=[P+"opw"])

        SS = os.environ.get("SCANSUB", "d")
        _nchcap = int(os.environ.get("NCHCAP", str(NCH)))
        _ndtcap = int(os.environ.get("NDTCAP", str(NDT)))
        for ch in range(_nchcap):
            t0 = ch * TC
            if SS == "a":
                em.v(lambda e: e.memset(yfin[:], 0.0), writes=[P+"yfin"])
            for n in (range(NS) if SS != "z" else []):
                for half in range(2):
                    row = n if half == 0 else NS + n
                    dst = bcB if half == 0 else bcC
                    kd = (P + "bcB") if half == 0 else (P + "bcC")
                    em.t(lambda e, row=row, t0=t0: e.matmul(
                        ps_b[:, :], cst.sel32[:, row * 128:(row + 1) * 128],
                        bmcm[:, t0:t0 + TC], start=True, stop=True),
                        reads=["sel32", P+"bmcm"], writes=[P+"psb"], tag="bc")
                    em.v(lambda e, dst=dst, n=n: e.tensor_copy(dst[:, n, :], ps_b[:, :]),
                         psreads=[P+"psb"], writes=[kd])
            DOPS = os.environ.get("DTOPS", "m12345uyrf")
            for dt_i in (range(_ndtcap) if SS not in ("a", "z") else []):
              if "m" in DOPS:
                em.t(lambda e, dt_i=dt_i, t0=t0: e.matmul(
                    ps_a[:, :], dtw_b[:, dt_i * 128:(dt_i + 1) * 128], dtfm[:, t0:t0 + TC],
                    start=True, stop=True),
                    reads=[P+"dtw", P+"dtfm"], writes=[P+"psa"], tag="dt")
              if "1" in DOPS:
                em.v(lambda e, dt_i=dt_i: e.tensor_scalar(
                    sp1[:], ps_a[:, :], dtb_s[:, dt_i:dt_i + 1], 0.0, ALU.add, ALU.max),
                    reads=[P+"dtb"], psreads=[P+"psa"], writes=[P+"sp1"])
              if "2" in DOPS:
                em.sc(lambda e, dt_i=dt_i: e.activation(out=spt[:], in_=ps_a[:, :], func=AF.Abs,
                                                        bias=dtb_s[:, dt_i:dt_i + 1]),
                      reads=[P+"dtb"], psreads=[P+"psa"], writes=[P+"spt"])
              if "A" in DOPS:
                em.sc(lambda e: e.activation(out=spt[:], in_=ps_a[:, :], func=AF.Abs),
                      psreads=[P+"psa"], writes=[P+"spt"])
              if "B" in DOPS:
                em.sc(lambda e, dt_i=dt_i: e.activation(out=spt[:], in_=ps_a[:, :], func=AF.Identity,
                                                        bias=dtb_s[:, dt_i:dt_i + 1]),
                      reads=[P+"dtb"], psreads=[P+"psa"], writes=[P+"spt"])
              if "C" in DOPS:
                em.sc(lambda e, dt_i=dt_i: e.activation(out=spt[:], in_=sp1[:], func=AF.Abs,
                                                        bias=dtb_s[:, dt_i:dt_i + 1]),
                      reads=[P+"sp1", P+"dtb"], writes=[P+"spt"])
              if "3" in DOPS:
                em.sc(lambda e: e.activation(out=spt[:], in_=spt[:], func=AF.Exp, scale=-1.0),
                      reads=[P+"spt"], writes=[P+"spt"])
              if "4" in DOPS:
                em.sc(lambda e: e.activation(out=spt[:], in_=spt[:], func=AF.Ln, bias=1.0),
                      reads=[P+"spt"], writes=[P+"spt"])
              if "5" in DOPS:
                em.v(lambda e: e.tensor_add(delta_t[:], sp1[:], spt[:]),
                     reads=[P+"sp1", P+"spt"], writes=[P+"delta"])
              iu = dt_i % 2
              if "u" in DOPS:
                ku = f"{P}uch{iu}"
                em.dma("sync", uch[iu][:], u_dram[dt_i, :, t0:t0 + TC], writes=[ku], tag="p3u")
                em.v(lambda e, iu=iu: e.tensor_mul(du_t[:], delta_t[:], uch[iu][:]),
                     reads=[P+"delta", ku], writes=[P+"du"])
                em.v(lambda e, iu=iu, dt_i=dt_i: e.tensor_scalar_mul(uDb_t[:], uch[iu][:], dskip_s[:, dt_i:dt_i + 1]),
                     reads=[ku, P+"dskip"], writes=[P+"uDb"])
              if "y" in DOPS:
                em.t(lambda e, _st=(SS == "b"): e.matmul(ps_y[:, :], cst.ident_b[:], uDb_t[:], start=True, stop=_st),
                     reads=["identb", P+"uDb"], writes=[P+"psy"], tag="yacc")
              if True:
                for n in (range(NS) if SS != "b" else []):
                    j = n % 4
                    col = dt_i * NS + n
                    em.sc(lambda e, j=j, col=col: e.activation(
                        out=dA[j][:], in_=delta_t[:], func=AF.Exp, scale=acol_s[:, col:col + 1]),
                        reads=[P+"delta", P+"acol"], writes=[f"{P}dA{j}"])
                    em.v(lambda e, j=j, n=n: e.tensor_mul(dBu[j][:], du_t[:], bcB[:, n, :]),
                         reads=[P+"du", P+"bcB"], writes=[f"{P}dBu{j}"])
                    if SS == "c":
                        em.v(lambda e, j=j: e.tensor_copy(h_t[j][:], dBu[j][:]),
                             reads=[f"{P}dBu{j}"], writes=[f"{P}h{j}"])
                    else:
                        em.v(lambda e, j=j, col=col: e.tensor_tensor_scan(
                            h_t[j][:], dA[j][:], dBu[j][:], state[:, col:col + 1], ALU.mult, ALU.add),
                            reads=[f"{P}dA{j}", f"{P}dBu{j}", P+"state"], writes=[f"{P}h{j}"])
                    em.sc(lambda e, j=j, col=col: e.copy(state[:, col:col + 1], h_t[j][:, TC - 1:TC]),
                          reads=[f"{P}h{j}"], writes=[f"{P}state{col}"])
                    em.g(lambda e, j=j, n=n: e.tensor_mul(tmpb[j][:], h_t[j][:], bcC[:, n, :]),
                         reads=[f"{P}h{j}", P+"bcC"], writes=[f"{P}tmpb{j}"])
                    em.t(lambda e, j=j, n=n: e.matmul(
                        ps_y[:, :], cst.ident_b[:], tmpb[j][:], start=False, stop=(n == NS - 1)),
                        reads=["identb", f"{P}tmpb{j}"], writes=[P+"psy"], tag="yacc")
              isr = dt_i % 2
              ksr = f"{P}sresc{isr}"
              if "r" in DOPS:
                em.dma("sync", sresc[isr][:], sres_dram[dt_i, :, t0:t0 + TC], writes=[ksr], tag="p3s")
              if "f" in DOPS:
                em.v(lambda e, isr=isr, dt_i=dt_i: e.tensor_mul(yfin[:, dt_i, :], ps_y[:, :], sresc[isr][:]),
                     reads=[ksr], psreads=[P+"psy"], writes=[P+"yfin"])
            for q in range(TC // 128):
                iq = q % 2
                rows = slice(t0 + q * 128, t0 + (q + 1) * 128)
                em.dma("sync", xres[iq][:], x_in[rows, :], writes=[f"{P}xres{iq}"], tag="p3x")
                for half in range(2):
                    for dt_i in range(NDT):
                        em.t(lambda e, q=q, dt_i=dt_i, half=half: e.matmul(
                            ps_o[half][:, :],
                            yfin[:, dt_i, q * 128:(q + 1) * 128],
                            opw_b[:, dt_i, half * 512:(half + 1) * 512],
                            start=(dt_i == 0), stop=(dt_i == NDT - 1)),
                            reads=[P+"yfin", P+"opw"], writes=[f"{P}pso{half}"], tag="op")
                for half in range(2):
                    em.v(lambda e, iq=iq, half=half: e.tensor_add(
                        s_t[iq][:, half * 512:(half + 1) * 512], ps_o[half][:, :],
                        xres[iq][:, half * 512:(half + 1) * 512]),
                        reads=[f"{P}xres{iq}"], psreads=[f"{P}pso{half}"], writes=[f"{P}st{iq}"])
                emit_ln(em, P, s_t[iq], f"{P}st{iq}", sq_t, red, outt[iq], f"{P}outt{iq}")
                em.dma("scalar", x_out[rows, :], outt[iq][:], reads=[f"{P}outt{iq}"], tag="p3o")
        em.barrier()




T = 4096
D = 1024
E = 8
K = 2
H = 4096
CAP = 640
NTT = T // 128
NST = E * CAP // 128
BIGP = 1.0e6
EPS = 1e-5


def l0f_host_prep(w, pref=""):
    b1 = w["b1"].reshape(E, 32, 128).transpose(2, 0, 1).copy()
    d = {
        "wr": np.ascontiguousarray(w["Wr"]),
        "w1": np.ascontiguousarray(w["W1"]),
        "b1": b1,
        "w2": np.ascontiguousarray(w["W2"]),
        "b2": np.ascontiguousarray(w["b2"]),
    }
    return {pref + k: v for k, v in d.items()}


def host_consts2():
    lstrict = np.tril(np.ones((128, 128), np.float32), -1).T
    return {"lstrict": lstrict.astype(ml_dtypes.bfloat16)}


def declare_consts2(nc):
    return {"lstrict": nc.dram_tensor("lstrict", [128, 128], BF16, kind="ExternalInput").ap()}


def l0f_declare_dram(nc, pref=""):
    d = {}
    def inp(name, shape, dt=F32):
        d[name] = nc.dram_tensor(pref + name, shape, dt, kind="ExternalInput").ap()
    inp("wr", [D, E])
    inp("w1", [E, D, H])
    inp("b1", [128, E, 32])
    inp("w2", [E, H, D])
    inp("b2", [E, D])
    return d


def l0f_emit(em, nc, ctx, cst, cst2, dw, x_in, x_out, pref="l0f"):
    P = pref
    def mksb(c):
        def sb(name, shape, dtype=F32):
            return c.enter_context(nc.sbuf_tensor(P + "sb_" + name, shape, dtype))
        return sb
    def mkps(c):
        def ps(name, shape, dtype=F32):
            return c.enter_context(nc.psum_tensor(P + "pp_" + name, shape, dtype))
        return ps
    sbp = mksb(ctx)

    slot_dram = nc.dram_tensor(P + "slot_scr", [E * CAP, 1], U32, kind="Internal").ap()
    y_dram = nc.dram_tensor(P + "y_scr", [E * CAP, D], BF16, kind="Internal").ap()

    lstr = sbp("lstr", [128, 128], BF16)
    wr_s = sbp("wr", [128, 8, E], F32)
    b1_s = sbp("b1", [128, E, 32], F32)
    addr_sb = sbp("addr", [128, NTT, K], U32)
    pbuf = sbp("pbuf", [128, NTT, K], F32)
    io8 = sbp("io8", [128, E], F32)
    io8B = sbp("io8B", [128, E], F32)
    tok_sb = sbp("tok", [128, NTT], U32)
    onesf1 = sbp("onesf1", [1, 128], F32)
    onesb1 = sbp("onesb1", [128, 1], BF16)
    onesbr = sbp("onesbr", [1, 128], BF16)
    red = sbp("red", [128, 4], F32)
    sq_t = sbp("sq", [128, D], F32)

    em.dma("sync", lstr[:], cst2["lstrict"][:, :], writes=[P+"lstr"])
    em.dma("sync", wr_s[:], dw["wr"].rearrange("(c p) m -> p c m", p=128), writes=[P+"wr"])
    em.dma("sync", b1_s[:], dw["b1"][:, :, :], writes=[P+"b1"])
    em.g(lambda e: e.iota(io8[:], pattern=[[1, E]], base=0, channel_multiplier=0,
                          allow_small_or_imprecise_dtypes=True), writes=[P+"io8"])
    em.g(lambda e: e.iota(io8B[:], pattern=[[1, E]], base=int(BIGP), channel_multiplier=0,
                          allow_small_or_imprecise_dtypes=True), writes=[P+"io8B"])
    em.g(lambda e: e.iota(tok_sb[:], pattern=[[128, NTT]], base=0, channel_multiplier=1),
         writes=[P+"tok"])
    bc_slot = nc.gpsimd.alloc_register(P + "bcslot")
    bc_tok = nc.gpsimd.alloc_register(P + "bctok")
    em.raw("gpsimd", lambda e: e.reg_mov(bc_slot, E * CAP - 1))
    em.raw("gpsimd", lambda e: e.reg_mov(bc_tok, T - 1))
    em.v(lambda e: e.memset(onesf1[:], 1.0), writes=[P+"onesf1"])
    em.v(lambda e: e.memset(onesb1[:], 1.0), writes=[P+"onesb1"])
    em.v(lambda e: e.memset(onesbr[:], 1.0), writes=[P+"onesbr"])

    ctxR = contextlib.ExitStack()
    sbr = mksb(ctxR)
    with ctxR:
        psr = mkps(ctxR)
        ps_log = psr("pslog", [128, E])
        ps_cum = psr("pscum", [128, E])
        ps_row = psr("psrow", [1, E])
        ps_car = psr("pscar", [128, E])
        xld = [sbr(f"xld{i}", [128, D], F32) for i in range(2)]
        xfm = sbr("xfm", [128, 8, 128], F32)
        pstrf = [psr(f"pstrf{i}", [128, 128]) for i in range(2)]
        probs = sbr("probs", [128, E])
        prow = sbr("prow", [128, E])
        scr = sbr("scr", [128, E])
        eqf = [sbr(f"eqf{i}", [128, E]) for i in range(2)]
        rsum2 = sbr("rsum2", [128, E], BF16)
        carry = sbr("carry", [128, E])
        posx = sbr("posx", [128, E])
        row_sb = sbr("rowsb", [1, E])
        cols = sbr("cols", [128, 12])
        addrf = sbr("addrf", [128, 2])
        em.v(lambda e: e.memset(carry[:], 0.0), writes=[P+"carry"])

        for tt in range(NTT):
            i = tt % 2
            kxl = f"{P}xld{i}"
            em.dma("sync", xld[i][:], x_in[tt * 128:(tt + 1) * 128, :], writes=[kxl])
            for dc in range(8):
                sl = dc % 2
                em.t(lambda e, i=i, dc=dc, sl=sl: e.transpose(
                    pstrf[sl][:, :], xld[i][:, dc * 128:(dc + 1) * 128], cst.ident_f[:]),
                    reads=[kxl, "identf"], writes=[f"{P}pstrf{sl}"])
                em.v(lambda e, dc=dc, sl=sl: e.tensor_copy(xfm[:, dc, :], pstrf[sl][:, :]),
                     psreads=[f"{P}pstrf{sl}"], writes=[P+"xfm"])
            for dc in range(8):
                em.t(lambda e, dc=dc: e.matmul(
                    ps_log[:, :], xfm[:, dc, :], wr_s[:, dc, :],
                    start=(dc == 0), stop=(dc == 7)),
                    reads=[P+"xfm", P+"wr"], writes=[P+"pslog"], tag="router")
            em.v(lambda e: e.reduce_max(cols[:, 0:1], ps_log[:, :], axis=AX.X),
                 psreads=[P+"pslog"], writes=[P+"cols"])
            em.v(lambda e: e.tensor_scalar_mul(cols[:, 0:1], cols[:, 0:1], -1.0),
                 reads=[P+"cols"], writes=[P+"cols"])
            em.sc(lambda e: e.activation(out=prow[:], in_=ps_log[:, :], func=AF.Exp,
                                         bias=cols[:, 0:1], accum_out=cols[:, 9:10]),
                  reads=[P+"cols"], psreads=[P+"pslog"], writes=[P+"prow", P+"cols"])
            em.v(lambda e: e.reciprocal(cols[:, 10:11], cols[:, 9:10]),
                 reads=[P+"cols"], writes=[P+"cols"])
            em.v(lambda e: e.tensor_scalar_mul(probs[:], prow[:], cols[:, 10:11]),
                 reads=[P+"prow", P+"cols"], writes=[P+"probs"])
            em.v(lambda e: e.reduce_max(cols[:, 3:4], probs[:], axis=AX.X),
                 reads=[P+"probs"], writes=[P+"cols"])
            em.v(lambda e: e.tensor_single_scalar(scr[:], probs[:], cols[:, 3:4], ALU.is_equal),
                 reads=[P+"probs", P+"cols"], writes=[P+"scr"])
            em.v(lambda e: e.scalar_tensor_tensor(scr[:], scr[:], -BIGP, io8B[:], ALU.mult, ALU.add),
                 reads=[P+"scr", P+"io8B"], writes=[P+"scr"])
            em.v(lambda e: e.tensor_reduce(cols[:, 1:2], scr[:], axis=AX.X, op=ALU.min),
                 reads=[P+"scr"], writes=[P+"cols"])
            em.v(lambda e: e.tensor_single_scalar(eqf[0][:], io8[:], cols[:, 1:2], ALU.is_equal),
                 reads=[P+"io8", P+"cols"], writes=[P+"eqf0"])
            em.v(lambda e: e.scalar_tensor_tensor(prow[:], eqf[0][:], -2.0, probs[:], ALU.mult, ALU.add),
                 reads=[P+"eqf0", P+"probs"], writes=[P+"prow"])
            em.v(lambda e: e.reduce_max(cols[:, 4:5], prow[:], axis=AX.X),
                 reads=[P+"prow"], writes=[P+"cols"])
            em.v(lambda e: e.tensor_single_scalar(scr[:], prow[:], cols[:, 4:5], ALU.is_equal),
                 reads=[P+"prow", P+"cols"], writes=[P+"scr"])
            em.v(lambda e: e.scalar_tensor_tensor(scr[:], scr[:], -BIGP, io8B[:], ALU.mult, ALU.add),
                 reads=[P+"scr", P+"io8B"], writes=[P+"scr"])
            em.v(lambda e: e.tensor_reduce(cols[:, 2:3], scr[:], axis=AX.X, op=ALU.min),
                 reads=[P+"scr"], writes=[P+"cols"])
            em.v(lambda e: e.tensor_single_scalar(eqf[1][:], io8[:], cols[:, 2:3], ALU.is_equal),
                 reads=[P+"io8", P+"cols"], writes=[P+"eqf1"])
            em.v(lambda e, tt=tt: e.tensor_copy(pbuf[:, tt, :], cols[:, 3:5]),
                 reads=[P+"cols"], writes=[P+"pbuf"])
            em.v(lambda e: e.tensor_add(rsum2[:], eqf[0][:], eqf[1][:]),
                 reads=[P+"eqf0", P+"eqf1"], writes=[P+"rsum2"])
            em.t(lambda e: e.matmul(ps_cum[:, :], lstr[:], rsum2[:], start=True, stop=True),
                 reads=[P+"lstr", P+"rsum2"], writes=[P+"pscum"], tag="cum")
            em.v(lambda e: e.tensor_add(posx[:], ps_cum[:, :], carry[:]),
                 reads=[P+"carry"], psreads=[P+"pscum"], writes=[P+"posx"])
            em.t(lambda e: e.matmul(ps_row[:, :], onesb1[:], rsum2[:], start=True, stop=True),
                 reads=[P+"onesb1", P+"rsum2"], writes=[P+"psrow"], tag="rowsum")
            em.v(lambda e: e.tensor_copy(row_sb[:], ps_row[:, :]),
                 psreads=[P+"psrow"], writes=[P+"rowsb"])
            em.t(lambda e: e.matmul(ps_car[:, :], onesf1[:], row_sb[:], start=True, stop=True),
                 reads=[P+"onesf1", P+"rowsb"], writes=[P+"pscar"], tag="carbc")
            em.v(lambda e: e.tensor_add(carry[:], carry[:], ps_car[:, :]),
                 reads=[P+"carry"], psreads=[P+"pscar"], writes=[P+"carry"])
            for k in range(K):
                em.v(lambda e, k=k: e.tensor_mul(scr[:], posx[:], eqf[k][:]),
                     reads=[P+"posx", f"{P}eqf{k}"], writes=[P+"scr"])
                em.v(lambda e, k=k: e.reduce_sum(cols[:, 5 + k:6 + k], scr[:], axis=AX.X),
                     reads=[P+"scr"], writes=[P+"cols"])
                em.v(lambda e, k=k: e.scalar_tensor_tensor(
                    addrf[:, k:k + 1], cols[:, 1 + k:2 + k], float(CAP), cols[:, 5 + k:6 + k],
                    ALU.mult, ALU.add),
                    reads=[P+"cols"], writes=[P+"addrf"])
                em.v(lambda e, k=k: e.tensor_scalar(
                    cols[:, 9:10], cols[:, 5 + k:6 + k], float(CAP), 8192.0, ALU.is_ge, ALU.mult),
                    reads=[P+"cols"], writes=[P+"cols"])
                em.v(lambda e, k=k: e.tensor_add(addrf[:, k:k + 1], addrf[:, k:k + 1], cols[:, 9:10]),
                     reads=[P+"cols", P+"addrf"], writes=[P+"addrf"])
            em.v(lambda e, tt=tt: e.tensor_copy(addr_sb[:, tt, :], addrf[:]),
                 reads=[P+"addrf"], writes=[P+"addr"])
        bigt = sbr("bigt", [128, 40], U32)
        em.v(lambda e: e.memset(bigt[:], 1 << 20), writes=[P+"bigt"])
        em.dma("gpsimd", slot_dram.rearrange("(a b) c -> a (b c)", a=128), bigt[:],
               reads=[P+"bigt"], writes=[P+"slotd"], tag="slotinit")
        for tt in range(NTT):
            for k in range(K):
                em.op("gpsimd", lambda e, tt=tt, k=k: e.indirect_dma_start(
                    out=slot_dram[:, :],
                    out_offset=bass.IndirectOffsetOnAxis(addr_sb[:, tt, k:k + 1], 0),
                    in_=tok_sb[:, tt:tt + 1],
                    in_offset=None,
                    bounds_check=bc_slot,
                    oob_is_err=False),
                    reads=[P+"addr", P+"tok", P+"slotd"], writes=[P+"slotd"],
                    dma=True, tag="scat")
        em.barrier()

    ctxF = contextlib.ExitStack()
    sbf = mksb(ctxF)
    with ctxF:
        psf = mkps(ctxF)
        ps_h = [psf(f"psh{i}", [128, 320]) for i in range(2)]
        ps_y = [psf(f"psy{i}", [128, 512]) for i in range(2)]
        ps_tr = [psf(f"pstr{i}", [128, 128], BF16) for i in range(2)]
        bufE = sbf("bufE", [128, 8, CAP], BF16)
        hfm = sbf("hfm", [128, 32, CAP], BF16)
        w1t = [sbf(f"w1t{i}", [128, 8, 1024], BF16) for i in range(2)]
        w2t = sbf("w2t", [128, 32, 1024], BF16)
        slot_ids = sbf("slotids", [128, 5], U32)
        gstag = [sbf(f"gst{i}", [128, D], BF16) for i in range(2)]
        b2row = sbf("b2row", [1, D], F32)
        b2hi = sbf("b2hi", [1, D], BF16)
        b2lo = sbf("b2lo", [1, D], BF16)
        b2lof = sbf("b2lof", [1, D], F32)
        ytile = sbf("ytile", [128, D], BF16)

        for e_i in range(E):
            for st in range(5):
                em.dma("sync", slot_ids[:, st:st + 1],
                       slot_dram[e_i * CAP + st * 128:e_i * CAP + (st + 1) * 128, :],
                       reads=[P+"slotd"], writes=[P+"slotids"], tag="sids")
            em.dma("sync", b2row[:], dw["b2"][e_i:e_i + 1, :], writes=[P+"b2row"])
            em.v(lambda e: e.tensor_copy(b2hi[:], b2row[:]), reads=[P+"b2row"], writes=[P+"b2hi"])
            em.v(lambda e: e.tensor_copy(b2lof[:], b2hi[:]), reads=[P+"b2hi"], writes=[P+"b2lof"])
            em.v(lambda e: e.tensor_sub(b2lof[:], b2row[:], b2lof[:]),
                 reads=[P+"b2row", P+"b2lof"], writes=[P+"b2lof"])
            em.v(lambda e: e.tensor_copy(b2lo[:], b2lof[:]), reads=[P+"b2lof"], writes=[P+"b2lo"])
            for st in range(5):
                i = st % 2
                kg = f"{P}gst{i}"
                em.v(lambda e, i=i: e.memset(gstag[i][:], 0.0), writes=[kg])
                em.op("gpsimd", lambda e, st=st, i=i: e.indirect_dma_start(
                    out=gstag[i][:, :],
                    out_offset=None,
                    in_=x_in[:, :],
                    in_offset=bass.IndirectOffsetOnAxis(slot_ids[:, st:st + 1], 0),
                    bounds_check=bc_tok,
                    oob_is_err=False),
                    reads=[P+"slotids", kg], writes=[kg], dma=True, tag="gath")
                for dc in range(8):
                    sl = dc % 2
                    em.t(lambda e, i=i, dc=dc, sl=sl: e.transpose(
                        ps_tr[sl][:, :], gstag[i][:, dc * 128:(dc + 1) * 128], cst.ident_b[:]),
                        reads=[kg, "identb"], writes=[f"{P}pstr{sl}"])
                    em.sc(lambda e, st=st, dc=dc, sl=sl: e.copy(
                        bufE[:, dc, st * 128:(st + 1) * 128], ps_tr[sl][:, :]),
                        psreads=[f"{P}pstr{sl}"], writes=[P+"bufE"])
            for q in range(4):
                iw = q % 2
                kw = f"{P}w1t{iw}"
                em.dma("gpsimd", w1t[iw][:],
                       dw["w1"][e_i, :, q * 1024:(q + 1) * 1024].rearrange("(c p) m -> p c m", p=128),
                       writes=[kw], tag="w1")
                for hh in range(8):
                    ht = q * 8 + hh
                    for w in range(2):
                        jp = (ht * 2 + w) % 2
                        kp = f"{P}psh{jp}"
                        for dc in range(8):
                            em.t(lambda e, iw=iw, hh=hh, w=w, dc=dc, jp=jp: e.matmul(
                                ps_h[jp][:, :], w1t[iw][:, dc, hh * 128:(hh + 1) * 128],
                                bufE[:, dc, w * 320:(w + 1) * 320],
                                start=(dc == 0), stop=(dc == 7)),
                                reads=[kw, P+"bufE"], writes=[kp], tag="w1mm")
                        em.sc(lambda e, ht=ht, w=w, e_i=e_i, jp=jp: e.activation(
                            out=hfm[:, ht, w * 320:(w + 1) * 320], in_=ps_h[jp][:, :], func=AF.Relu,
                            bias=b1_s[:, e_i, ht:ht + 1]),
                            reads=[P+"b1"], psreads=[kp], writes=[P+"hfm"])
            for q in range(4):
                em.dma("gpsimd", w2t[:, q * 8:(q + 1) * 8, :],
                       dw["w2"][e_i, q * 1024:(q + 1) * 1024, :].rearrange("(c p) m -> p c m", p=128),
                       writes=[P+"w2t"], tag="w2")
            for st in range(5):
                for cw in range(2):
                    jp = (st * 2 + cw) % 2
                    kp = f"{P}psy{jp}"
                    em.t(lambda e, cw=cw, jp=jp: e.matmul(
                        ps_y[jp][:, :], onesbr[:], b2hi[:, cw * 512:(cw + 1) * 512],
                        start=True, stop=False),
                        reads=[P+"onesbr", P+"b2hi"], writes=[kp], tag="b2a")
                    em.t(lambda e, cw=cw, jp=jp: e.matmul(
                        ps_y[jp][:, :], onesbr[:], b2lo[:, cw * 512:(cw + 1) * 512],
                        start=False, stop=False),
                        reads=[P+"onesbr", P+"b2lo"], writes=[kp], tag="b2b")
                    for hc in range(32):
                        em.t(lambda e, st=st, cw=cw, hc=hc, jp=jp: e.matmul(
                            ps_y[jp][:, :], hfm[:, hc, st * 128:(st + 1) * 128],
                            w2t[:, hc, cw * 512:(cw + 1) * 512],
                            start=False, stop=(hc == 31)),
                            reads=[P+"hfm", P+"w2t"], writes=[kp], tag="w2mm")
                    em.sc(lambda e, cw=cw, jp=jp: e.copy(
                        ytile[:, cw * 512:(cw + 1) * 512], ps_y[jp][:, :]),
                        psreads=[kp], writes=[P+"ytile"])
                em.dma("scalar", y_dram[e_i * CAP + st * 128:e_i * CAP + (st + 1) * 128, :],
                       ytile[:], reads=[P+"ytile"], writes=[P+"ydram"], tag="yst")
        em.barrier()

    ctxC = contextlib.ExitStack()
    sbc = mksb(ctxC)
    with ctxC:
        ybt = [sbc(f"ybt{i}", [128, D], BF16) for i in range(2)]
        xres = [sbc(f"xres{i}", [128, D], F32) for i in range(2)]
        s_t = [sbc(f"sc{i}", [128, D], F32) for i in range(2)]
        outt = [sbc(f"outc{i}", [128, D], F32) for i in range(2)]
        for tt in range(NTT):
            iq = tt % 2
            rows = slice(tt * 128, (tt + 1) * 128)
            em.dma("sync", xres[iq][:], x_in[rows, :], writes=[f"{P}xres{iq}"])
            em.v(lambda e, iq=iq: e.tensor_copy(s_t[iq][:], xres[iq][:]),
                 reads=[f"{P}xres{iq}"], writes=[f"{P}sc{iq}"])
            for k in range(K):
                em.v(lambda e, iq=iq: e.memset(ybt[iq][:], 0.0), writes=[f"{P}ybt{iq}"])
                em.op("gpsimd", lambda e, tt=tt, k=k, iq=iq: e.indirect_dma_start(
                    out=ybt[iq][:, :],
                    out_offset=None,
                    in_=y_dram[:, :],
                    in_offset=bass.IndirectOffsetOnAxis(addr_sb[:, tt, k:k + 1], 0),
                    bounds_check=bc_slot,
                    oob_is_err=False),
                    reads=[P+"addr", P+"ydram", f"{P}ybt{iq}"], writes=[f"{P}ybt{iq}"],
                    dma=True, tag="gathy")
                em.v(lambda e, tt=tt, k=k, iq=iq: e.scalar_tensor_tensor(
                    s_t[iq][:], ybt[iq][:], pbuf[:, tt, k:k + 1], s_t[iq][:], ALU.mult, ALU.add),
                    reads=[f"{P}ybt{iq}", P+"pbuf", f"{P}sc{iq}"], writes=[f"{P}sc{iq}"])
            emit_ln(em, P, s_t[iq], f"{P}sc{iq}", sq_t, red, outt[iq], f"{P}outc{iq}")
            em.dma("scalar", x_out[rows, :], outt[iq][:], reads=[f"{P}outc{iq}"], tag="xout")
        em.barrier()




T = 4096
D = 1024
NH = 16
HD = 64
W2 = 128
NTT = T // 128
EPS = 1e-5
NEG = -30000.0


def l1a_host_prep(w, pref=""):
    d = {
        "wqkT": np.ascontiguousarray(w["w_qkv"][0:2048, :].T),
        "wvT": np.ascontiguousarray(w["w_qkv"][2048:3072, :].T),
        "woT": np.ascontiguousarray(w["w_out"].T),
        "gamma": w["gamma"].reshape(1, D),
        "beta": w["beta"].reshape(1, D),
    }
    return {pref + k: v for k, v in d.items()}


def host_consts3():
    p = np.arange(128)[:, None]
    c = np.arange(384)[None, :]
    mi = np.where(np.abs(c - 128 - p) <= W2, 0.0, NEG).astype(np.float32)
    c2 = np.arange(256)[None, :]
    mf = np.where(c2 - p <= W2, 0.0, NEG).astype(np.float32)
    ml = np.where(c2 >= p, 0.0, NEG).astype(np.float32)
    return {"maski": mi, "maskf": mf, "maskl": ml}


def declare_consts3(nc):
    return {
        "maski": nc.dram_tensor("maski", [128, 384], F32, kind="ExternalInput").ap(),
        "maskf": nc.dram_tensor("maskf", [128, 256], F32, kind="ExternalInput").ap(),
        "maskl": nc.dram_tensor("maskl", [128, 256], F32, kind="ExternalInput").ap(),
    }


def l1a_declare_dram(nc, pref=""):
    d = {}
    def inp(name, shape, dt=F32):
        d[name] = nc.dram_tensor(pref + name, shape, dt, kind="ExternalInput").ap()
    inp("wqkT", [D, 2048])
    inp("wvT", [D, D])
    inp("woT", [D, D])
    inp("gamma", [1, D])
    inp("beta", [1, D])
    return d


def l1a_emit(em, nc, ctx, cst, cst3, dw, x_in, x_out, pref="l1a"):
    P = pref
    def mksb(c):
        def sb(name, shape, dtype=F32):
            return c.enter_context(nc.sbuf_tensor(P + "sb_" + name, shape, dtype))
        return sb
    def mkps(c):
        def ps(name, shape, dtype=F32):
            return c.enter_context(nc.psum_tensor(P + "pp_" + name, shape, dtype))
        return ps
    sbp = mksb(ctx)

    xn_dram = nc.dram_tensor(P + "xn_scr", [T, D], BF16, kind="Internal").ap()

    wqk_b = sbp("wqk", [128, 8, 2048], BF16)
    wv_b = sbp("wv", [128, 8, D], BF16)
    wo_b = sbp("wo", [128, 8, D], BF16)
    gbc = sbp("gbc", [128, D])
    bbc = sbp("bbc", [128, D])
    grow = sbp("grow", [1, D])
    brow = sbp("brow", [1, D])
    onesf1 = sbp("onesf1", [1, 128], F32)
    maski = sbp("maski", [128, 384])
    maskf = sbp("maskf", [128, 256])
    maskl = sbp("maskl", [128, 256])
    red = sbp("red", [128, 8], F32)
    sq_t = sbp("sq", [128, D], F32)

    em.dma("gpsimd", wqk_b[:], dw["wqkT"].rearrange("(c p) m -> p c m", p=128), writes=[P+"wqk"])
    em.dma("gpsimd", wv_b[:], dw["wvT"].rearrange("(c p) m -> p c m", p=128), writes=[P+"wv"])
    em.dma("gpsimd", wo_b[:], dw["woT"].rearrange("(c p) m -> p c m", p=128), writes=[P+"wo"])
    em.dma("sync", grow[:], dw["gamma"][:, :], writes=[P+"grow"])
    em.dma("sync", brow[:], dw["beta"][:, :], writes=[P+"brow"])
    em.dma("sync", maski[:], cst3["maski"][:, :], writes=[P+"maski"])
    em.dma("sync", maskf[:], cst3["maskf"][:, :], writes=[P+"maskf"])
    em.dma("sync", maskl[:], cst3["maskl"][:, :], writes=[P+"maskl"])
    em.v(lambda e: e.memset(onesf1[:], 1.0), writes=[P+"onesf1"])

    ctxN = contextlib.ExitStack()
    sbn = mksb(ctxN)
    with ctxN:
        psn = mkps(ctxN)
        ps_g = psn("psg", [128, 512])
        ps_g2 = psn("psg2", [128, 512])
        xld = [sbn(f"xld{i}", [128, D], F32) for i in range(2)]
        xnb = [sbn(f"xnb{i}", [128, D], BF16) for i in range(2)]
        tmp = sbn("tmp", [128, D], F32)
        for half in range(2):
            row = grow if half == 0 else brow
            krow = P + ("grow" if half == 0 else "brow")
            dst = gbc if half == 0 else bbc
            kdst = P + ("gbc" if half == 0 else "bbc")
            for hw in range(2):
                pg = ps_g if hw == 0 else ps_g2
                kpg = P + ("psg0" if hw == 0 else "psg1")
                em.t(lambda e, pg=pg, row=row, hw=hw: e.matmul(
                    pg[:, :], onesf1[:], row[:, hw * 512:(hw + 1) * 512],
                    start=True, stop=True),
                    reads=[P+"onesf1", krow], writes=[kpg], tag="gb")
                em.v(lambda e, dst=dst, hw=hw, pg=pg: e.tensor_copy(
                    dst[:, hw * 512:(hw + 1) * 512], pg[:, :]),
                    psreads=[kpg], writes=[kdst])
        for tt in range(NTT):
            i = tt % 2
            kxl = f"{P}xld{i}"
            rows = slice(tt * 128, (tt + 1) * 128)
            em.dma("sync", xld[i][:], x_in[rows, :], writes=[kxl])
            em.v(lambda e, i=i: e.reduce_sum(red[:, 0:1], xld[i][:], axis=AX.X),
                 reads=[kxl], writes=[P+"red"])
            em.v(lambda e: e.tensor_scalar_mul(red[:, 1:2], red[:, 0:1], -1.0 / D),
                 reads=[P+"red"], writes=[P+"red"])
            em.v(lambda e, i=i: e.tensor_scalar_add(tmp[:], xld[i][:], red[:, 1:2]),
                 reads=[kxl, P+"red"], writes=[P+"tmp"])
            em.sc(lambda e: e.activation(out=sq_t[:], in_=tmp[:], func=AF.Square,
                                         accum_out=red[:, 2:3]),
                  reads=[P+"tmp"], writes=[P+"sq", P+"red"])
            em.v(lambda e: e.tensor_scalar(red[:, 3:4], red[:, 2:3], 1.0 / D, EPS, ALU.mult, ALU.add),
                 reads=[P+"red"], writes=[P+"red"])
            em.sc(lambda e: e.activation(out=red[:, 3:4], in_=red[:, 3:4], func=AF.Ln),
                  reads=[P+"red"], writes=[P+"red"])
            em.sc(lambda e: e.activation(out=red[:, 3:4], in_=red[:, 3:4], func=AF.Exp, scale=-0.5),
                  reads=[P+"red"], writes=[P+"red"])
            em.v(lambda e: e.tensor_scalar_mul(tmp[:], tmp[:], red[:, 3:4]),
                 reads=[P+"tmp", P+"red"], writes=[P+"tmp"])
            em.v(lambda e: e.tensor_mul(tmp[:], tmp[:], gbc[:]),
                 reads=[P+"tmp", P+"gbc"], writes=[P+"tmp"])
            em.v(lambda e, i=i: e.tensor_add(xnb[i][:], tmp[:], bbc[:]),
                 reads=[P+"tmp", P+"bbc"], writes=[f"{P}xnb{i}"])
            em.dma("scalar", xn_dram[rows, :], xnb[i][:], reads=[f"{P}xnb{i}"],
                   writes=[P+"xnd"], tag="xnst")
        em.barrier()

    ctxK = contextlib.ExitStack()
    sbk = mksb(ctxK)
    kfm = ctx.enter_context(nc.sbuf_tensor(P + "sb_kfm", [128, 8, T], BF16))
    with ctxK:
        psk = mkps(ctxK)
        ps_trK = [psk(f"pstr{i}", [128, 128], BF16) for i in range(2)]
        ps_kK = [psk(f"psk{i}", [128, 128]) for i in range(2)]
        xldbK = [sbk(f"xldbK{i}", [128, D], BF16) for i in range(2)]
        xfmtK = sbk("xfmtK", [128, 8, 128], BF16)
        for tt in range(NTT):
            i = tt % 2
            kxl = f"{P}xldbK{i}"
            em.dma("sync", xldbK[i][:], xn_dram[tt * 128:(tt + 1) * 128, :],
                   reads=[P+"xnd"], writes=[kxl])
            for dc in range(8):
                sl = dc % 2
                em.t(lambda e, i=i, dc=dc, sl=sl: e.transpose(
                    ps_trK[sl][:, :], xldbK[i][:, dc * 128:(dc + 1) * 128], cst.ident_b[:]),
                    reads=[kxl, "identb"], writes=[f"{P}pstr{sl}"])
                em.v(lambda e, dc=dc, sl=sl: e.tensor_copy(xfmtK[:, dc, :], ps_trK[sl][:, :]),
                     psreads=[f"{P}pstr{sl}"], writes=[P+"xfmtK"])
            for kf in range(8):
                sl = kf % 2
                kp = f"{P}psk{sl}"
                for dc in range(8):
                    em.t(lambda e, kf=kf, dc=dc, sl=sl: e.matmul(
                        ps_kK[sl][:, :], wqk_b[:, dc, 1024 + kf * 128:1024 + (kf + 1) * 128],
                        xfmtK[:, dc, :], start=(dc == 0), stop=(dc == 7)),
                        reads=[P+"wqk", P+"xfmtK"], writes=[kp], tag="kmm")
                em.sc(lambda e, kf=kf, tt=tt, sl=sl: e.copy(
                    kfm[:, kf, tt * 128:(tt + 1) * 128], ps_kK[sl][:, :]),
                    psreads=[kp], writes=[P+"kfm"])
        em.barrier()

    ctxA = contextlib.ExitStack()
    sba = mksb(ctxA)
    with ctxA:
        psa = mkps(ctxA)
        ps_q = [psa(f"psq{i}", [128, 128]) for i in range(2)]
        ps_v = [psa("psv0", [128, 512]), ]
        ps_s = psa("pss", [128, 384])
        ps_trA = [psa(f"pstrA{i}", [128, 128], BF16) for i in range(2)]
        ps_av = psa("psav", [128, 128])
        ps_o = psa("pso", [128, 512])
        xnr = sba("xnr", [128, 2, 8, 128], BF16)
        vring = sba("vring", [128, 3, D], BF16)
        q_sb = sba("qsb", [128, 8, 128], BF16)
        att = [sba(f"att{i}", [128, 384]) for i in range(2)]
        attb = [sba(f"attb{i}", [128, 384], BF16) for i in range(2)]
        attT = [sba(f"attT{i}", [128, 3, 128], BF16) for i in range(2)]
        aout = sba("aout", [128, 8, 128], BF16)
        xres = [sba("xres0", [128, D], F32)]
        s_t = [sba("sa0", [128, D], F32)]
        outt = [sba("outa0", [128, D], F32)]
        xldbA = [sba(f"xldc{i}", [128, D], BF16) for i in range(2)]
        redL = sba("redL", [128, 4], F32)

        def load_xn_tile(tile, slot):
            i = tile % 2
            kxl = f"{P}xldc{i}"
            em.dma("sync", xldbA[i][:], xn_dram[tile * 128:(tile + 1) * 128, :],
                   reads=[P+"xnd"], writes=[kxl])
            for dc in range(8):
                sl = dc % 2
                tsl = dc % 2
                em.t(lambda e, i=i, dc=dc, tsl=tsl: e.transpose(
                    ps_trA[tsl][:, :], xldbA[i][:, dc * 128:(dc + 1) * 128], cst.ident_b[:]),
                    reads=[kxl, "identb"], writes=[f"{P}pstrA{tsl}"])
                em.v(lambda e, dc=dc, slot=slot, tsl=tsl: e.tensor_copy(xnr[:, slot, dc, :], ps_trA[tsl][:, :]),
                     psreads=[f"{P}pstrA{tsl}"], writes=[P+"xnr"])

        def compute_v_tile(tile, vslot, xslot):
            for hw in range(2):
                kp = "l1apsv0"
                for dc in range(8):
                    em.t(lambda e, hw=hw, dc=dc, xslot=xslot: e.matmul(
                        ps_v[0][:, :], xnr[:, xslot, dc, :], wv_b[:, dc, hw * 512:(hw + 1) * 512],
                        start=(dc == 0), stop=(dc == 7)),
                        reads=[P+"xnr", P+"wv"], writes=[kp], tag="vmm")
                em.sc(lambda e, hw=hw, vslot=vslot: e.copy(
                    vring[:, vslot, hw * 512:(hw + 1) * 512], ps_v[0][:, :]),
                    psreads=[kp], writes=[P+"vring"])

        load_xn_tile(0, 0)
        compute_v_tile(0, 0, 0)
        load_xn_tile(1, 1)
        compute_v_tile(1, 1, 1)

        for qt in range(NTT):
            if qt + 1 < NTT and qt > 0:
                xslot = (qt + 1) % 2
                load_xn_tile(qt + 1, xslot)
                compute_v_tile(qt + 1, (qt + 1) % 3, xslot)
            for qf in range(8):
                sl = qf % 2
                kp = f"{P}psq{sl}"
                for dc in range(8):
                    em.t(lambda e, qf=qf, dc=dc, sl=sl, qt=qt: e.matmul(
                        ps_q[sl][:, :], wqk_b[:, dc, qf * 128:(qf + 1) * 128],
                        xnr[:, qt % 2, dc, :], start=(dc == 0), stop=(dc == 7)),
                        reads=[P+"wqk", P+"xnr"], writes=[kp], tag="qmm")
                em.sc(lambda e, qf=qf, sl=sl: e.activation(
                    out=q_sb[:, qf, :], in_=ps_q[sl][:, :], func=AF.Copy, scale=0.125),
                    psreads=[kp], writes=[P+"qsb"])
            ks = [t for t in (qt - 1, qt, qt + 1) if 0 <= t < NTT]
            kw = len(ks) * 128
            k0 = ks[0] * 128
            mask = maski if len(ks) == 3 else (maskf if qt == 0 else maskl)
            kmask = P + ("maski" if len(ks) == 3 else ("maskf" if qt == 0 else "maskl"))
            for h in range(NH):
                hp = (h % 2) * 64
                hf = h // 2
                hs = h % 2
                rb = hs * 4
                ka, kab, kat = f"{P}att{hs}", f"{P}attb{hs}", f"{P}attT{hs}"
                em.t(lambda e, hp=hp, hf=hf, k0=k0, kw=kw: e.matmul(
                    ps_s[:, 0:kw], q_sb[hp:hp + 64, hf, :], kfm[hp:hp + 64, hf, k0:k0 + kw],
                    start=True, stop=True),
                    reads=[P+"qsb", P+"kfm"], writes=[P+"pss"], tag="smm")
                em.v(lambda e, kw=kw, mask=mask, hs=hs: e.tensor_add(att[hs][:, 0:kw], ps_s[:, 0:kw], mask[:, 0:kw]),
                     reads=[kmask], psreads=[P+"pss"], writes=[ka])
                em.v(lambda e, kw=kw, hs=hs, rb=rb: e.reduce_max(red[:, rb:rb + 1], att[hs][:, 0:kw], axis=AX.X),
                     reads=[ka], writes=[P+"red"+str(hs)])
                em.v(lambda e, rb=rb: e.tensor_scalar_mul(red[:, rb + 1:rb + 2], red[:, rb:rb + 1], -1.0),
                     reads=[P+"red"+str(hs)], writes=[P+"red"+str(hs)])
                em.sc(lambda e, kw=kw, hs=hs, rb=rb: e.activation(out=att[hs][:, 0:kw], in_=att[hs][:, 0:kw], func=AF.Exp,
                                                    bias=red[:, rb + 1:rb + 2], accum_out=red[:, rb + 2:rb + 3]),
                      reads=[ka, P+"red"+str(hs)], writes=[ka, P+"red"+str(hs)])
                em.v(lambda e, rb=rb: e.reciprocal(red[:, rb + 3:rb + 4], red[:, rb + 2:rb + 3]),
                     reads=[P+"red"+str(hs)], writes=[P+"red"+str(hs)])
                em.sc(lambda e, kw=kw, hs=hs, rb=rb: e.mul(attb[hs][:, 0:kw], att[hs][:, 0:kw], red[:, rb + 3:rb + 4]),
                      reads=[ka, P+"red"+str(hs)], writes=[kab])
                for j in range(len(ks)):
                    em.t(lambda e, j=j, hs=hs: e.transpose(ps_trA[hs][:, :], attb[hs][:, j * 128:(j + 1) * 128],
                                                    cst.ident_b[:]),
                         reads=[kab, "identb"], writes=[f"{P}pstrA{hs}"])
                    em.sc(lambda e, j=j, hs=hs: e.copy(attT[hs][:, j, :], ps_trA[hs][:, :]),
                          psreads=[f"{P}pstrA{hs}"], writes=[kat])
                for j, kt in enumerate(ks):
                    em.t(lambda e, j=j, kt=kt, hp=hp, hf=hf, hs=hs, nk=len(ks): e.matmul(
                        ps_av[0:64, :], vring[:, kt % 3, (hf * 2 * 64 + hp):(hf * 2 * 64 + hp) + 64],
                        attT[hs][:, j, :], start=(j == 0), stop=(j == nk - 1)),
                        reads=[P+"vring", kat], writes=[P+"psav"], tag="avmm")
                em.v(lambda e, hp=hp, hf=hf: e.tensor_copy(aout[hp:hp + 64, hf, :], ps_av[0:64, :]),
                     psreads=[P+"psav"], writes=[P+"aout"])
            iq = 0
            rows = slice(qt * 128, (qt + 1) * 128)
            em.dma("sync", xres[iq][:], x_in[rows, :], writes=[f"{P}xres{iq}"])
            for hw in range(2):
                for dc in range(8):
                    em.t(lambda e, hw=hw, dc=dc: e.matmul(
                        ps_o[:, :], aout[:, dc, :], wo_b[:, dc, hw * 512:(hw + 1) * 512],
                        start=(dc == 0), stop=(dc == 7)),
                        reads=[P+"aout", P+"wo"], writes=[P+"pso"], tag="omm")
                em.v(lambda e, iq=iq, hw=hw: e.scalar_tensor_tensor(
                    s_t[iq][:, hw * 512:(hw + 1) * 512], xres[iq][:, hw * 512:(hw + 1) * 512],
                    2.0, ps_o[:, :], ALU.mult, ALU.add),
                    reads=[f"{P}xres{iq}"], psreads=[P+"pso"], writes=[f"{P}sa{iq}"])
            emit_ln(em, P + "A", s_t[iq], f"{P}sa{iq}", sq_t, redL, outt[iq], f"{P}outa{iq}")
            em.dma("scalar", x_out[rows, :], outt[iq][:], reads=[f"{P}outa{iq}"], tag="xout")
        em.barrier()




T = 4096
D = 1024
E = 8
HH = 1024
NTT = T // 128
BIGP = 1.0e6
EPS = 1e-5


def l1f_host_prep(w, pref=""):
    d = {
        "wr": np.ascontiguousarray(w["Wr"]),
        "wi": np.ascontiguousarray(w["Wi"]),
        "wo": np.ascontiguousarray(w["Wo"]),
    }
    return {pref + k: v for k, v in d.items()}


def host_consts4():
    sel = np.zeros((8, 8, 128), np.float32)
    for e in range(8):
        sel[e, e, :] = 1.0
    return {"sel8f": sel.reshape(8, 8 * 128)}


def declare_consts4(nc):
    return {"sel8f": nc.dram_tensor("sel8f", [8, 8 * 128], F32, kind="ExternalInput").ap()}


def l1f_declare_dram(nc, pref=""):
    d = {}
    def inp(name, shape, dt=F32):
        d[name] = nc.dram_tensor(pref + name, shape, dt, kind="ExternalInput").ap()
    inp("wr", [D, E])
    inp("wi", [E, D, HH])
    inp("wo", [E, HH, D])
    return d


def l1f_emit(em, nc, ctx, cst, cst4, dw, x_in, x_out, pref="l1f"):
    P = pref
    def mksb(c):
        def sb(name, shape, dtype=F32):
            return c.enter_context(nc.sbuf_tensor(P + "sb_" + name, shape, dtype))
        return sb
    def mkps(c):
        def ps(name, shape, dtype=F32):
            return c.enter_context(nc.psum_tensor(P + "pp_" + name, shape, dtype))
        return ps
    sbp = mksb(ctx)

    xbf_dram = nc.dram_tensor(P + "xbf_scr", [T, D], BF16, kind="Internal").ap()

    wr_s = sbp("wr", [128, 8, E], F32)
    cfm = sbp("cfm", [8, T], F32)
    io8 = sbp("io8", [128, E], F32)
    io8B = sbp("io8B", [128, E], F32)
    sel8f = sbp("sel8f", [8, 8 * 128], F32)
    red = sbp("red", [128, 4], F32)
    sq_t = sbp("sq", [128, D], F32)

    em.dma("sync", wr_s[:], dw["wr"].rearrange("(c p) m -> p c m", p=128), writes=[P+"wr"])
    em.dma("sync", sel8f[:], cst4["sel8f"][:, :], writes=[P+"sel8f"])
    em.g(lambda e: e.iota(io8[:], pattern=[[1, E]], base=0, channel_multiplier=0,
                          allow_small_or_imprecise_dtypes=True), writes=[P+"io8"])
    em.g(lambda e: e.iota(io8B[:], pattern=[[1, E]], base=int(BIGP), channel_multiplier=0,
                          allow_small_or_imprecise_dtypes=True), writes=[P+"io8B"])

    ctxR = contextlib.ExitStack()
    sbr = mksb(ctxR)
    with ctxR:
        psr = mkps(ctxR)
        ps_log = psr("pslog", [128, E])
        ps_ct = psr("psct", [128, 128])
        pstrf = [psr(f"pstrf{i}", [128, 128]) for i in range(2)]
        xld = [sbr(f"xld{i}", [128, D], F32) for i in range(2)]
        xbfs = [sbr(f"xbfs{i}", [128, D], BF16) for i in range(2)]
        xfmR = sbr("xfmR", [128, 8, 128], F32)
        probs = sbr("probs", [128, E])
        prow = sbr("prow", [128, E])
        scr = sbr("scr", [128, E])
        eqf = [sbr(f"eqf{i}", [128, E]) for i in range(2)]
        csb = sbr("csb", [128, E])
        cols = sbr("cols", [128, 12])

        for tt in range(NTT):
            i = tt % 2
            kxl = f"{P}xld{i}"
            rows = slice(tt * 128, (tt + 1) * 128)
            em.dma("sync", xld[i][:], x_in[rows, :], writes=[kxl])
            em.v(lambda e, i=i: e.tensor_copy(xbfs[i][:], xld[i][:]),
                 reads=[kxl], writes=[f"{P}xbfs{i}"])
            em.dma("scalar", xbf_dram[rows, :], xbfs[i][:], reads=[f"{P}xbfs{i}"],
                   writes=[P+"xbfd"], tag="xbst")
            for dc in range(8):
                sl = dc % 2
                em.t(lambda e, i=i, dc=dc, sl=sl: e.transpose(
                    pstrf[sl][:, :], xld[i][:, dc * 128:(dc + 1) * 128], cst.ident_f[:]),
                    reads=[kxl, "identf"], writes=[f"{P}pstrf{sl}"])
                em.v(lambda e, dc=dc, sl=sl: e.tensor_copy(xfmR[:, dc, :], pstrf[sl][:, :]),
                     psreads=[f"{P}pstrf{sl}"], writes=[P+"xfmR"])
            for dc in range(8):
                em.t(lambda e, dc=dc: e.matmul(
                    ps_log[:, :], xfmR[:, dc, :], wr_s[:, dc, :],
                    start=(dc == 0), stop=(dc == 7)),
                    reads=[P+"xfmR", P+"wr"], writes=[P+"pslog"], tag="router")
            em.v(lambda e: e.reduce_max(cols[:, 0:1], ps_log[:, :], axis=AX.X),
                 psreads=[P+"pslog"], writes=[P+"cols"])
            em.v(lambda e: e.tensor_scalar_mul(cols[:, 0:1], cols[:, 0:1], -1.0),
                 reads=[P+"cols"], writes=[P+"cols"])
            em.sc(lambda e: e.activation(out=prow[:], in_=ps_log[:, :], func=AF.Exp,
                                         bias=cols[:, 0:1], accum_out=cols[:, 9:10]),
                  reads=[P+"cols"], psreads=[P+"pslog"], writes=[P+"prow", P+"cols"])
            em.v(lambda e: e.reciprocal(cols[:, 10:11], cols[:, 9:10]),
                 reads=[P+"cols"], writes=[P+"cols"])
            em.v(lambda e: e.tensor_scalar_mul(probs[:], prow[:], cols[:, 10:11]),
                 reads=[P+"prow", P+"cols"], writes=[P+"probs"])
            em.v(lambda e: e.reduce_max(cols[:, 3:4], probs[:], axis=AX.X),
                 reads=[P+"probs"], writes=[P+"cols"])
            em.v(lambda e: e.tensor_single_scalar(scr[:], probs[:], cols[:, 3:4], ALU.is_equal),
                 reads=[P+"probs", P+"cols"], writes=[P+"scr"])
            em.v(lambda e: e.scalar_tensor_tensor(scr[:], scr[:], -BIGP, io8B[:], ALU.mult, ALU.add),
                 reads=[P+"scr", P+"io8B"], writes=[P+"scr"])
            em.v(lambda e: e.tensor_reduce(cols[:, 1:2], scr[:], axis=AX.X, op=ALU.min),
                 reads=[P+"scr"], writes=[P+"cols"])
            em.v(lambda e: e.tensor_single_scalar(eqf[0][:], io8[:], cols[:, 1:2], ALU.is_equal),
                 reads=[P+"io8", P+"cols"], writes=[P+"eqf0"])
            em.v(lambda e: e.scalar_tensor_tensor(prow[:], eqf[0][:], -2.0, probs[:], ALU.mult, ALU.add),
                 reads=[P+"eqf0", P+"probs"], writes=[P+"prow"])
            em.v(lambda e: e.reduce_max(cols[:, 4:5], prow[:], axis=AX.X),
                 reads=[P+"prow"], writes=[P+"cols"])
            em.v(lambda e: e.tensor_single_scalar(scr[:], prow[:], cols[:, 4:5], ALU.is_equal),
                 reads=[P+"prow", P+"cols"], writes=[P+"scr"])
            em.v(lambda e: e.scalar_tensor_tensor(scr[:], scr[:], -BIGP, io8B[:], ALU.mult, ALU.add),
                 reads=[P+"scr", P+"io8B"], writes=[P+"scr"])
            em.v(lambda e: e.tensor_reduce(cols[:, 2:3], scr[:], axis=AX.X, op=ALU.min),
                 reads=[P+"scr"], writes=[P+"cols"])
            em.v(lambda e: e.tensor_single_scalar(eqf[1][:], io8[:], cols[:, 2:3], ALU.is_equal),
                 reads=[P+"io8", P+"cols"], writes=[P+"eqf1"])
            em.v(lambda e: e.tensor_add(scr[:], eqf[0][:], eqf[1][:]),
                 reads=[P+"eqf0", P+"eqf1"], writes=[P+"scr"])
            em.v(lambda e: e.tensor_mul(csb[:], probs[:], scr[:]),
                 reads=[P+"probs", P+"scr"], writes=[P+"csb"])
            em.t(lambda e: e.transpose(ps_ct[0:8, :], csb[:], cst.ident_f[:]),
                 reads=[P+"csb", "identf"], writes=[P+"psct"])
            em.v(lambda e, tt=tt: e.tensor_copy(cfm[:, tt * 128:(tt + 1) * 128], ps_ct[0:8, :]),
                 psreads=[P+"psct"], writes=[P+"cfm"])
        em.barrier()

    ctxF = contextlib.ExitStack()
    sbf = mksb(ctxF)
    with ctxF:
        psf = mkps(ctxF)
        ps_cb = psf("pscb", [128, 512])
        ps_he = [psf(f"pshe{i}", [128, 512]) for i in range(2)]
        ps_o = [psf(f"pso{i}", [128, 512]) for i in range(2)]
        xfmH = sbf("xfmH", [128, 8, 2048], BF16)
        out_acc = sbf("oacc", [128, 16, D])
        wi_b = [sbf("wib0", [128, 8, HH], BF16)]
        wo_b = [sbf("wob0", [128, 8, D], BF16)]
        cb_sb = sbf("cbsb", [128, 512])
        g_fm = sbf("gfm", [128, 8, 512], BF16)
        xres = [sbf(f"xres{i}", [128, D], F32) for i in range(2)]
        s_t = [sbf(f"sf{i}", [128, D], F32) for i in range(2)]
        outt = [sbf(f"outf{i}", [128, D], F32) for i in range(2)]

        for half in range(2):
            th0 = half * 2048
            for ch in range(4):
                for dc in range(8):
                    em.dma("sync", xfmH[:, dc, ch * 512:(ch + 1) * 512],
                           xbf_dram[th0 + ch * 512:th0 + (ch + 1) * 512, dc * 128:(dc + 1) * 128],
                           reads=[P+"xbfd"], writes=[P+"xfmH"], transpose=True, tag="xtr")
            em.v(lambda e: e.memset(out_acc[:], 0.0), writes=[P+"oacc"])
            for e_i in range(E):
                ib = 0
                kwi, kwo = f"{P}wib0", f"{P}wob0"
                em.dma("gpsimd", wi_b[ib][:], dw["wi"][e_i].rearrange("(c p) m -> p c m", p=128),
                       writes=[kwi], tag="wi")
                em.dma("gpsimd", wo_b[ib][:], dw["wo"][e_i].rearrange("(c p) m -> p c m", p=128),
                       writes=[kwo], tag="wo")
                for tch in range(4):
                    tw = slice(tch * 512, (tch + 1) * 512)
                    em.t(lambda e, e_i=e_i, tch=tch, th0=th0: e.matmul(
                        ps_cb[:, :], sel8f[:, e_i * 128:(e_i + 1) * 128],
                        cfm[:, th0 + tch * 512:th0 + (tch + 1) * 512],
                        start=True, stop=True),
                        reads=[P+"sel8f", P+"cfm"], writes=[P+"pscb"], tag="cb")
                    em.v(lambda e: e.tensor_copy(cb_sb[:], ps_cb[:, :]),
                         psreads=[P+"pscb"], writes=[P+"cbsb"])
                    for ht in range(8):
                        sl = ht % 2
                        kp = f"{P}pshe{sl}"
                        for dc in range(8):
                            em.t(lambda e, ht=ht, dc=dc, sl=sl, ib=ib, tch=tch: e.matmul(
                                ps_he[sl][:, :], wi_b[ib][:, dc, ht * 128:(ht + 1) * 128],
                                xfmH[:, dc, tch * 512:(tch + 1) * 512],
                                start=(dc == 0), stop=(dc == 7)),
                                reads=[kwi, P+"xfmH"], writes=[kp], tag="hemm")
                        em.v(lambda e, ht=ht, sl=sl: e.tensor_mul(g_fm[:, ht, :], ps_he[sl][:, :], cb_sb[:]),
                             reads=[P+"cbsb"], psreads=[kp], writes=[P+"gfm"])
                    for tt2 in range(4):
                        for cw in range(2):
                            sl = (tt2 * 2 + cw) % 2
                            kp = f"{P}pso{sl}"
                            for hc in range(8):
                                em.t(lambda e, tt2=tt2, cw=cw, hc=hc, sl=sl, ib=ib: e.matmul(
                                    ps_o[sl][:, :], g_fm[:, hc, tt2 * 128:(tt2 + 1) * 128],
                                    wo_b[ib][:, hc, cw * 512:(cw + 1) * 512],
                                    start=(hc == 0), stop=(hc == 7)),
                                    reads=[P+"gfm", kwo], writes=[kp], tag="womm")
                            em.v(lambda e, tch=tch, tt2=tt2, cw=cw, sl=sl: e.tensor_add(
                                out_acc[:, tch * 4 + tt2, cw * 512:(cw + 1) * 512],
                                out_acc[:, tch * 4 + tt2, cw * 512:(cw + 1) * 512],
                                ps_o[sl][:, :]),
                                reads=[P+"oacc"], psreads=[kp], writes=[P+"oacc"])
            for tl in range(16):
                tt = half * 16 + tl
                iq = tl % 2
                rows = slice(tt * 128, (tt + 1) * 128)
                em.dma("sync", xres[iq][:], x_in[rows, :], writes=[f"{P}xres{iq}"])
                em.v(lambda e, iq=iq, tl=tl: e.tensor_add(s_t[iq][:], out_acc[:, tl, :], xres[iq][:]),
                     reads=[P+"oacc", f"{P}xres{iq}"], writes=[f"{P}sf{iq}"])
                emit_ln(em, P, s_t[iq], f"{P}sf{iq}", sq_t, red, outt[iq], f"{P}outf{iq}")
                em.dma("scalar", x_out[rows, :], outt[iq][:], reads=[f"{P}outf{iq}"], tag="xout")
        em.barrier()







def _build_fused():
    nc = bass.Bass("TRN2", target_bir_lowering=False, debug=False)
    ctx = contextlib.ExitStack()
    with ctx:
        x_in = nc.dram_tensor("x", [T, D], F32, kind="ExternalInput").ap()
        x_out = nc.dram_tensor("x4", [T, D], F32, kind="ExternalOutput").ap()
        x1s = nc.dram_tensor("x1s", [T, D], F32, kind="Internal").ap()
        x2s = nc.dram_tensor("x2s", [T, D], F32, kind="Internal").ap()
        x3s = nc.dram_tensor("x3s", [T, D], F32, kind="Internal").ap()
        dcst = declare_consts(nc)
        dcst2 = declare_consts2(nc)
        dcst3 = declare_consts3(nc)
        dcst4 = declare_consts4(nc)
        dw0a = l0a_declare_dram(nc, "l0a")
        dw0f = l0f_declare_dram(nc, "l0f")
        dw1a = l1a_declare_dram(nc, "l1a")
        dw1f = l1f_declare_dram(nc, "l1f")
        em = Em(nc, ctx)
        cst = Consts(em, nc, ctx, dcst)
        with contextlib.ExitStack() as c1:
            l0a_emit(em, nc, c1, cst, dw0a, x_in, x1s, pref="l0a")
            em.barrier()
        with contextlib.ExitStack() as c2:
            l0f_emit(em, nc, c2, cst, dcst2, dw0f, x1s, x2s, pref="l0f")
            em.barrier()
        with contextlib.ExitStack() as c3:
            l1a_emit(em, nc, c3, cst, dcst3, dw1a, x2s, x3s, pref="l1a")
            em.barrier()
        with contextlib.ExitStack() as c4:
            l1f_emit(em, nc, c4, cst, dcst4, dw1f, x3s, x_out, pref="l1f")
            em.barrier()
        em.barrier_reads_of_all()
        block = ctx.enter_context(nc.Block())
        em.replay(block)
    return nc


def kernel(**inputs):
    def grp(pref):
        return {k[len(pref):]: np.ascontiguousarray(np.asarray(v, np.float32))
                for k, v in inputs.items() if k.startswith(pref)}

    cst_all = host_consts()
    hp = {}
    hp.update(l0a_host_prep(grp("L0_attn_"), "l0a"))
    hp.update(l0f_host_prep(grp("L0_ffn_"), "l0f"))
    hp.update(l1a_host_prep(grp("L1_attn_"), "l1a"))
    hp.update(l1f_host_prep(grp("L1_ffn_"), "l1f"))
    hp.update(cst_all)
    hp.update(host_consts2())
    hp.update(host_consts3())
    hp.update(host_consts4())

    x = np.ascontiguousarray(np.asarray(inputs["L0_attn_hidden_states"], np.float32))
    nc = _build_fused()
    maps = [{**hp, "x": x[b]} for b in range(8)]
    res = run_bass_kernel_spmd(nc, maps, core_ids=list(range(8)))
    out = np.stack([np.asarray(res.results[b]["x4"]) for b in range(8)])
    return np.ascontiguousarray(out.astype(np.float32))
